# version 10
# speedup vs baseline: 1.8020x; 1.0009x over previous
import numpy as np
from contextlib import ExitStack

import concourse.bass as bass
import concourse.mybir as mybir
from concourse.bass_utils import run_bass_kernel_spmd
from concourse.bass_core_rust import IndirectOffsetOnAxis

F32 = mybir.dt.float32
BF16 = mybir.dt.bfloat16
I32 = mybir.dt.int32
AF = mybir.ActivationFunctionType
ALU = mybir.AluOpType
AX = mybir.AxisListType

D = 2048
NCH = 16
T = 1024
NH0 = 44
LAMBDA_INIT = 0.8 - 0.6 * float(np.exp(-0.3 * 12))


class Sem:
    def __init__(self, handle):
        self.h = handle
        self.n = 0

    def inc(self, k=1):
        self.n += k
        return self.n


class Prog:
    ENGINES = ("sync", "vector", "scalar", "tensor", "gpsimd")

    def __init__(self, nc):
        self.nc = nc
        self.streams = {e: [] for e in self.ENGINES}

    def on(self, eng, fn):
        self.streams[eng].append(fn)

    def wait(self, eng, sem, val):
        if val > 0:
            self.on(eng, lambda e, s=sem, v=val: e.wait_ge(s.h, v))

    def finalize(self):
        nc = self.nc
        with nc.Block() as block:
            @block.sync
            def _(e):
                for fn in self.streams["sync"]:
                    fn(e)

            @block.vector
            def _(e):
                for fn in self.streams["vector"]:
                    fn(e)

            @block.scalar
            def _(e):
                for fn in self.streams["scalar"]:
                    fn(e)

            @block.tensor
            def _(e):
                for fn in self.streams["tensor"]:
                    fn(e)

            @block.gpsimd
            def _(e):
                for fn in self.streams["gpsimd"]:
                    fn(e)


def make_consts():
    import ml_dtypes
    return {
        "ones128_f": np.ones((128, 1), np.float32),
        "ones128_b": np.ones((128, 1), np.float32).astype(ml_dtypes.bfloat16),
        "ones1_f": np.ones((1, 128), np.float32),
        "m128_b": np.full((1, 128), -128.0, np.float32).astype(ml_dtypes.bfloat16),
        "ident_b": np.eye(128, dtype=np.float32).astype(ml_dtypes.bfloat16),
    }


CONST_DTYPES = {"ones128_f": F32, "ones128_b": BF16, "ones1_f": F32,
                "m128_b": BF16, "ident_b": BF16}


def const_inputs(consts):
    return {"c_" + k: v for k, v in consts.items()}



def _inc(ins, sem=None, k=1):
    if sem is not None:
        ins.then_inc(sem.h, k)
    return ins


def emit_ln(p, s, xin_r, xln, t=T, col0=0, want_u=True):
    nt = t // 512
    lx, pe, act, dve = s["lx"], s["pe"], s["act"], s["dve"]
    cst = s["cst"]
    xa, xsq = s["xa"], s["xsq"]
    Pb = s["Pbig"]
    mu_sb, rs_sb = s["mu_sb"], s["rs_sb"]
    rows = s["rows"]
    dve0, act0 = dve.n, act.n
    p.wait("sync", dve, dve0)
    p.wait("scalar", dve, dve0)
    p.wait("tensor", dve, dve0)
    p.wait("tensor", act, act0)

    pe_mark = {}
    for dc in range(NCH):
        b = dc % 2
        if dc >= 2:
            p.wait("sync", pe, pe_mark[dc - 2])
        p.on("sync", lambda e, dc_=dc, b_=b: e.dma_start(
            out=xa[b_][:, 0:t], in_=xin_r[dc_][:, col0:col0 + t]).then_inc(lx[b_].h, 16))
        lv = lx[b].inc(16)
        p.wait("scalar", lx[b], lv)
        if dc >= 1:
            p.wait("scalar", pe, pe_mark[dc - 1])
        p.on("scalar", lambda e, b_=b: e.activation(
            out=xsq[:, 0:t], in_=xa[b_][:, 0:t], func=AF.Square).then_inc(act.h, 1))
        av = act.inc()
        p.wait("tensor", lx[b], lv)
        p.wait("tensor", act, av)
        for ti in range(nt):
            sl = slice(ti * 512, (ti + 1) * 512)
            p.on("tensor", lambda e, dc_=dc, b_=b, sl_=sl: e.matmul(
                out=Pb[0:1, sl_], lhsT=cst["ones128_f"][:, 0:1],
                rhs=xa[b_][:, sl_],
                start=(dc_ == 0), stop=(dc_ == NCH - 1), skip_group_check=True))
            p.on("tensor", lambda e, dc_=dc, sl_=sl, inc=(pe if ti == nt - 1 else None):
                 _inc(e.matmul(
                     out=Pb[0:1, t + sl_.start:t + sl_.stop],
                     lhsT=cst["ones128_f"][:, 0:1], rhs=xsq[:, sl_],
                     start=(dc_ == 0), stop=(dc_ == NCH - 1),
                     skip_group_check=True), inc))
        pe_mark[dc] = pe.inc()
    pe_done = pe.n

    p.wait("vector", pe, pe_done)
    p.on("vector", lambda e: e.tensor_scalar_mul(
        out=rows[0:1, 0:t], in0=Pb[0:1, 0:t], scalar1=1.0 / D))
    p.on("vector", lambda e: e.tensor_scalar_mul(
        out=rows[0:1, t:2 * t], in0=Pb[0:1, t:2 * t], scalar1=1.0 / D))
    p.on("vector", lambda e: e.tensor_mul(
        out=rows[0:1, 2 * t:3 * t], in0=rows[0:1, 0:t], in1=rows[0:1, 0:t]))
    p.on("vector", lambda e: e.tensor_sub(
        out=rows[0:1, t:2 * t], in0=rows[0:1, t:2 * t], in1=rows[0:1, 2 * t:3 * t]))
    p.on("vector", lambda e: e.tensor_scalar_add(
        out=rows[0:1, t:2 * t], in0=rows[0:1, t:2 * t],
        scalar1=1e-5).then_inc(dve.h, 1))
    dv = dve.inc()
    p.wait("scalar", dve, dv)
    p.on("scalar", lambda e: e.activation(
        out=rows[0:1, 2 * t:3 * t], in_=rows[0:1, t:2 * t],
        func=AF.Sqrt).then_inc(act.h, 1))
    av = act.inc()
    p.wait("vector", act, av)
    p.on("vector", lambda e: e.reciprocal(
        out=rows[0:1, t:2 * t], in_=rows[0:1, 2 * t:3 * t]).then_inc(dve.h, 1))
    dv = dve.inc()

    p.wait("tensor", dve, dv)
    for ti in range(nt):
        sl = slice(ti * 512, (ti + 1) * 512)
        p.on("tensor", lambda e, sl_=sl: e.matmul(
            out=Pb[:, sl_], lhsT=cst["ones1_f"][0:1, :],
            rhs=rows[0:1, sl_], start=True, stop=True, skip_group_check=True))
        p.on("tensor", lambda e, sl_=sl, inc=(pe if ti == nt - 1 else None):
             _inc(e.matmul(
                 out=Pb[:, t + sl_.start:t + sl_.stop], lhsT=cst["ones1_f"][0:1, :],
                 rhs=rows[0:1, t + sl_.start:t + sl_.stop], start=True, stop=True,
                 skip_group_check=True), inc))
    pe_bc = pe.inc()
    p.wait("vector", pe, pe_bc)
    p.on("vector", lambda e: e.tensor_copy(out=mu_sb[:, 0:t], in_=Pb[:, 0:t]))
    p.on("vector", lambda e: e.tensor_copy(
        out=rs_sb[:, 0:t], in_=Pb[:, t:2 * t]).then_inc(dve.h, 1))
    bc_ev = dve.inc()

    dve_mark = {}
    for dc in range(NCH):
        b = dc % 2
        if dc < 2:
            p.wait("sync", pe, pe_done)
        else:
            p.wait("sync", dve, dve_mark[dc - 2])
        p.on("sync", lambda e, dc_=dc, b_=b: e.dma_start(
            out=xa[b_][:, 0:t], in_=xin_r[dc_][:, col0:col0 + t]).then_inc(lx[b_].h, 16))
        lv = lx[b].inc(16)
        p.wait("vector", lx[b], lv)
        if dc == 0:
            p.wait("vector", dve, bc_ev)
        p.on("vector", lambda e, b_=b: e.tensor_sub(
            out=xsq[:, 0:t], in0=xa[b_][:, 0:t], in1=mu_sb[:, 0:t]))
        p.on("vector", lambda e, dc_=dc: e.tensor_mul(
            out=xln[dc_], in0=xsq[:, 0:t], in1=rs_sb[:, 0:t]).then_inc(dve.h, 1))
        dve_mark[dc] = dve.inc()
    s["ln_xln_dve"] = dve.n

    if want_u:
        ubf = s["ubf"]
        p.wait("tensor", dve, dve_mark[NCH - 1])
        for dc in range(NCH):
            for ti in range(nt):
                sl = slice(ti * 512, (ti + 1) * 512)
                last = (dc == NCH - 1 and ti == nt - 1)
                p.on("tensor", lambda e, dc_=dc, sl_=sl, inc=(pe if last else None):
                     _inc(e.matmul(
                         out=Pb[0:1, sl_], lhsT=cst["ones128_b"][:, 0:1],
                         rhs=xln[dc_][:, sl_], start=(dc_ == 0),
                         stop=(dc_ == NCH - 1), skip_group_check=True), inc))
        pe_u = pe.inc()
        p.wait("vector", pe, pe_u)
        p.on("vector", lambda e: e.tensor_copy(
            out=ubf[0:1, 0:t], in_=Pb[0:1, 0:t]).then_inc(dve.h, 1))
        s["ubf_ready"] = dve.inc()
    s["ln_last_pe"] = pe.n



def build_l0ffn(debug=False):
    nc = bass.Bass("TRN2", target_bir_lowering=False)
    xin = nc.dram_tensor("xin", [D, T], F32, kind="ExternalInput").ap()
    wg = nc.dram_tensor("wg", [NH0, 128, NCH, 128], I32, kind="ExternalInput").ap()
    wu = nc.dram_tensor("wu", [NH0, 128, NCH, 128], I32, kind="ExternalInput").ap()
    wd = nc.dram_tensor("wd", [NCH, 128, NH0, 128], I32, kind="ExternalInput").ap()
    sg = nc.dram_tensor("sg", [128, NH0], F32, kind="ExternalInput").ap()
    su = nc.dram_tensor("su", [128, NH0], F32, kind="ExternalInput").ap()
    sd = nc.dram_tensor("sd", [128, NCH], F32, kind="ExternalInput").ap()
    cst_d = {k: nc.dram_tensor("c_" + k, list(v.shape), CONST_DTYPES[k],
                               kind="ExternalInput").ap()
             for k, v in make_consts().items()}
    yout = nc.dram_tensor("yout", [D, T], F32, kind="ExternalOutput").ap()
    xin_r = xin.rearrange("(c p) t -> c p t", p=128)
    yout_r = yout.rearrange("(c p) t -> c p t", p=128)
    if debug:
        dbg_hid = nc.dram_tensor("dbg_hid", [NH0 * 128, T], F32,
                                 kind="ExternalOutput").ap()
        dbg_hid_r = dbg_hid.rearrange("(c p) t -> c p t", p=128)
        dbg_u2 = nc.dram_tensor("dbg_u2", [1, T], F32, kind="ExternalOutput").ap()
        dbg_xln = nc.dram_tensor("dbg_xln", [D, T], F32, kind="ExternalOutput").ap()
        dbg_xln_r = dbg_xln.rearrange("(c p) t -> c p t", p=128)

    p = Prog(nc)
    with ExitStack() as ctx:
        E = ctx.enter_context
        xln_all = E(nc.sbuf_tensor("xln", [128, NCH * T], BF16))
        xln = [xln_all[:, i * T:(i + 1) * T] for i in range(NCH)]
        hid_all = E(nc.sbuf_tensor("hid", [128, NH0 * T], BF16))
        hid = [hid_all[:, i * T:(i + 1) * T] for i in range(NH0)]
        xa = [E(nc.sbuf_tensor(f"xa{i}", [128, T], F32)) for i in range(2)]
        xsq = E(nc.sbuf_tensor("xsq", [128, T], F32))
        rows = E(nc.sbuf_tensor("rows", [1, 3 * T], F32))
        ubf = E(nc.sbuf_tensor("ubf", [1, T], BF16))
        u2bf = E(nc.sbuf_tensor("u2bf", [1, T], BF16))
        wst_i = [E(nc.sbuf_tensor(f"wsti{i}", [128, 512], I32)) for i in range(2)]
        wgb = E(nc.sbuf_tensor("wgb", [128, NCH * 128], BF16))
        wub = E(nc.sbuf_tensor("wub", [128, NCH * 128], BF16))
        wdst_i = [E(nc.sbuf_tensor(f"wdsti{i}", [128, 1408], I32)) for i in range(2)]
        wdb = E(nc.sbuf_tensor("wdb", [128, 22 * 128], BF16))
        sg_t = E(nc.sbuf_tensor("sg_t", [128, NH0], F32))
        su_t = E(nc.sbuf_tensor("su_t", [128, NH0], F32))
        sd_t = E(nc.sbuf_tensor("sd_t", [128, NCH], F32))
        sd5_t = E(nc.sbuf_tensor("sd5_t", [128, NCH], F32))
        c_ones128_f = E(nc.sbuf_tensor("c1f", [128, 1], F32))
        c_ones128_b = E(nc.sbuf_tensor("c1b", [128, 1], BF16))
        c_ones1_f = E(nc.sbuf_tensor("c1rf", [1, 128], F32))
        c_m128_b = E(nc.sbuf_tensor("cm128", [1, 128], BF16))
        sgbuf = E(nc.sbuf_tensor("sgbuf", [128, T], BF16))
        subuf = E(nc.sbuf_tensor("subuf", [128, T], BF16))
        ybuf = [E(nc.sbuf_tensor(f"ybuf{i}", [128, T], F32)) for i in range(2)]
        ytmp = E(nc.sbuf_tensor("ytmp", [128, T], F32))
        PA = E(nc.psum_tensor("PA", [128, 2 * T], F32))
        PB = E(nc.psum_tensor("PB", [128, 2 * T], F32))
        mu_sb = E(nc.sbuf_tensor("mu_sb", [128, T], F32))
        rs_sb = E(nc.sbuf_tensor("rs_sb", [128, T], F32))

        lx = [Sem(E(nc.semaphore(f"lx{i}"))) for i in range(2)]
        lw = [Sem(E(nc.semaphore(f"lw{i}"))) for i in range(2)]
        lwg = Sem(E(nc.semaphore("lwg")))
        gx = [Sem(E(nc.semaphore(f"gx{i}"))) for i in range(2)]
        stp = [Sem(E(nc.semaphore(f"st{i}"))) for i in range(2)]
        ldc = Sem(E(nc.semaphore("ldc")))
        std = Sem(E(nc.semaphore("std")))
        pe = Sem(E(nc.semaphore("pe")))
        act = Sem(E(nc.semaphore("act")))
        dve = Sem(E(nc.semaphore("dve")))

        cstt = {"ones128_f": c_ones128_f, "ones128_b": c_ones128_b,
                "ones1_f": c_ones1_f, "m128_b": c_m128_b}
        s = dict(lx=lx, pe=pe, act=act, dve=dve, cst=cstt, xa=xa, xsq=xsq,
                 Pbig=PA, mu_sb=mu_sb, rs_sb=rs_sb, rows=rows, ubf=ubf)

        for dst, src in [(c_ones128_f, cst_d["ones128_f"]),
                         (c_ones128_b, cst_d["ones128_b"]),
                         (c_ones1_f, cst_d["ones1_f"]),
                         (c_m128_b, cst_d["m128_b"]),
                         (sg_t, sg), (su_t, su), (sd_t, sd)]:
            p.on("gpsimd", lambda e, d=dst, sr=src: e.dma_start(
                out=d[:], in_=sr).then_inc(ldc.h, 16))
            ldc.inc(16)
        consts_ok = ldc.n
        p.wait("vector", ldc, consts_ok)
        p.on("vector", lambda e: e.tensor_scalar_mul(
            out=sd5_t[:], in0=sd_t[:], scalar1=0.5).then_inc(dve.h, 1))
        sd5_ready = dve.inc()
        p.wait("tensor", ldc, consts_ok)
        p.wait("scalar", ldc, consts_ok)

        emit_ln(p, s, xin_r, xln, t=T)

        conv_mark = {}
        mm_mark = {}
        evac_mark = {}
        hid_mark = {}
        gq = 0
        for hc in range(NH0):
            par = hc % 2
            pg = PA[:, par * T:(par + 1) * T]
            pu = PB[:, par * T:(par + 1) * T]
            for mi, (wsrc, wdstb) in enumerate([(wg, wgb), (wu, wub)]):
                for q in range(4):
                    sb = gq % 2
                    if gq >= 2:
                        p.wait("sync", act, conv_mark[gq - 2])
                    p.on("sync", lambda e, hc_=hc, q_=q, sb_=sb, w_=wsrc: e.dma_start(
                        out=wst_i[sb_][:],
                        in_=w_[hc_, :, q_ * 4:(q_ + 1) * 4, :].rearrange(
                            "p a b -> p (a b)")).then_inc(lw[sb_].h, 16))
                    lv = lw[sb].inc(16)
                    p.wait("scalar", lw[sb], lv)
                    if hc > 0 and q == 0:
                        p.wait("scalar", pe, mm_mark[hc - 1])
                    p.on("scalar", lambda e, sb_=sb, q_=q, d_=wdstb: e.activation(
                        out=d_[:, q_ * 512:(q_ + 1) * 512], in_=wst_i[sb_][:],
                        func=AF.Copy).then_inc(act.h, 1))
                    conv_mark[gq] = act.inc()
                    gq += 1
            wready = act.n
            p.wait("tensor", act, wready)
            if hc == 0:
                p.wait("tensor", dve, s["ubf_ready"])
            if hc >= 2:
                p.wait("tensor", act, evac_mark[hc - 2])
            for ti in range(2):
                sl = slice(ti * 512, (ti + 1) * 512)
                for dc in range(NCH):
                    p.on("tensor", lambda e, dc_=dc, sl_=sl, pg_=pg: e.matmul(
                        out=pg_[:, sl_], lhsT=wgb[:, dc_ * 128:(dc_ + 1) * 128],
                        rhs=xln[dc_][:, sl_], start=(dc_ == 0), stop=False,
                        skip_group_check=True))
                p.on("tensor", lambda e, sl_=sl, pg_=pg: e.matmul(
                    out=pg_[:, sl_], lhsT=cstt["m128_b"][0:1, :],
                    rhs=ubf[0:1, sl_], start=False, stop=True,
                    skip_group_check=True))
                for dc in range(NCH):
                    p.on("tensor", lambda e, dc_=dc, sl_=sl, pu_=pu: e.matmul(
                        out=pu_[:, sl_], lhsT=wub[:, dc_ * 128:(dc_ + 1) * 128],
                        rhs=xln[dc_][:, sl_], start=(dc_ == 0), stop=False,
                        skip_group_check=True))
                last = (ti == 1)
                p.on("tensor", lambda e, sl_=sl, pu_=pu, last_=last: (
                    e.matmul(out=pu_[:, sl_], lhsT=cstt["m128_b"][0:1, :],
                             rhs=ubf[0:1, sl_], start=False, stop=True,
                             skip_group_check=True).then_inc(pe.h, 1) if last_ else
                    e.matmul(out=pu_[:, sl_], lhsT=cstt["m128_b"][0:1, :],
                             rhs=ubf[0:1, sl_], start=False, stop=True,
                             skip_group_check=True)))
            mm_mark[hc] = pe.inc()
            p.wait("scalar", pe, mm_mark[hc])
            if hc >= 1:
                p.wait("scalar", dve, hid_mark[hc - 1])
            for ti in range(2):
                sl = slice(ti * 512, (ti + 1) * 512)
                p.on("scalar", lambda e, hc_=hc, sl_=sl, pg_=pg: e.activation(
                    out=sgbuf[:, sl_], in_=pg_[:, sl_], func=AF.Silu,
                    scale=sg_t[:, hc_:hc_ + 1]))
                last = (ti == 1)
                p.on("scalar", lambda e, hc_=hc, sl_=sl, pu_=pu, last_=last: (
                    e.activation(out=subuf[:, sl_], in_=pu_[:, sl_], func=AF.Copy,
                                 scale=su_t[:, hc_:hc_ + 1]).then_inc(act.h, 1)
                    if last_ else
                    e.activation(out=subuf[:, sl_], in_=pu_[:, sl_], func=AF.Copy,
                                 scale=su_t[:, hc_:hc_ + 1])))
            evac_mark[hc] = act.inc()
            p.wait("vector", act, evac_mark[hc])
            p.on("vector", lambda e, hc_=hc: e.tensor_mul(
                out=hid[hc_][:, 0:T], in0=sgbuf[:, 0:T],
                in1=subuf[:, 0:T]).then_inc(dve.h, 1))
            hid_mark[hc] = dve.inc()
        hid_done = dve.n

        p.wait("tensor", dve, hid_done)
        for hc in range(NH0):
            for ti in range(2):
                last = (hc == NH0 - 1 and ti == 1)
                sl = slice(ti * 512, (ti + 1) * 512)
                p.on("tensor", lambda e, hc_=hc, sl_=sl, last_=last: (
                    e.matmul(out=PA[0:1, sl_], lhsT=cstt["ones128_b"][:, 0:1],
                             rhs=hid[hc_][:, sl_], start=(hc_ == 0),
                             stop=(hc_ == NH0 - 1),
                             skip_group_check=True).then_inc(pe.h, 1) if last_ else
                    e.matmul(out=PA[0:1, sl_], lhsT=cstt["ones128_b"][:, 0:1],
                             rhs=hid[hc_][:, sl_], start=(hc_ == 0),
                             stop=(hc_ == NH0 - 1), skip_group_check=True)))
        u2_pe = pe.inc()
        p.wait("vector", pe, u2_pe)
        p.on("vector", lambda e: e.tensor_copy(
            out=u2bf[0:1, 0:T], in_=PA[0:1, 0:T]).then_inc(dve.h, 1))
        u2_ready = dve.inc()

        dconv_mark = {}
        dmm_mark = {}
        devac_mark = {}
        y_mark = {}
        gq = 0
        for dc in range(NCH):
            par = dc % 2
            pd = PA[:, par * T:(par + 1) * T]
            if dc < 2:
                p.wait("gpsimd", dve, u2_ready)
            else:
                p.wait("gpsimd", dve, y_mark[dc - 2])
            p.on("gpsimd", lambda e, dc_=dc, par_=par: e.dma_start(
                out=xa[par_][:], in_=xin_r[dc_]).then_inc(gx[par_].h, 16))
            xv = gx[par].inc(16)
            for half in range(2):
                for qq in range(2):
                    q = half * 2 + qq
                    sb = gq % 2
                    if gq >= 2:
                        p.wait("sync", act, dconv_mark[gq - 2])
                    p.on("sync", lambda e, dc_=dc, q_=q, sb_=sb: e.dma_start(
                        out=wdst_i[sb_][:],
                        in_=wd[dc_, :, q_ * 11:(q_ + 1) * 11, :].rearrange(
                            "p a b -> p (a b)")).then_inc(lw[sb_].h, 16))
                    lv = lw[sb].inc(16)
                    p.wait("scalar", lw[sb], lv)
                    if gq >= 2:
                        p.wait("scalar", pe, dmm_mark[(gq - 2) // 2])
                    p.on("scalar", lambda e, sb_=sb, qq_=qq: e.activation(
                        out=wdb[:, qq_ * 1408:(qq_ + 1) * 1408], in_=wdst_i[sb_][:],
                        func=AF.Copy).then_inc(act.h, 1))
                    dconv_mark[gq] = act.inc()
                    gq += 1
                wready = act.n
                p.wait("tensor", act, wready)
                if dc == 0 and half == 0:
                    p.wait("tensor", dve, u2_ready)
                if dc >= 2 and half == 0:
                    p.wait("tensor", act, devac_mark[dc - 2])
                for ti in range(2):
                    sl = slice(ti * 512, (ti + 1) * 512)
                    for hh in range(22):
                        hc = half * 22 + hh
                        lastmm = (half == 0 and ti == 1 and hh == 21)
                        p.on("tensor", lambda e, hc_=hc, hh_=hh, sl_=sl, pd_=pd,
                             h0=(half == 0 and hh == 0), lm=lastmm: (
                            e.matmul(
                                out=pd_[:, sl_],
                                lhsT=wdb[:, hh_ * 128:(hh_ + 1) * 128],
                                rhs=hid[hc_][:, sl_], start=h0, stop=False,
                                skip_group_check=True).then_inc(pe.h, 1) if lm else
                            e.matmul(
                                out=pd_[:, sl_],
                                lhsT=wdb[:, hh_ * 128:(hh_ + 1) * 128],
                                rhs=hid[hc_][:, sl_], start=h0, stop=False,
                                skip_group_check=True)))
                    if half == 1:
                        last = (ti == 1)
                        p.on("tensor", lambda e, sl_=sl, pd_=pd, last_=last: (
                            e.matmul(out=pd_[:, sl_], lhsT=cstt["m128_b"][0:1, :],
                                     rhs=u2bf[0:1, sl_], start=False, stop=True,
                                     skip_group_check=True).then_inc(pe.h, 1)
                            if last_ else
                            e.matmul(out=pd_[:, sl_], lhsT=cstt["m128_b"][0:1, :],
                                     rhs=u2bf[0:1, sl_], start=False, stop=True,
                                     skip_group_check=True)))
                dmm_mark[dc * 2 + half] = pe.inc()
            p.wait("scalar", pe, dmm_mark[dc * 2 + 1])
            p.wait("scalar", dve, sd5_ready)
            if dc >= 1:
                p.wait("scalar", dve, y_mark[dc - 1])
            for ti in range(2):
                sl = slice(ti * 512, (ti + 1) * 512)
                last = (ti == 1)
                p.on("scalar", lambda e, dc_=dc, sl_=sl, pd_=pd, last_=last: (
                    e.activation(out=ytmp[:, sl_], in_=pd_[:, sl_], func=AF.Copy,
                                 scale=sd5_t[:, dc_:dc_ + 1]).then_inc(act.h, 1)
                    if last_ else
                    e.activation(out=ytmp[:, sl_], in_=pd_[:, sl_], func=AF.Copy,
                                 scale=sd5_t[:, dc_:dc_ + 1])))
            devac_mark[dc] = act.inc()
            p.wait("vector", act, devac_mark[dc])
            p.wait("vector", gx[par], xv)
            if dc >= 2:
                p.wait("vector", stp[par], stp[par].n)
            p.on("vector", lambda e, par_=par: e.tensor_add(
                out=ybuf[par_][:], in0=ytmp[:], in1=xa[par_][:]).then_inc(dve.h, 1))
            y_mark[dc] = dve.inc()
            p.wait("gpsimd", dve, y_mark[dc])
            p.on("gpsimd", lambda e, dc_=dc, par_=par: e.dma_start(
                out=yout_r[dc_], in_=ybuf[par_][:]).then_inc(stp[par_].h, 16))
            stp[par].inc(16)

        p.wait("gpsimd", stp[0], stp[0].n)
        p.wait("gpsimd", stp[1], stp[1].n)
        if debug:
            p.wait("vector", stp[0], stp[0].n)
            p.wait("vector", stp[1], stp[1].n)
            p.on("vector", lambda e: e.tensor_copy(
                out=ytmp[0:1, 0:T], in_=u2bf[0:1, 0:T]).then_inc(dve.h, 1))
            v = dve.inc()
            p.wait("gpsimd", dve, v)
            p.on("gpsimd", lambda e: e.dma_start(
                out=dbg_u2, in_=ytmp[0:1, 0:T]).then_inc(std.h, 16))
            std.inc(16)
            for hc in range(NH0):
                p.wait("vector", std, std.n)
                p.on("vector", lambda e, hc_=hc: e.tensor_copy(
                    out=ytmp[:], in_=hid[hc_][:, 0:T]).then_inc(dve.h, 1))
                v = dve.inc()
                p.wait("gpsimd", dve, v)
                p.on("gpsimd", lambda e, hc_=hc: e.dma_start(
                    out=dbg_hid_r[hc_], in_=ytmp[:]).then_inc(std.h, 16))
                std.inc(16)
            for dc in range(NCH):
                p.wait("vector", std, std.n)
                p.on("vector", lambda e, dc_=dc: e.tensor_copy(
                    out=ytmp[:], in_=xln[dc_][:, 0:T]).then_inc(dve.h, 1))
                v = dve.inc()
                p.wait("gpsimd", dve, v)
                p.on("gpsimd", lambda e, dc_=dc: e.dma_start(
                    out=dbg_xln_r[dc_], in_=ytmp[:]).then_inc(std.h, 16))
                std.inc(16)
        if debug:
            p.wait("gpsimd", std, std.n)

        p.finalize()
    return nc



S_FULL = 2048
NVH = 8


def build_l0attn(debug=False):
    nc = bass.Bass("TRN2", target_bir_lowering=False)
    xin = nc.dram_tensor("xin", [D, S_FULL], F32, kind="ExternalInput").ap()
    wq_s = nc.dram_tensor("wq_s", [NVH, 128, NCH, 128], F32, kind="ExternalInput").ap()
    wk_s = nc.dram_tensor("wk_s", [NVH, 128, NCH, 128], F32, kind="ExternalInput").ap()
    wv_s = nc.dram_tensor("wv_s", [NVH, 128, NCH, 128], F32, kind="ExternalInput").ap()
    cc_d = nc.dram_tensor("cc_t", [128, S_FULL], F32, kind="ExternalInput").ap()
    ss_d = nc.dram_tensor("ss_t", [128, S_FULL], F32, kind="ExternalInput").ap()
    lam_d = nc.dram_tensor("lam_in", [1, 256], F32, kind="ExternalInput").ap()
    cst_d = {k: nc.dram_tensor("c_" + k, list(v.shape), CONST_DTYPES[k],
                               kind="ExternalInput").ap()
             for k, v in make_consts().items()}
    aout = nc.dram_tensor("aout", [NVH * 128, S_FULL], BF16, kind="ExternalOutput").ap()
    xin_r = xin.rearrange("(c p) t -> c p t", p=128)
    aout_r = aout.rearrange("(c p) t -> c p t", p=128)
    if debug:
        dbg_q = nc.dram_tensor("dbg_q", [128, NVH * S_FULL], BF16, kind="ExternalOutput").ap()
        dbg_k = nc.dram_tensor("dbg_k", [128, NVH * S_FULL], BF16, kind="ExternalOutput").ap()
        dbg_v = nc.dram_tensor("dbg_v", [128, NVH * S_FULL], BF16, kind="ExternalOutput").ap()
        dbg_p0 = nc.dram_tensor("dbg_p0", [128, 2048], BF16, kind="ExternalOutput").ap()
        dbg_d2 = nc.dram_tensor("dbg_d2", [128, 2048], BF16, kind="ExternalOutput").ap()
        dbg_mrow = nc.dram_tensor("dbg_mrow", [128, 8], F32, kind="ExternalOutput").ap()
        dbg_lrows = nc.dram_tensor("dbg_lrows", [1, 256], F32, kind="ExternalOutput").ap()
        dbg_lamn = nc.dram_tensor("dbg_lamn", [128, 1], F32, kind="ExternalOutput").ap()

    sbuf = nc.sbuf_tensor
    psum = nc.psum_tensor
    p = Prog(nc)
    with ExitStack() as ctx:
        E = ctx.enter_context
        Q = E(nc.sbuf_tensor("Qb", [128, NVH * S_FULL], BF16))
        K = E(nc.sbuf_tensor("Kb", [128, NVH * S_FULL], BF16))
        V = E(nc.sbuf_tensor("Vb", [128, NVH * S_FULL], BF16))
        lamn = E(nc.sbuf_tensor("lamn", [128, 1], F32))
        c_ones128_f = E(nc.sbuf_tensor("c1f", [128, 1], F32))
        c_ones128_b = E(nc.sbuf_tensor("c1b", [128, 1], BF16))
        c_ones1_f = E(nc.sbuf_tensor("c1rf", [1, 128], F32))
        c_ident_b = E(nc.sbuf_tensor("cidb", [128, 128], BF16))
        lrows = E(nc.sbuf_tensor("lrows", [1, 256], F32))
        S4 = E(nc.psum_tensor("S4", [128, 2048], F32))
        Tp = E(nc.psum_tensor("Tp", [128, 2048], BF16))
        Po = E(nc.psum_tensor("Po", [128, 512], F32))
        Pr = E(nc.psum_tensor("Pr", [128, 512], F32))
        lx = [Sem(E(nc.semaphore(f"lx{i}"))) for i in range(2)]
        lw = [Sem(E(nc.semaphore(f"lw{i}"))) for i in range(2)]
        lwg = Sem(E(nc.semaphore("lwg")))
        gsw = [Sem(E(nc.semaphore(f"gsw{i}"))) for i in range(2)]
        stp = [Sem(E(nc.semaphore(f"st{i}"))) for i in range(2)]
        ldc = Sem(E(nc.semaphore("ldc")))
        pe = Sem(E(nc.semaphore("pe")))
        act = Sem(E(nc.semaphore("act")))
        dve = Sem(E(nc.semaphore("dve")))

        cstt = {"ones128_f": c_ones128_f, "ones128_b": c_ones128_b,
                "ones1_f": c_ones1_f, "ident_b": c_ident_b}

        for dst, srcp in [(c_ones128_f, cst_d["ones128_f"]),
                          (c_ones128_b, cst_d["ones128_b"]),
                          (c_ones1_f, cst_d["ones1_f"]),
                          (c_ident_b, cst_d["ident_b"])]:
            p.on("gpsimd", lambda e, d=dst, sr=srcp: e.dma_start(
                out=d[:], in_=sr).then_inc(ldc.h, 16))
            ldc.inc(16)

        with ExitStack() as ph1:
            E1 = ph1.enter_context
            xln_all = E1(sbuf("xlnh", [128, NCH * T], BF16))
            xln = [xln_all[:, i * T:(i + 1) * T] for i in range(NCH)]
            xa = [E1(sbuf(f"xa{i}", [128, T], F32)) for i in range(2)]
            xsq = E1(sbuf("xsq", [128, T], F32))
            rows = E1(sbuf("rows", [1, 3 * T], F32))
            mu_sb = E1(sbuf("mu_sb", [128, T], F32))
            rs_sb = E1(sbuf("rs_sb", [128, T], F32))
            cc_t = E1(sbuf("cc_s", [128, S_FULL], F32))
            ss_t = E1(sbuf("ss_s", [128, S_FULL], F32))
            wstg = E1(sbuf("wstg", [128, T], F32))
            wbf = E1(sbuf("wbf", [128, 2048], BF16))
            t1b = E1(sbuf("t1b", [128, T], F32))
            t2b = E1(sbuf("t2b", [128, T], F32))
            t2s = E1(sbuf("t2s", [128, T], F32))
            vtmp = E1(sbuf("vtmp", [128, T], BF16))
            lam4 = E1(sbuf("lam4", [1, 256], F32))

            p.on("gpsimd", lambda e: e.dma_start(out=cc_t[:], in_=cc_d).then_inc(ldc.h, 16))
            ldc.inc(16)
            p.on("gpsimd", lambda e: e.dma_start(out=ss_t[:], in_=ss_d).then_inc(ldc.h, 16))
            ldc.inc(16)
            p.on("gpsimd", lambda e: e.dma_start(out=lam4[:], in_=lam_d).then_inc(ldc.h, 16))
            consts_ok = ldc.inc(16)
            p.wait("tensor", ldc, consts_ok)
            p.wait("scalar", ldc, consts_ok)
            p.wait("vector", ldc, consts_ok)

            p.on("vector", lambda e: e.tensor_mul(
                out=lrows[0:1, 0:64], in0=lam4[0:1, 0:64], in1=lam4[0:1, 128:192]))
            p.on("vector", lambda e: e.tensor_mul(
                out=lrows[0:1, 64:128], in0=lam4[0:1, 64:128],
                in1=lam4[0:1, 192:256]).then_inc(dve.h, 1))
            p.wait("vector", dve, dve.inc())
            p.on("vector", lambda e: e.reduce_sum(
                out=lrows[0:1, 128:129], in_=lrows[0:1, 0:64],
                axis=AX.X).then_inc(dve.h, 1))
            p.wait("vector", dve, dve.inc())
            p.on("vector", lambda e: e.reduce_sum(
                out=lrows[0:1, 129:130], in_=lrows[0:1, 64:128],
                axis=AX.X).then_inc(dve.h, 1))
            dv = dve.inc()
            p.wait("scalar", dve, dv)
            p.on("scalar", lambda e: e.activation(
                out=lrows[0:1, 130:132], in_=lrows[0:1, 128:130],
                func=AF.Exp).then_inc(act.h, 1))
            av = act.inc()
            p.wait("vector", act, av)
            p.on("vector", lambda e: e.tensor_sub(
                out=lrows[0:1, 132:133], in0=lrows[0:1, 130:131],
                in1=lrows[0:1, 131:132]).then_inc(dve.h, 1))
            p.wait("vector", dve, dve.inc())
            p.on("vector", lambda e: e.tensor_scalar(
                out=lrows[0:1, 133:134], in0=lrows[0:1, 132:133],
                scalar1=LAMBDA_INIT, scalar2=-1.0, op0=ALU.add,
                op1=ALU.mult).then_inc(dve.h, 1))
            dv = dve.inc()
            p.wait("tensor", dve, dv)
            p.on("tensor", lambda e: e.matmul(
                out=Pr[:, 0:1], lhsT=cstt["ones1_f"][0:1, :],
                rhs=lrows[0:1, 133:134], start=True, stop=True,
                skip_group_check=True).then_inc(pe.h, 1))
            pv = pe.inc()
            p.wait("vector", pe, pv)
            p.on("vector", lambda e: e.tensor_copy(
                out=lamn[:, 0:1], in_=Pr[:, 0:1]).then_inc(dve.h, 1))
            lam_ready = dve.inc()
            if debug:
                p.wait("gpsimd", dve, lam_ready)
                p.on("gpsimd", lambda e: e.dma_start(
                    out=dbg_lrows, in_=lrows[0:1, :]).then_inc(ldc.h, 16))
                ldc.inc(16)
                p.on("gpsimd", lambda e: e.dma_start(
                    out=dbg_lamn, in_=lamn[:, 0:1]).then_inc(ldc.h, 16))
                ldc.inc(16)

            s = dict(lx=lx, pe=pe, act=act, dve=dve, cst=cstt, xa=xa, xsq=xsq,
                     Pbig=S4, mu_sb=mu_sb, rs_sb=rs_sb, rows=rows)

            wconv_mark = {}
            proj_mark = {}
            rope_mark = {}
            swp_ctr = 0
            last_rope = None
            last_swap = (None, 0)
            for th in range(2):
                emit_ln(p, s, xin_r, xln, t=T, col0=th * T, want_u=False)
                xln_ok = s["ln_xln_dve"]
                for pi, wsrc in enumerate([wq_s, wk_s, wv_s]):
                    for oc in range(NVH):
                        gi = (th * 3 + pi) * NVH + oc
                        par = gi % 2
                        psl = slice(par * 1024, (par + 1) * 1024)
                        for hf in range(2):
                            sb = (gi * 2 + hf) % 2
                            prev = gi * 2 + hf - 1
                            if prev >= 0:
                                p.wait("sync", act, wconv_mark[prev])
                            p.on("sync", lambda e, pi_=pi, oc_=oc, hf_=hf, sb_=sb,
                                 w_=wsrc: e.dma_start(
                                out=wstg[:, 0:T],
                                in_=w_[oc_, :, hf_ * 8:(hf_ + 1) * 8, :].rearrange(
                                    "p a b -> p (a b)")).then_inc(lw[sb_].h, 16))
                            lv = lw[sb].inc(16)
                            p.wait("scalar", lw[sb], lv)
                            if gi > 0 and hf == 0:
                                p.wait("scalar", pe, proj_mark[gi - 1])
                            p.on("scalar", lambda e, hf_=hf: e.activation(
                                out=wbf[:, hf_ * 1024:(hf_ + 1) * 1024],
                                in_=wstg[:, 0:T], func=AF.Copy).then_inc(act.h, 1))
                            wconv_mark[gi * 2 + hf] = act.inc()
                        wready = act.n
                        p.wait("tensor", act, wready)
                        if th == 0 and pi == 0 and oc == 0:
                            p.wait("tensor", dve, xln_ok)
                        if gi >= 2:
                            p.wait("tensor", dve, rope_mark[gi - 2])
                        for ti in range(2):
                            for dc in range(NCH):
                                last = (ti == 1 and dc == NCH - 1)
                                p.on("tensor", lambda e, dc_=dc, ti_=ti, psl_=psl,
                                     inc=(pe if last else None): _inc(e.matmul(
                                        out=S4[:, psl_.start + ti_ * 512:
                                               psl_.start + (ti_ + 1) * 512],
                                        lhsT=wbf[:, dc_ * 128:(dc_ + 1) * 128],
                                        rhs=xln[dc_][:, ti_ * 512:(ti_ + 1) * 512],
                                        start=(dc_ == 0), stop=(dc_ == NCH - 1),
                                        skip_group_check=True), inc))
                        proj_mark[gi] = pe.inc()
                        if pi < 2:
                            dst = Q if pi == 0 else K
                            p.wait("vector", pe, proj_mark[gi])
                            if last_swap[0] is not None:
                                p.wait("vector", last_swap[0], last_swap[1])
                            p.on("vector", lambda e, psl_=psl, th_=th: e.tensor_mul(
                                out=t1b[:, 0:T], in0=S4[:, psl_],
                                in1=cc_t[:, th_ * T:(th_ + 1) * T]))
                            p.on("vector", lambda e, psl_=psl, th_=th: e.tensor_mul(
                                out=t2b[:, 0:T], in0=S4[:, psl_],
                                in1=ss_t[:, th_ * T:(th_ + 1) * T]).then_inc(dve.h, 1))
                            t2v = dve.inc()
                            sw = swp_ctr % 2
                            swp_ctr += 1
                            p.wait("gpsimd", dve, t2v)
                            if last_rope is not None:
                                p.wait("gpsimd", dve, last_rope)
                            for (ob, ib) in ((0, 32), (32, 0), (64, 96), (96, 64)):
                                p.on("gpsimd", lambda e, ob_=ob, ib_=ib, sw_=sw:
                                     e.dma_start(
                                        out=t2s[ob_:ob_ + 32, 0:T],
                                        in_=t2b[ib_:ib_ + 32, 0:T]).then_inc(
                                        gsw[sw_].h, 16))
                            swv = gsw[sw].inc(64)
                            p.wait("vector", gsw[sw], swv)
                            p.on("vector", lambda e, oc_=oc, th_=th, d_=dst: e.tensor_add(
                                out=d_[:, oc_ * S_FULL + th_ * T:
                                       oc_ * S_FULL + (th_ + 1) * T],
                                in0=t1b[:, 0:T], in1=t2s[:, 0:T]).then_inc(dve.h, 1))
                            rope_mark[gi] = dve.inc()
                            last_rope = rope_mark[gi]
                            last_swap = (gsw[sw], swv)
                        else:
                            p.wait("vector", pe, proj_mark[gi])
                            p.on("vector", lambda e, psl_=psl: e.tensor_copy(
                                out=vtmp[:, 0:T], in_=S4[:, psl_]).then_inc(dve.h, 1))
                            vev = dve.inc()
                            p.wait("tensor", dve, vev)
                            for tc in range(8):
                                last = (tc == 7)
                                p.on("tensor", lambda e, tc_=tc,
                                     inc=(pe if last else None): _inc(e.transpose(
                                        out=Tp[:, tc_ * 128:(tc_ + 1) * 128],
                                        in_=vtmp[:, tc_ * 128:(tc_ + 1) * 128],
                                        identity=cstt["ident_b"][:, :]), inc))
                            trv = pe.inc()
                            p.wait("vector", pe, trv)
                            p.on("vector", lambda e, oc_=oc, th_=th: e.tensor_copy(
                                out=V[:, oc_ * S_FULL + th_ * T:
                                      oc_ * S_FULL + (th_ + 1) * T],
                                in_=Tp[:, 0:T]).then_inc(dve.h, 1))
                            rope_mark[gi] = dve.inc()
            proj_all_pe = pe.n
            proj_all_dve = dve.n

        with ExitStack() as ph2:
            E2 = ph2.enter_context
            attn_sb = E2(sbuf("attn_sb", [128, NVH * S_FULL], BF16))
            phat0 = E2(sbuf("phat0", [128, 2048], BF16))
            phat1 = E2(sbuf("phat1", [128, 2048], BF16))
            dbuf = E2(sbuf("dbuf", [128, 2048], BF16))
            dbuf2 = E2(sbuf("dbuf2", [128, 2048], BF16))
            diffT = E2(sbuf("diffT", [128, 16 * 512], BF16))
            sqf = E2(sbuf("sqf", [128, 512], F32))
            rs512 = E2(sbuf("rs512", [128, 512], F32))
            mrow = E2(sbuf("mrow", [128, 8], F32))
            for eng in ("tensor", "vector", "scalar", "gpsimd"):
                p.wait(eng, dve, proj_all_dve)
                p.wait(eng, pe, proj_all_pe)
            p.wait("vector", dve, lam_ready)

            sc_mark = {}
            cb_mark = {}
            tp_mark = {}
            pv_mark = {}
            ao_mark = {}
            last_tp_ev = None
            last_cb = None
            last_tp = None
            for vh in range(NVH):
                for qg in range(4):
                    for qt in range(4):
                        qgi = qg * 4 + qt
                        for hh in range(2):
                            hb = hh * 64
                            if (vh, qg, qt, hh) != (0, 0, 0, 0):
                                p.wait("tensor", act, sc_mark[("exp_prev",)])
                            if hh == 0 and last_cb is not None:
                                p.wait("scalar", dve, last_cb)
                            for kt in range(4):
                                last = (kt == 3)
                                p.on("tensor", lambda e, vh_=vh, qgi_=qgi, hb_=hb,
                                     kt_=kt, inc=(pe if last else None): _inc(e.matmul(
                                        out=S4[:, kt_ * 512:(kt_ + 1) * 512],
                                        lhsT=Q[hb_:hb_ + 64,
                                               vh_ * S_FULL + qgi_ * 128:
                                               vh_ * S_FULL + (qgi_ + 1) * 128],
                                        rhs=K[hb_:hb_ + 64,
                                              vh_ * S_FULL + kt_ * 512:
                                              vh_ * S_FULL + (kt_ + 1) * 512],
                                        start=True, stop=True,
                                        skip_group_check=True), inc))
                            scv = pe.inc()
                            p.wait("vector", pe, scv)
                            p.on("vector", lambda e, hh_=hh: e.reduce_max(
                                out=mrow[:, hh_:hh_ + 1], in_=S4[:, 0:2048],
                                axis=AX.X, negate=True).then_inc(dve.h, 1))
                            mv = dve.inc()
                            p.wait("scalar", dve, mv)
                            p.on("scalar", lambda e, hh_=hh,
                                 ph_=(phat0 if hh == 0 else phat1): e.activation(
                                out=ph_[:], in_=S4[:, 0:2048], func=AF.Exp,
                                bias=mrow[:, hh_:hh_ + 1],
                                accum_out=mrow[:, 2 + hh_:3 + hh_]).then_inc(act.h, 1))
                            sc_mark[("exp_prev",)] = act.inc()
                        ev = sc_mark[("exp_prev",)]
                        p.wait("vector", act, ev)
                        if last_tp is not None:
                            p.wait("vector", pe, last_tp)
                        p.on("vector", lambda e: e.reciprocal(
                            out=mrow[:, 4:5], in_=mrow[:, 2:3]))
                        p.on("vector", lambda e: e.reciprocal(
                            out=mrow[:, 5:6], in_=mrow[:, 3:4]).then_inc(dve.h, 1))
                        p.wait("vector", dve, dve.inc())
                        p.on("vector", lambda e: e.tensor_mul(
                            out=mrow[:, 6:7], in0=mrow[:, 5:6],
                            in1=lamn[:, 0:1]).then_inc(dve.h, 1))
                        p.wait("vector", dve, dve.inc())
                        p.on("vector", lambda e: e.tensor_scalar_mul(
                            out=dbuf[:], in0=phat0[:], scalar1=mrow[:, 4:5]))
                        p.on("vector", lambda e: e.scalar_tensor_tensor(
                            out=dbuf2[:], in0=phat1[:], scalar=mrow[:, 6:7],
                            in1=dbuf[:], op0=ALU.mult,
                            op1=ALU.add).then_inc(dve.h, 1))
                        cb_mark[qt] = dve.inc()
                        p.wait("tensor", dve, cb_mark[qt])
                        if last_tp_ev is not None:
                            p.wait("tensor", dve, last_tp_ev)
                        if qt == 0 and (vh, qg) != (0, 0):
                            p.wait("tensor", pe, pv_mark[("pv",)])
                        for kt in range(16):
                            last = (kt == 15)
                            p.on("tensor", lambda e, kt_=kt,
                                 inc=(pe if last else None): _inc(e.transpose(
                                    out=Tp[:, kt_ * 128:(kt_ + 1) * 128],
                                    in_=dbuf2[:, kt_ * 128:(kt_ + 1) * 128],
                                    identity=cstt["ident_b"][:, :]), inc))
                        tp_mark[qt] = pe.inc()
                        last_tp = tp_mark[qt]
                        p.wait("vector", pe, tp_mark[qt])
                        dTr = diffT.rearrange("p (k q) -> p k q", q=512)
                        p.on("vector", lambda e, qt_=qt, dTr_=dTr: e.tensor_copy(
                            out=dTr_[:, :, qt_ * 128:(qt_ + 1) * 128],
                            in_=Tp.rearrange("p (k q) -> p k q", q=128)[:, :, :]
                        ).then_inc(dve.h, 1))
                        tp_mark[("ev", qt)] = dve.inc()
                        last_tp_ev = tp_mark[("ev", qt)]
                    p.wait("tensor", dve, tp_mark[("ev", 3)])
                    if (vh, qg) != (0, 0):
                        p.wait("tensor", dve, ao_mark[("ao",)])
                    for kt in range(16):
                        last = (kt == 15)
                        p.on("tensor", lambda e, vh_=vh, kt_=kt,
                             inc=(pe if last else None): _inc(e.matmul(
                                out=Po[:, 0:512],
                                lhsT=V[:, (vh_ * 16 + kt_) * 128:
                                       (vh_ * 16 + kt_ + 1) * 128],
                                rhs=diffT[:, kt_ * 512:(kt_ + 1) * 512],
                                start=(kt_ == 0), stop=(kt_ == 15),
                                skip_group_check=True), inc))
                    pv_mark[("pv",)] = pe.inc()
                    p.wait("scalar", pe, pv_mark[("pv",)])
                    p.on("scalar", lambda e: e.activation(
                        out=sqf[:], in_=Po[:, 0:512],
                        func=AF.Square).then_inc(act.h, 1))
                    sqv = act.inc()
                    p.wait("tensor", act, sqv)
                    p.on("tensor", lambda e: e.matmul(
                        out=Pr[0:1, 0:512], lhsT=cstt["ones128_f"][:, 0:1],
                        rhs=sqf[:], start=True, stop=True,
                        skip_group_check=True).then_inc(pe.h, 1))
                    msv = pe.inc()
                    p.wait("vector", pe, msv)
                    p.on("vector", lambda e: e.tensor_scalar(
                        out=rs512[0:1, 0:512], in0=Pr[0:1, 0:512],
                        scalar1=1.0 / 128.0, scalar2=1e-5, op0=ALU.mult,
                        op1=ALU.add).then_inc(dve.h, 1))
                    rv = dve.inc()
                    p.wait("scalar", dve, rv)
                    p.on("scalar", lambda e: e.activation(
                        out=rs512[1:2, 0:512] if False else sqf[0:1, 0:512],
                        in_=rs512[0:1, 0:512], func=AF.Sqrt).then_inc(act.h, 1))
                    sqv2 = act.inc()
                    p.wait("vector", act, sqv2)
                    p.on("vector", lambda e: e.reciprocal(
                        out=rs512[0:1, 0:512],
                        in_=sqf[0:1, 0:512]).then_inc(dve.h, 1))
                    rv2 = dve.inc()
                    p.wait("tensor", dve, rv2)
                    p.on("tensor", lambda e: e.matmul(
                        out=Pr[:, 0:512], lhsT=cstt["ones1_f"][0:1, :],
                        rhs=rs512[0:1, 0:512], start=True, stop=True,
                        skip_group_check=True).then_inc(pe.h, 1))
                    bcv = pe.inc()
                    p.wait("vector", pe, bcv)
                    p.on("vector", lambda e: e.tensor_copy(
                        out=rs512[:], in_=Pr[:, 0:512]).then_inc(dve.h, 1))
                    p.wait("vector", dve, dve.inc())
                    p.on("vector", lambda e, vh_=vh, qg_=qg: e.tensor_mul(
                        out=attn_sb[:, vh_ * S_FULL + qg_ * 512:
                                    vh_ * S_FULL + (qg_ + 1) * 512],
                        in0=Po[:, 0:512], in1=rs512[:]).then_inc(dve.h, 1))
                    ao_mark[("ao",)] = dve.inc()
                sw = vh % 2
                p.wait("gpsimd", dve, ao_mark[("ao",)])
                p.on("gpsimd", lambda e, vh_=vh, sw_=sw: e.dma_start(
                    out=aout_r[vh_],
                    in_=attn_sb[:, vh_ * S_FULL:(vh_ + 1) * S_FULL]).then_inc(
                    stp[sw_].h, 16))
                stp[sw].inc(16)
            if debug:
                p.wait("gpsimd", dve, dve.n)
                p.wait("gpsimd", pe, pe.n)
                p.wait("gpsimd", act, act.n)
                for (dst, srcb) in [(dbg_q, Q), (dbg_k, K), (dbg_v, V),
                                    (dbg_p0, phat0), (dbg_d2, dbuf2),
                                    (dbg_mrow, mrow)]:
                    p.on("gpsimd", lambda e, d_=dst, s_=srcb: e.dma_start(
                        out=d_, in_=s_[:]).then_inc(stp[0].h, 16))
                    stp[0].inc(16)
        p.wait("gpsimd", stp[0], stp[0].n)
        p.wait("gpsimd", stp[1], stp[1].n)
        p.finalize()
    return nc



def build_outproj():
    nc = bass.Bass("TRN2", target_bir_lowering=False)
    xin = nc.dram_tensor("xin", [D, T], F32, kind="ExternalInput").ap()
    ain = nc.dram_tensor("ain", [D, T], BF16, kind="ExternalInput").ap()
    wo = nc.dram_tensor("wo", [NCH, 128, NCH, 128], F32, kind="ExternalInput").ap()
    subln = nc.dram_tensor("subln", [128, 1], F32, kind="ExternalInput").ap()
    cst_d = {k: nc.dram_tensor("c_" + k, list(v.shape), CONST_DTYPES[k],
                               kind="ExternalInput").ap()
             for k, v in make_consts().items()}
    yout = nc.dram_tensor("yout", [D, T], F32, kind="ExternalOutput").ap()
    xin_r = xin.rearrange("(c p) t -> c p t", p=128)
    ain_r = ain.rearrange("(c p) t -> c p t", p=128)
    yout_r = yout.rearrange("(c p) t -> c p t", p=128)

    sbuf = nc.sbuf_tensor
    psum = nc.psum_tensor
    p = Prog(nc)
    with ExitStack() as ctx:
        E = ctx.enter_context
        a_all = E(nc.sbuf_tensor("a_all", [128, NCH * T], BF16))
        a_t = [a_all[:, i * T:(i + 1) * T] for i in range(NCH)]
        xa = [E(nc.sbuf_tensor(f"xa{i}", [128, T], F32)) for i in range(2)]
        wstg = E(nc.sbuf_tensor("wstg", [128, T], F32))
        wbf = E(nc.sbuf_tensor("wbf", [128, 2048], BF16))
        sub_t = E(nc.sbuf_tensor("sub_t", [128, 1], F32))
        ybuf = [E(nc.sbuf_tensor(f"ybuf{i}", [128, T], F32)) for i in range(2)]
        PA = E(nc.psum_tensor("PA", [128, 2 * T], F32))
        la = Sem(E(nc.semaphore("la")))
        lw = [Sem(E(nc.semaphore(f"lw{i}"))) for i in range(2)]
        gx = [Sem(E(nc.semaphore(f"gx{i}"))) for i in range(2)]
        stp = [Sem(E(nc.semaphore(f"st{i}"))) for i in range(2)]
        pe = Sem(E(nc.semaphore("pe")))
        act = Sem(E(nc.semaphore("act")))
        dve = Sem(E(nc.semaphore("dve")))

        p.on("sync", lambda e: e.dma_start(out=sub_t[:], in_=subln).then_inc(la.h, 16))
        la.inc(16)
        for ic in range(NCH):
            p.on("sync", lambda e, ic_=ic: e.dma_start(
                out=a_t[ic_], in_=ain_r[ic_]).then_inc(la.h, 16))
            la.inc(16)
        ain_ok = la.n
        p.wait("tensor", la, ain_ok)
        p.wait("scalar", la, ain_ok)

        wconv_mark = {}
        mm_mark = {}
        ev_mark = {}
        y_mark = {}
        for oc in range(NCH):
            par = oc % 2
            pd = PA[:, par * T:(par + 1) * T]
            if oc < 2:
                pass
            else:
                p.wait("gpsimd", dve, y_mark[oc - 2])
            p.on("gpsimd", lambda e, oc_=oc, par_=par: e.dma_start(
                out=xa[par_][:], in_=xin_r[oc_]).then_inc(gx[par_].h, 16))
            xv = gx[par].inc(16)
            for hf in range(2):
                sb = (oc * 2 + hf) % 2
                prev = oc * 2 + hf - 1
                if prev >= 0:
                    p.wait("sync", act, wconv_mark[prev])
                p.on("sync", lambda e, oc_=oc, hf_=hf, sb_=sb: e.dma_start(
                    out=wstg[:, 0:T],
                    in_=wo[oc_, :, hf_ * 8:(hf_ + 1) * 8, :].rearrange(
                        "p a b -> p (a b)")).then_inc(lw[sb_].h, 16))
                lv = lw[sb].inc(16)
                p.wait("scalar", lw[sb], lv)
                if oc > 0 and hf == 0:
                    p.wait("scalar", pe, mm_mark[oc - 1])
                p.on("scalar", lambda e, hf_=hf: e.activation(
                    out=wbf[:, hf_ * 1024:(hf_ + 1) * 1024], in_=wstg[:, 0:T],
                    func=AF.Copy, scale=sub_t[:, 0:1]).then_inc(act.h, 1))
                wconv_mark[oc * 2 + hf] = act.inc()
            wready = act.n
            p.wait("tensor", act, wready)
            if oc >= 2:
                p.wait("tensor", dve, y_mark[oc - 2])
            for ti in range(2):
                sl = slice(ti * 512, (ti + 1) * 512)
                for ic in range(NCH):
                    last = (ti == 1 and ic == NCH - 1)
                    p.on("tensor", lambda e, ic_=ic, sl_=sl, pd_=pd,
                         inc=(pe if last else None): _inc(e.matmul(
                            out=pd_[:, sl_], lhsT=wbf[:, ic_ * 128:(ic_ + 1) * 128],
                            rhs=a_t[ic_][:, sl_], start=(ic_ == 0),
                            stop=(ic_ == NCH - 1), skip_group_check=True), inc))
            mm_mark[oc] = pe.inc()
            p.wait("vector", pe, mm_mark[oc])
            p.wait("vector", gx[par], xv)
            if oc >= 2:
                p.wait("vector", stp[par], stp[par].n)
            p.on("vector", lambda e, par_=par, pd_=pd: e.tensor_add(
                out=ybuf[par_][:], in0=xa[par_][:], in1=pd_[:, 0:T]).then_inc(dve.h, 1))
            y_mark[oc] = dve.inc()
            p.wait("gpsimd", dve, y_mark[oc])
            p.on("gpsimd", lambda e, oc_=oc, par_=par: e.dma_start(
                out=yout_r[oc_], in_=ybuf[par_][:]).then_inc(stp[par_].h, 16))
            stp[par].inc(16)
        p.wait("gpsimd", stp[0], stp[0].n)
        p.wait("gpsimd", stp[1], stp[1].n)
        p.finalize()
    return nc



def build_l1attn():
    nc = bass.Bass("TRN2", target_bir_lowering=False)
    xin = nc.dram_tensor("xin", [D, 2 * T], F32, kind="ExternalInput").ap()
    wq_s = nc.dram_tensor("wq_s", [NCH, 128, NCH, 128], F32, kind="ExternalInput").ap()
    wk_s = nc.dram_tensor("wk_s", [NCH, 128, NCH, 128], F32, kind="ExternalInput").ap()
    wv_s = nc.dram_tensor("wv_s", [NCH, 128, NCH, 128], F32, kind="ExternalInput").ap()
    wo_s = nc.dram_tensor("wo_s", [NCH, 128, NCH, 128], F32, kind="ExternalInput").ap()
    bq_d = nc.dram_tensor("bq", [128, NCH], F32, kind="ExternalInput").ap()
    bk_d = nc.dram_tensor("bk", [128, NCH], F32, kind="ExternalInput").ap()
    bv_d = nc.dram_tensor("bv", [128, NCH], F32, kind="ExternalInput").ap()
    bo_d = nc.dram_tensor("bo", [128, NCH], F32, kind="ExternalInput").ap()
    wmq_d = nc.dram_tensor("wmq", [128, 64], F32, kind="ExternalInput").ap()
    wmk_d = nc.dram_tensor("wmk", [128, 64], F32, kind="ExternalInput").ap()
    cst_d = {k: nc.dram_tensor("c_" + k, list(v.shape), CONST_DTYPES[k],
                               kind="ExternalInput").ap()
             for k, v in make_consts().items()}
    yout = nc.dram_tensor("yout", [D, T], F32, kind="ExternalOutput").ap()
    xin_r = xin.rearrange("(c p) t -> c p t", p=128)
    yout_r = yout.rearrange("(c p) t -> c p t", p=128)

    sbuf = nc.sbuf_tensor
    psum = nc.psum_tensor
    p = Prog(nc)
    with ExitStack() as ctx:
        E = ctx.enter_context
        xln_all = E(nc.sbuf_tensor("xlnf", [128, NCH * 2 * T], BF16))
        xlnF = [xln_all[:, i * 2 * T:(i + 1) * 2 * T] for i in range(NCH)]
        attn_all = E(nc.sbuf_tensor("attnf", [128, NCH * T], BF16))
        attn_fm = [attn_all[:, i * T:(i + 1) * T] for i in range(NCH)]
        xa = [E(nc.sbuf_tensor(f"xa{i}", [128, T], F32)) for i in range(2)]
        xsq = E(nc.sbuf_tensor("xsq", [128, T], F32))
        rows = E(nc.sbuf_tensor("rows", [1, 3 * T], F32))
        mu_sb = E(nc.sbuf_tensor("mu_sb", [128, T], F32))
        rs_sb = E(nc.sbuf_tensor("rs_sb", [128, T], F32))
        wstg = E(nc.sbuf_tensor("wstg", [128, T], F32))
        wbf = E(nc.sbuf_tensor("wbf", [128, 2048], BF16))
        bq_t = E(nc.sbuf_tensor("bq_t", [128, NCH], F32))
        bk_t = E(nc.sbuf_tensor("bk_t", [128, NCH], F32))
        bv_t = E(nc.sbuf_tensor("bv_t", [128, NCH], F32))
        bo_t = E(nc.sbuf_tensor("bo_t", [128, NCH], F32))
        wmq_f = E(nc.sbuf_tensor("wmq_f", [128, 64], F32))
        wmk_f = E(nc.sbuf_tensor("wmk_f", [128, 64], F32))
        wmq_b = E(nc.sbuf_tensor("wmq_b", [128, 64], BF16))
        wmk_b = E(nc.sbuf_tensor("wmk_b", [128, 64], BF16))
        kf_bf = E(nc.sbuf_tensor("kf_bf", [128, 2048], BF16))
        vf_bf = E(nc.sbuf_tensor("vf_bf", [128, 2048], BF16))
        v_tm = E(nc.sbuf_tensor("v_tm", [128, 2048], BF16))
        qf_bf = E(nc.sbuf_tensor("qf_bf", [128, T], BF16))
        kmt = E(nc.sbuf_tensor("kmt", [128, 16 * 64], BF16))
        qm_bf = E(nc.sbuf_tensor("qm_bf", [64, T], BF16))
        qm_sc = E(nc.sbuf_tensor("qm_sc", [64, T], BF16))
        e1 = E(nc.sbuf_tensor("e1", [128, T], F32))
        e2 = E(nc.sbuf_tensor("e2", [128, T], F32))
        e3 = E(nc.sbuf_tensor("e3", [128, T], F32))
        kv_bf = E(nc.sbuf_tensor("kv_bf", [64, 128], BF16))
        krow_sb = E(nc.sbuf_tensor("krow_sb", [1, 64], BF16))
        kcol_sb = E(nc.sbuf_tensor("kcol_sb", [64, 1], BF16))
        nrm_sb = E(nc.sbuf_tensor("nrm_sb", [1, T], F32))
        ones11b = E(nc.sbuf_tensor("ones11b", [1, 1], BF16))
        ytmp = E(nc.sbuf_tensor("ytmp", [128, T], F32))
        ybuf = [E(nc.sbuf_tensor(f"ybuf{i}", [128, T], F32)) for i in range(2)]
        c_ones128_f = E(nc.sbuf_tensor("c1f", [128, 1], F32))
        c_ones128_b = E(nc.sbuf_tensor("c1b", [128, 1], BF16))
        c_ones1_f = E(nc.sbuf_tensor("c1rf", [1, 128], F32))
        c_ident_b = E(nc.sbuf_tensor("cidb", [128, 128], BF16))
        S4 = E(nc.psum_tensor("S4", [128, 2048], F32))
        Tp = E(nc.psum_tensor("Tp", [128, 2048], BF16))
        Pr = E(nc.psum_tensor("Pr", [128, 1024], F32))

        lx = [Sem(E(nc.semaphore(f"lx{i}"))) for i in range(2)]
        lw = [Sem(E(nc.semaphore(f"lw{i}"))) for i in range(2)]
        lwg = Sem(E(nc.semaphore("lwg")))
        gx = [Sem(E(nc.semaphore(f"gx{i}"))) for i in range(2)]
        stp = [Sem(E(nc.semaphore(f"st{i}"))) for i in range(2)]
        ldc = Sem(E(nc.semaphore("ldc")))
        pe = Sem(E(nc.semaphore("pe")))
        act = Sem(E(nc.semaphore("act")))
        dve = Sem(E(nc.semaphore("dve")))

        cstt = {"ones128_f": c_ones128_f, "ones128_b": c_ones128_b,
                "ones1_f": c_ones1_f, "ident_b": c_ident_b}

        for dst, srcp in [(c_ones128_f, cst_d["ones128_f"]),
                          (c_ones128_b, cst_d["ones128_b"]),
                          (c_ones1_f, cst_d["ones1_f"]),
                          (c_ident_b, cst_d["ident_b"]),
                          (bq_t, bq_d), (bk_t, bk_d), (bv_t, bv_d), (bo_t, bo_d),
                          (wmq_f, wmq_d), (wmk_f, wmk_d)]:
            p.on("gpsimd", lambda e, d=dst, sr=srcp: e.dma_start(
                out=d[:], in_=sr).then_inc(ldc.h, 16))
            ldc.inc(16)
        consts_ok = ldc.n
        for eng in ("tensor", "scalar", "vector"):
            p.wait(eng, ldc, consts_ok)
        p.on("vector", lambda e: e.memset(ones11b[:], 1.0))
        p.on("vector", lambda e: e.tensor_copy(out=wmq_b[:], in_=wmq_f[:]))
        p.on("vector", lambda e: e.tensor_copy(
            out=wmk_b[:], in_=wmk_f[:]).then_inc(dve.h, 1))
        wm_ok = dve.inc()

        s = dict(lx=lx, pe=pe, act=act, dve=dve, cst=cstt, xa=xa, xsq=xsq,
                 Pbig=S4, mu_sb=mu_sb, rs_sb=rs_sb, rows=rows)

        for th in range(2):
            xv = [xlnF[dc][:, th * T:(th + 1) * T] for dc in range(NCH)]
            emit_ln(p, s, xin_r, xv, t=T, col0=th * T, want_u=False)
        xln_ok = s["ln_xln_dve"]

        wconv_mark = {}
        gqw = 0

        def stream_slab(wsrc, oc, pe_free_mark):
            nonlocal gqw
            for hf in range(2):
                sb = gqw % 2
                if gqw >= 1:
                    p.wait("sync", act, wconv_mark[gqw - 1])
                p.on("sync", lambda e, oc_=oc, hf_=hf, sb_=sb, w_=wsrc: e.dma_start(
                    out=wstg[:, 0:T],
                    in_=w_[oc_, :, hf_ * 8:(hf_ + 1) * 8, :].rearrange(
                        "p a b -> p (a b)")).then_inc(lw[sb_].h, 16))
                lv = lw[sb].inc(16)
                p.wait("scalar", lw[sb], lv)
                if hf == 0 and pe_free_mark is not None:
                    p.wait("scalar", pe, pe_free_mark)
                p.on("scalar", lambda e, hf_=hf: e.activation(
                    out=wbf[:, hf_ * 1024:(hf_ + 1) * 1024], in_=wstg[:, 0:T],
                    func=AF.Copy).then_inc(act.h, 1))
                wconv_mark[gqw] = act.inc()
                gqw += 1
            return act.n

        def elu1(src_psum, dst, fd, pbase=0):
            p.on("vector", lambda e: e.tensor_scalar_min(
                out=e1[pbase:pbase + src_psum.shape[0], 0:fd], in0=src_psum,
                scalar1=0.0))
            p.on("vector", lambda e: e.tensor_scalar_max(
                out=e3[pbase:pbase + src_psum.shape[0], 0:fd], in0=src_psum,
                scalar1=0.0).then_inc(dve.h, 1))
            dv = dve.inc()
            p.wait("scalar", dve, dv)
            p.on("scalar", lambda e: e.activation(
                out=e2[pbase:pbase + src_psum.shape[0], 0:fd],
                in_=e1[pbase:pbase + src_psum.shape[0], 0:fd],
                func=AF.Exp).then_inc(act.h, 1))
            av = act.inc()
            p.wait("vector", act, av)
            p.on("vector", lambda e, d_=dst: e.tensor_add(
                out=d_, in0=e2[pbase:pbase + src_psum.shape[0], 0:fd],
                in1=e3[pbase:pbase + src_psum.shape[0], 0:fd]).then_inc(dve.h, 1))
            return dve.inc()

        last_proj = None
        attn_mark = {}
        for h in range(NCH):
            wready = stream_slab(wk_s, h, last_proj)
            p.wait("tensor", act, wready)
            if h == 0:
                p.wait("tensor", dve, xln_ok)
                p.wait("tensor", dve, wm_ok)
            if last_proj is not None:
                p.wait("tensor", dve, attn_mark[h - 1])
            for ti in range(4):
                sl = slice(ti * 512, (ti + 1) * 512)
                for dc in range(NCH):
                    last = (ti == 3 and dc == NCH - 1)
                    p.on("tensor", lambda e, dc_=dc, sl_=sl,
                         inc=(pe if last else None): _inc(e.matmul(
                            out=S4[:, sl_], lhsT=wbf[:, dc_ * 128:(dc_ + 1) * 128],
                            rhs=xlnF[dc_][:, sl_], start=(dc_ == 0),
                            stop=(dc_ == NCH - 1), skip_group_check=True), inc))
            kp = pe.inc()
            p.wait("vector", pe, kp)
            p.on("vector", lambda e, h_=h: e.tensor_scalar_add(
                out=kf_bf[:], in0=S4[:, 0:2048],
                scalar1=bk_t[:, h_:h_ + 1]).then_inc(dve.h, 1))
            kev = dve.inc()
            wready = stream_slab(wv_s, h, kp)
            p.wait("tensor", act, wready)
            p.wait("tensor", dve, kev)
            for ti in range(4):
                sl = slice(ti * 512, (ti + 1) * 512)
                for dc in range(NCH):
                    last = (ti == 3 and dc == NCH - 1)
                    p.on("tensor", lambda e, dc_=dc, sl_=sl,
                         inc=(pe if last else None): _inc(e.matmul(
                            out=S4[:, sl_], lhsT=wbf[:, dc_ * 128:(dc_ + 1) * 128],
                            rhs=xlnF[dc_][:, sl_], start=(dc_ == 0),
                            stop=(dc_ == NCH - 1), skip_group_check=True), inc))
            vp = pe.inc()
            p.wait("vector", pe, vp)
            p.on("vector", lambda e, h_=h: e.tensor_scalar_add(
                out=vf_bf[:], in0=S4[:, 0:2048],
                scalar1=bv_t[:, h_:h_ + 1]).then_inc(dve.h, 1))
            vev = dve.inc()
            p.wait("tensor", dve, vev)
            for tc in range(16):
                last = (tc == 15)
                p.on("tensor", lambda e, tc_=tc, inc=(pe if last else None):
                     _inc(e.transpose(out=Tp[:, tc_ * 128:(tc_ + 1) * 128],
                                      in_=vf_bf[:, tc_ * 128:(tc_ + 1) * 128],
                                      identity=cstt["ident_b"][:, :]), inc))
            vt = pe.inc()
            p.wait("vector", pe, vt)
            p.on("vector", lambda e: e.tensor_copy(
                out=v_tm[:], in_=Tp[:, 0:2048]).then_inc(dve.h, 1))
            vtm_ok = dve.inc()
            wready = stream_slab(wq_s, h, vp)
            p.wait("tensor", act, wready)
            for ti in range(2):
                sl = slice(ti * 512, (ti + 1) * 512)
                for dc in range(NCH):
                    last = (ti == 1 and dc == NCH - 1)
                    p.on("tensor", lambda e, dc_=dc, sl_=sl,
                         inc=(pe if last else None): _inc(e.matmul(
                            out=S4[:, sl_], lhsT=wbf[:, dc_ * 128:(dc_ + 1) * 128],
                            rhs=xlnF[dc_][:, sl_], start=(dc_ == 0),
                            stop=(dc_ == NCH - 1), skip_group_check=True), inc))
            qp = pe.inc()
            p.wait("vector", pe, qp)
            p.on("vector", lambda e, h_=h: e.tensor_scalar_add(
                out=qf_bf[:], in0=S4[:, 0:1024],
                scalar1=bq_t[:, h_:h_ + 1]).then_inc(dve.h, 1))
            qev = dve.inc()
            p.wait("tensor", dve, qev)
            for ti in range(2):
                last = (ti == 1)
                p.on("tensor", lambda e, ti_=ti, inc=(pe if last else None):
                     _inc(e.matmul(
                        out=S4[0:64, 1024 + ti_ * 512:1024 + (ti_ + 1) * 512],
                        lhsT=wmq_b[:, 0:64],
                        rhs=qf_bf[:, ti_ * 512:(ti_ + 1) * 512],
                        start=True, stop=True, skip_group_check=True), inc))
            qm = pe.inc()
            p.wait("vector", pe, qm)
            elu1(S4[0:64, 1024:2048], qm_bf[0:64, 0:T], T)
            km_mark = {}
            for tc in range(16):
                slot = (tc % 2) * 512
                if tc >= 2:
                    p.wait("tensor", dve, km_mark[tc - 2])
                last = False
                p.on("tensor", lambda e, tc_=tc, slot_=slot: e.matmul(
                    out=S4[:, slot_:slot_ + 64], lhsT=kf_bf[:, tc_ * 128:(tc_ + 1) * 128],
                    rhs=wmk_b[:, 0:64], start=True, stop=True,
                    skip_group_check=True).then_inc(pe.h, 1))
                mp = pe.inc()
                p.wait("vector", pe, mp)
                km_mark[tc] = elu1(S4[:, slot:slot + 64],
                                   kmt[:, tc * 64:(tc + 1) * 64], 64)
                p.wait("tensor", dve, km_mark[tc])
                p.on("tensor", lambda e, tc_=tc: e.matmul(
                    out=S4[0:64, 1536:1664], lhsT=kmt[:, tc_ * 64:(tc_ + 1) * 64],
                    rhs=v_tm[:, tc_ * 128:(tc_ + 1) * 128], start=(tc_ == 0),
                    stop=(tc_ == 15), skip_group_check=True))
                p.on("tensor", lambda e, tc_=tc, inc=(pe if tc == 15 else None):
                     _inc(e.matmul(
                        out=S4[0:1, 1792:1856], lhsT=cstt["ones128_b"][:, 0:1],
                        rhs=kmt[:, tc_ * 64:(tc_ + 1) * 64], start=(tc_ == 0),
                        stop=(tc_ == 15), skip_group_check=True), inc))
            kvp = pe.inc()
            p.wait("vector", pe, kvp)
            p.on("vector", lambda e: e.tensor_copy(
                out=krow_sb[:], in_=S4[0:1, 1792:1856]))
            p.on("vector", lambda e: e.tensor_copy(
                out=kv_bf[:], in_=S4[0:64, 1536:1664]).then_inc(dve.h, 1))
            p.wait("vector", dve, dve.inc())
            dv = dve.n
            p.wait("tensor", dve, dv)
            p.on("tensor", lambda e: e.matmul(
                out=Pr[0:64, 0:1], lhsT=krow_sb[0:1, 0:64], rhs=ones11b[0:1, 0:1],
                start=True, stop=True, skip_group_check=True).then_inc(pe.h, 1))
            kc = pe.inc()
            p.wait("vector", pe, kc)
            p.on("vector", lambda e: e.tensor_copy(
                out=kcol_sb[:], in_=Pr[0:64, 0:1]).then_inc(dve.h, 1))
            p.wait("vector", dve, dve.inc())
            kc2 = dve.n
            p.wait("tensor", dve, kc2)
            for ti in range(2):
                last = (ti == 1)
                p.on("tensor", lambda e, ti_=ti, inc=(pe if last else None):
                     _inc(e.matmul(
                        out=S4[0:1, ti_ * 512:(ti_ + 1) * 512],
                        lhsT=kcol_sb[0:64, 0:1],
                        rhs=qm_bf[0:64, ti_ * 512:(ti_ + 1) * 512],
                        start=True, stop=True, skip_group_check=True), inc))
            nm = pe.inc()
            p.wait("vector", pe, nm)
            p.on("vector", lambda e: e.tensor_scalar_add(
                out=nrm_sb[0:1, 0:T], in0=S4[0:1, 0:1024],
                scalar1=1e-6).then_inc(dve.h, 1))
            p.wait("vector", dve, dve.inc())
            p.on("vector", lambda e: e.reciprocal(
                out=nrm_sb[0:1, 0:T], in_=nrm_sb[0:1, 0:T]).then_inc(dve.h, 1))
            rn = dve.inc()
            p.wait("tensor", dve, rn)
            for ti in range(2):
                last = (ti == 1)
                p.on("tensor", lambda e, ti_=ti, inc=(pe if last else None):
                     _inc(e.matmul(
                        out=S4[0:64, 1024 + ti_ * 512:1024 + (ti_ + 1) * 512],
                        lhsT=cstt["ones1_f"][0:1, 0:64],
                        rhs=nrm_sb[0:1, ti_ * 512:(ti_ + 1) * 512],
                        start=True, stop=True, skip_group_check=True), inc))
            bc = pe.inc()
            p.wait("vector", pe, bc)
            p.on("vector", lambda e: e.tensor_mul(
                out=qm_sc[0:64, 0:T], in0=qm_bf[0:64, 0:T],
                in1=S4[0:64, 1024:2048]).then_inc(dve.h, 1))
            qs = dve.inc()
            p.wait("tensor", dve, qs)
            for ti in range(2):
                last = (ti == 1)
                p.on("tensor", lambda e, ti_=ti, inc=(pe if last else None):
                     _inc(e.matmul(
                        out=S4[:, ti_ * 512:(ti_ + 1) * 512],
                        lhsT=kv_bf[0:64, :],
                        rhs=qm_sc[0:64, ti_ * 512:(ti_ + 1) * 512],
                        start=True, stop=True, skip_group_check=True), inc))
            om = pe.inc()
            p.wait("vector", pe, om)
            p.on("vector", lambda e, h_=h: e.tensor_copy(
                out=attn_fm[h_], in_=S4[:, 0:1024]).then_inc(dve.h, 1))
            attn_mark[h] = dve.inc()
            last_proj = om
        heads_done = dve.n

        mm_mark = {}
        y_mark = {}
        for oc in range(NCH):
            par = oc % 2
            pd = S4[:, par * T:(par + 1) * T]
            if oc < 2:
                p.wait("gpsimd", dve, heads_done)
            else:
                p.wait("gpsimd", dve, y_mark[oc - 2])
            p.on("gpsimd", lambda e, oc_=oc, par_=par: e.dma_start(
                out=xa[par_][:], in_=xin_r[oc_][:, 0:T]).then_inc(gx[par_].h, 16))
            xv = gx[par].inc(16)
            wready = stream_slab(wo_s, oc, mm_mark.get(oc - 1, last_proj))
            p.wait("tensor", act, wready)
            if oc == 0:
                p.wait("tensor", dve, heads_done)
            if oc >= 2:
                p.wait("tensor", dve, y_mark[oc - 2])
            for ti in range(2):
                sl = slice(ti * 512, (ti + 1) * 512)
                for ic in range(NCH):
                    last = (ti == 1 and ic == NCH - 1)
                    p.on("tensor", lambda e, ic_=ic, sl_=sl, pd_=pd,
                         inc=(pe if last else None): _inc(e.matmul(
                            out=pd_[:, sl_], lhsT=wbf[:, ic_ * 128:(ic_ + 1) * 128],
                            rhs=attn_fm[ic_][:, sl_], start=(ic_ == 0),
                            stop=(ic_ == NCH - 1), skip_group_check=True), inc))
            mm_mark[oc] = pe.inc()
            p.wait("vector", pe, mm_mark[oc])
            p.on("vector", lambda e, oc_=oc, pd_=pd: e.tensor_scalar_add(
                out=ytmp[:], in0=pd_[:, 0:T],
                scalar1=bo_t[:, oc_:oc_ + 1]).then_inc(dve.h, 1))
            p.wait("vector", dve, dve.inc())
            p.wait("vector", gx[par], xv)
            if oc >= 2:
                p.wait("vector", stp[par], stp[par].n)
            p.on("vector", lambda e, par_=par: e.tensor_add(
                out=ybuf[par_][:], in0=ytmp[:], in1=xa[par_][:]).then_inc(dve.h, 1))
            y_mark[oc] = dve.inc()
            p.wait("gpsimd", dve, y_mark[oc])
            p.on("gpsimd", lambda e, oc_=oc, par_=par: e.dma_start(
                out=yout_r[oc_], in_=ybuf[par_][:]).then_inc(stp[par_].h, 16))
            stp[par].inc(16)
        p.wait("gpsimd", stp[0], stp[0].n)
        p.wait("gpsimd", stp[1], stp[1].n)
        p.finalize()
    return nc



NF1 = 11


def build_l1moe():
    nc = bass.Bass("TRN2", target_bir_lowering=False)
    xin = nc.dram_tensor("xin", [D, 2 * T], F32, kind="ExternalInput").ap()
    w1 = nc.dram_tensor("w1", [6, NF1, 128, NCH, 128], F32, kind="ExternalInput").ap()
    w2 = nc.dram_tensor("w2", [6, NCH, 128, NF1, 128], F32, kind="ExternalInput").ap()
    b1_d = nc.dram_tensor("b1t", [128, 6 * NF1], F32, kind="ExternalInput").ap()
    b2_d = nc.dram_tensor("b2t", [16, 6, 128], F32, kind="ExternalInput").ap()
    gw_d = nc.dram_tensor("gwt", [16, 128, 6], F32, kind="ExternalInput").ap()
    iota_d = nc.dram_tensor("iota54", [128, 54], F32, kind="ExternalInput").ap()
    b1r_d = nc.dram_tensor("b1r", [6, 1408], F32, kind="ExternalInput").ap()
    cst_d = {k: nc.dram_tensor("c_" + k, list(v.shape), CONST_DTYPES[k],
                               kind="ExternalInput").ap()
             for k, v in make_consts().items()}
    yout = nc.dram_tensor("yout", [D, T], F32, kind="ExternalOutput").ap()
    xin_r = xin.rearrange("(c p) t -> c p t", p=128)
    yout_r = yout.rearrange("(c p) t -> c p t", p=128)

    sbuf = nc.sbuf_tensor
    psum = nc.psum_tensor
    p = Prog(nc)
    with ExitStack() as ctx:
        E = ctx.enter_context
        xln_all = E(nc.sbuf_tensor("xlno", [128, NCH * T], BF16))
        xln = [xln_all[:, i * T:(i + 1) * T] for i in range(NCH)]
        out_acc = E(nc.sbuf_tensor("oacc", [128, NCH * T], F32))
        hid_all = E(nc.sbuf_tensor("hid", [128, NF1 * T], BF16))
        hid = [hid_all[:, i * T:(i + 1) * T] for i in range(NF1)]
        xa = [E(nc.sbuf_tensor(f"xa{i}", [128, T], F32)) for i in range(2)]
        xsq = E(nc.sbuf_tensor("xsq", [128, T], F32))
        rows = E(nc.sbuf_tensor("rows", [1, 3 * T], F32))
        mu_sb = E(nc.sbuf_tensor("mu_sb", [128, T], F32))
        rs_sb = E(nc.sbuf_tensor("rs_sb", [128, T], F32))
        xsum = E(nc.sbuf_tensor("xsum", [128, NCH], F32))
        xmean = E(nc.sbuf_tensor("xmean", [128, NCH], F32))
        gw_t = E(nc.sbuf_tensor("gw_t", [128, NCH * 6], F32))
        b1_t = E(nc.sbuf_tensor("b1_t", [128, 6 * NF1], F32))
        b2_t = E(nc.sbuf_tensor("b2_t", [6, NCH * 128], F32))
        b2_b = E(nc.sbuf_tensor("b2_b", [6, NCH * 128], BF16))
        b2w_sb = E(nc.sbuf_tensor("b2w_sb", [128, NCH], F32))
        grow = E(nc.sbuf_tensor("grow", [1, 64], F32))
        wrow_b = E(nc.sbuf_tensor("wrow_b", [1, 6], BF16))
        w6col_b = E(nc.sbuf_tensor("w6col_b", [6, 1], BF16))
        wcol_sb = E(nc.sbuf_tensor("wcol_sb", [128, 6], F32))
        ones11b = E(nc.sbuf_tensor("ones11b", [1, 1], BF16))
        wstg = E(nc.sbuf_tensor("wstg", [128, T], F32))
        wbf = E(nc.sbuf_tensor("wbf", [128, 2048], BF16))
        wstg2 = E(nc.sbuf_tensor("wstg2", [128, 1408], F32))
        wbf2 = E(nc.sbuf_tensor("wbf2", [128, 1408], BF16))
        sgbuf = E(nc.sbuf_tensor("sgbuf", [128, T], F32))
        ybuf = [E(nc.sbuf_tensor(f"ybuf{i}", [128, T], F32)) for i in range(2)]
        c_ones128_f = E(nc.sbuf_tensor("c1f", [128, 1], F32))
        c_ones128_b = E(nc.sbuf_tensor("c1b", [128, 1], BF16))
        c_ones1_f = E(nc.sbuf_tensor("c1rf", [1, 128], F32))
        c_ident_b = E(nc.sbuf_tensor("cidb", [128, 128], BF16))
        S4 = E(nc.psum_tensor("S4", [128, 2048], F32))
        Pr = E(nc.psum_tensor("Pr", [128, 1024], F32))

        lx = [Sem(E(nc.semaphore(f"lx{i}"))) for i in range(2)]
        lw = [Sem(E(nc.semaphore(f"lw{i}"))) for i in range(2)]
        lwg = Sem(E(nc.semaphore("lwg")))
        gx = [Sem(E(nc.semaphore(f"gx{i}"))) for i in range(2)]
        stp = [Sem(E(nc.semaphore(f"st{i}"))) for i in range(2)]
        ldc = Sem(E(nc.semaphore("ldc")))
        pe = Sem(E(nc.semaphore("pe")))
        act = Sem(E(nc.semaphore("act")))
        dve = Sem(E(nc.semaphore("dve")))

        cstt = {"ones128_f": c_ones128_f, "ones128_b": c_ones128_b,
                "ones1_f": c_ones1_f, "ident_b": c_ident_b}

        for dst, srcp in [(c_ones128_f, cst_d["ones128_f"]),
                          (c_ones128_b, cst_d["ones128_b"]),
                          (c_ones1_f, cst_d["ones1_f"]),
                          (c_ident_b, cst_d["ident_b"]),
                          (gw_t.rearrange("p (a b) -> p a b", b=6),
                           gw_d.transpose([1, 0, 2])),
                          (b1_t, b1_d),
                          (b2_t.rearrange("p (a q) -> p a q", q=128),
                           b2_d.transpose([1, 0, 2]))]:
            p.on("gpsimd", lambda e, d=dst, sr=srcp: e.dma_start(
                out=d[:], in_=sr).then_inc(ldc.h, 16))
            ldc.inc(16)
        consts_ok = ldc.n
        for eng in ("tensor", "scalar", "vector"):
            p.wait(eng, ldc, consts_ok)
        p.on("vector", lambda e: e.memset(ones11b[:], 1.0))
        p.on("vector", lambda e: e.tensor_copy(
            out=b2_b[:], in_=b2_t[:]).then_inc(dve.h, 1))
        dve.inc()

        s = dict(lx=lx, pe=pe, act=act, dve=dve, cst=cstt, xa=xa, xsq=xsq,
                 Pbig=S4, mu_sb=mu_sb, rs_sb=rs_sb, rows=rows)

        for pass_i, th in enumerate([1, 0]):
            emit_ln(p, s, xin_r, xln, t=T, col0=th * T, want_u=False)
            for dc in range(NCH):
                last = (dc == NCH - 1)
                if pass_i == 0:
                    p.on("vector", lambda e, dc_=dc, inc=(dve if last else None):
                         _inc(e.reduce_sum(out=xsum[:, dc_:dc_ + 1],
                                           in_=xln[dc_][:, 0:T], axis=AX.X), inc))
                else:
                    p.on("vector", lambda e, dc_=dc, inc=(dve if last else None):
                         _inc(e.tensor_reduce(out=xmean[:, dc_:dc_ + 1],
                                              in_=xln[dc_][:, 0:T], axis=AX.X,
                                              op=ALU.add), inc))
            p.wait("vector", dve, dve.inc())
        xln_ok = s["ln_xln_dve"]
        p.on("vector", lambda e: e.tensor_add(
            out=xmean[:], in0=xmean[:], in1=xsum[:]).then_inc(dve.h, 1))
        p.wait("vector", dve, dve.inc())
        p.on("vector", lambda e: e.tensor_scalar_mul(
            out=xmean[:], in0=xmean[:], scalar1=1.0 / 2048.0).then_inc(dve.h, 1))
        gm = dve.inc()

        p.wait("tensor", dve, gm)
        for dc in range(NCH):
            last = (dc == NCH - 1)
            p.on("tensor", lambda e, dc_=dc, inc=(pe if last else None):
                 _inc(e.matmul(out=S4[0:1, 0:6], lhsT=xmean[:, dc_:dc_ + 1],
                               rhs=gw_t[:, dc_ * 6:(dc_ + 1) * 6],
                               start=(dc_ == 0), stop=(dc_ == NCH - 1),
                               skip_group_check=True), inc))
        lg = pe.inc()

        def vedge(fn):
            p.on("vector", fn)
            pv = None

        p.wait("vector", pe, lg)
        p.on("vector", lambda e: e.tensor_copy(
            out=grow[0:1, 0:6], in_=S4[0:1, 0:6]).then_inc(dve.h, 1))
        p.wait("vector", dve, dve.inc())
        p.on("vector", lambda e: e.reduce_max(
            out=grow[0:1, 8:9], in_=grow[0:1, 0:6], axis=AX.X,
            negate=True).then_inc(dve.h, 1))
        mxv = dve.inc()
        p.wait("scalar", dve, mxv)
        p.on("scalar", lambda e: e.activation(
            out=grow[0:1, 16:22], in_=grow[0:1, 0:6], func=AF.Exp,
            bias=grow[0:1, 8:9]).then_inc(act.h, 1))
        exv = act.inc()
        p.wait("vector", act, exv)
        p.on("vector", lambda e: e.reduce_max(
            out=grow[0:1, 9:10], in_=grow[0:1, 16:22], axis=AX.X).then_inc(dve.h, 1))
        p.wait("vector", dve, dve.inc())
        p.on("vector", lambda e: e.tensor_single_scalar(
            out=grow[0:1, 24:30], in_=grow[0:1, 16:22], scalar=grow[0:1, 9:10],
            op=ALU.is_ge).then_inc(dve.h, 1))
        p.wait("vector", dve, dve.inc())
        p.on("vector", lambda e: e.scalar_tensor_tensor(
            out=grow[0:1, 32:38], in0=grow[0:1, 24:30], scalar=-1e4,
            in1=grow[0:1, 16:22], op0=ALU.mult, op1=ALU.add).then_inc(dve.h, 1))
        p.wait("vector", dve, dve.inc())
        p.on("vector", lambda e: e.reduce_max(
            out=grow[0:1, 10:11], in_=grow[0:1, 32:38], axis=AX.X).then_inc(dve.h, 1))
        p.wait("vector", dve, dve.inc())
        p.on("vector", lambda e: e.tensor_single_scalar(
            out=grow[0:1, 40:46], in_=grow[0:1, 16:22], scalar=grow[0:1, 10:11],
            op=ALU.is_ge).then_inc(dve.h, 1))
        p.wait("vector", dve, dve.inc())
        p.on("vector", lambda e: e.tensor_add(
            out=grow[0:1, 11:12], in0=grow[0:1, 9:10],
            in1=grow[0:1, 10:11]).then_inc(dve.h, 1))
        p.wait("vector", dve, dve.inc())
        p.on("vector", lambda e: e.reciprocal(
            out=grow[0:1, 12:13], in_=grow[0:1, 11:12]).then_inc(dve.h, 1))
        p.wait("vector", dve, dve.inc())
        p.on("vector", lambda e: e.tensor_mul(
            out=grow[0:1, 48:54], in0=grow[0:1, 16:22],
            in1=grow[0:1, 40:46]).then_inc(dve.h, 1))
        p.wait("vector", dve, dve.inc())
        p.on("vector", lambda e: e.tensor_single_scalar(
            out=grow[0:1, 56:62], in_=grow[0:1, 48:54], scalar=grow[0:1, 12:13],
            op=ALU.mult).then_inc(dve.h, 1))
        p.wait("vector", dve, dve.inc())
        p.on("vector", lambda e: e.tensor_copy(
            out=wrow_b[0:1, 0:6], in_=grow[0:1, 56:62]).then_inc(dve.h, 1))
        wrow_ok = dve.inc()
        p.wait("tensor", dve, wrow_ok)
        p.on("tensor", lambda e: e.matmul(
            out=Pr[:, 0:6], lhsT=cstt["ones1_f"][0:1, :], rhs=grow[0:1, 56:62],
            start=True, stop=True, skip_group_check=True))
        p.on("tensor", lambda e: e.matmul(
            out=Pr[0:6, 8:9], lhsT=wrow_b[0:1, 0:6], rhs=ones11b[0:1, 0:1],
            start=True, stop=True, skip_group_check=True).then_inc(pe.h, 1))
        bc1 = pe.inc()
        p.wait("vector", pe, bc1)
        p.on("vector", lambda e: e.tensor_copy(out=wcol_sb[:], in_=Pr[:, 0:6]))
        p.on("vector", lambda e: e.tensor_copy(
            out=w6col_b[:], in_=Pr[0:6, 8:9]).then_inc(dve.h, 1))
        p.wait("vector", dve, dve.inc())
        w6_ok = dve.n
        p.wait("tensor", dve, w6_ok)
        for dc in range(NCH):
            last = (dc == NCH - 1)
            p.on("tensor", lambda e, dc_=dc, inc=(pe if last else None):
                 _inc(e.matmul(out=Pr[:, 16 + dc_:17 + dc_],
                               lhsT=b2_b[0:6, dc_ * 128:(dc_ + 1) * 128],
                               rhs=w6col_b[0:6, 0:1], start=True, stop=True,
                               skip_group_check=True), inc))
        b2wp = pe.inc()
        p.wait("vector", pe, b2wp)
        p.on("vector", lambda e: e.tensor_copy(
            out=b2w_sb[:], in_=Pr[:, 16:32]).then_inc(dve.h, 1))
        b2w_ok = dve.inc()

        wconv_mark = {}
        gqw = 0

        def stream_w1(e_i, fc, pe_free):
            nonlocal gqw
            for hf in range(2):
                sb = gqw % 2
                if gqw >= 1:
                    p.wait("sync", act, wconv_mark[gqw - 1])
                p.on("sync", lambda e, e_=e_i, fc_=fc, hf_=hf, sb_=sb: e.dma_start(
                    out=wstg[:, 0:T],
                    in_=w1[e_, fc_, :, hf_ * 8:(hf_ + 1) * 8, :].rearrange(
                        "p a b -> p (a b)")).then_inc(lw[sb_].h, 16))
                lv = lw[sb].inc(16)
                p.wait("scalar", lw[sb], lv)
                if hf == 0 and pe_free is not None:
                    p.wait("scalar", pe, pe_free)
                p.on("scalar", lambda e, hf_=hf: e.activation(
                    out=wbf[:, hf_ * 1024:(hf_ + 1) * 1024], in_=wstg[:, 0:T],
                    func=AF.Copy).then_inc(act.h, 1))
                wconv_mark[gqw] = act.inc()
                gqw += 1
            return act.n

        def stream_w2(e_i, dc, pe_free):
            nonlocal gqw
            sb = gqw % 2
            if gqw >= 1:
                p.wait("sync", act, wconv_mark[gqw - 1])
            p.on("sync", lambda e, e_=e_i, dc_=dc, sb_=sb: e.dma_start(
                out=wstg2[:, 0:1408],
                in_=w2[e_, dc_, :, :, :].rearrange("p a b -> p (a b)")).then_inc(
                lw[sb_].h, 16))
            lv = lw[sb].inc(16)
            p.wait("scalar", lw[sb], lv)
            if pe_free is not None:
                p.wait("scalar", pe, pe_free)
            p.on("scalar", lambda e: e.activation(
                out=wbf2[:, 0:1408], in_=wstg2[:, 0:1408],
                func=AF.Copy).then_inc(act.h, 1))
            wconv_mark[gqw] = act.inc()
            gqw += 1
            return act.n

        prev_pe_mark = None
        acc_mark = {}
        hid_mark = {}
        for e_i in range(6):
            for fc in range(NF1):
                par = fc % 2
                ph = S4[:, par * T:(par + 1) * T]
                wready = stream_w1(e_i, fc, prev_pe_mark)
                p.wait("tensor", act, wready)
                if e_i == 0 and fc == 0:
                    p.wait("tensor", dve, xln_ok)
                    p.wait("tensor", dve, b2w_ok)
                if prev_pe_mark is not None and hid_mark.get("last_evac") is not None:
                    p.wait("tensor", dve, hid_mark["evac", (e_i * NF1 + fc) - 2])                         if ("evac", (e_i * NF1 + fc) - 2) in hid_mark else None
                for ti in range(2):
                    sl = slice(ti * 512, (ti + 1) * 512)
                    for dc in range(NCH):
                        last = (ti == 1 and dc == NCH - 1)
                        p.on("tensor", lambda e, dc_=dc, sl_=sl, ph_=ph,
                             inc=(pe if last else None): _inc(e.matmul(
                                out=ph_[:, sl_],
                                lhsT=wbf[:, dc_ * 128:(dc_ + 1) * 128],
                                rhs=xln[dc_][:, sl_], start=(dc_ == 0),
                                stop=(dc_ == NCH - 1), skip_group_check=True), inc))
                prev_pe_mark = pe.inc()
                p.wait("scalar", pe, prev_pe_mark)
                if ("gelu", e_i * NF1 + fc - 1) in hid_mark:
                    p.wait("scalar", dve, hid_mark["gelu", e_i * NF1 + fc - 1])
                p.on("scalar", lambda e, e_=e_i, fc_=fc, ph_=ph: e.activation(
                    out=sgbuf[:], in_=ph_[:, 0:T], func=AF.Gelu,
                    bias=b1_t[:, e_ * NF1 + fc_:e_ * NF1 + fc_ + 1]).then_inc(act.h, 1))
                gv = act.inc()
                p.wait("vector", act, gv)
                p.on("vector", lambda e, e_=e_i, fc_=fc: e.tensor_scalar_mul(
                    out=hid[fc_][:, 0:T], in0=sgbuf[:],
                    scalar1=wcol_sb[:, e_:e_ + 1]).then_inc(dve.h, 1))
                hid_mark["gelu", e_i * NF1 + fc] = dve.inc()
            hid_done = dve.n
            for dc in range(NCH):
                par = dc % 2
                pd = S4[:, par * T:(par + 1) * T]
                wready = stream_w2(e_i, dc, prev_pe_mark)
                p.wait("tensor", act, wready)
                if dc == 0:
                    p.wait("tensor", dve, hid_done)
                if ("acc", e_i, dc - 2) in acc_mark:
                    p.wait("tensor", dve, acc_mark["acc", e_i, dc - 2])
                elif e_i > 0 and dc < 2:
                    p.wait("tensor", dve, acc_mark["acc", e_i - 1, NCH - 2 + dc])
                for ti in range(2):
                    sl = slice(ti * 512, (ti + 1) * 512)
                    for hc in range(NF1):
                        last = (ti == 1 and hc == NF1 - 1)
                        p.on("tensor", lambda e, hc_=hc, sl_=sl, pd_=pd,
                             inc=(pe if last else None): _inc(e.matmul(
                                out=pd_[:, sl_],
                                lhsT=wbf2[:, hc_ * 128:(hc_ + 1) * 128],
                                rhs=hid[hc_][:, sl_], start=(hc_ == 0),
                                stop=(hc_ == NF1 - 1), skip_group_check=True), inc))
                prev_pe_mark = pe.inc()
                p.wait("vector", pe, prev_pe_mark)
                if e_i == 0:
                    p.on("vector", lambda e, dc_=dc, pd_=pd: e.tensor_copy(
                        out=out_acc[:, dc_ * T:(dc_ + 1) * T],
                        in_=pd_[:, 0:T]).then_inc(dve.h, 1))
                else:
                    p.on("vector", lambda e, dc_=dc, pd_=pd: e.tensor_add(
                        out=out_acc[:, dc_ * T:(dc_ + 1) * T],
                        in0=out_acc[:, dc_ * T:(dc_ + 1) * T],
                        in1=pd_[:, 0:T]).then_inc(dve.h, 1))
                acc_mark["acc", e_i, dc] = dve.inc()
        acc_done = dve.n

        y_mark = {}
        for dc in range(NCH):
            par = dc % 2
            if dc < 2:
                p.wait("gpsimd", dve, acc_done)
            else:
                p.wait("gpsimd", dve, y_mark[dc - 2])
            p.on("gpsimd", lambda e, dc_=dc, par_=par: e.dma_start(
                out=xa[par_][:], in_=xin_r[dc_][:, 0:T]).then_inc(gx[par_].h, 16))
            xv = gx[par].inc(16)
            p.wait("vector", gx[par], xv)
            if dc >= 2:
                p.wait("vector", stp[par], stp[par].n)
            p.on("vector", lambda e, dc_=dc: e.tensor_scalar(
                out=sgbuf[:], in0=out_acc[:, dc_ * T:(dc_ + 1) * T],
                scalar1=b2w_sb[:, dc_:dc_ + 1], scalar2=0.5, op0=ALU.add,
                op1=ALU.mult).then_inc(dve.h, 1))
            p.wait("vector", dve, dve.inc())
            p.on("vector", lambda e, par_=par: e.tensor_add(
                out=ybuf[par_][:], in0=sgbuf[:], in1=xa[par_][:]).then_inc(dve.h, 1))
            y_mark[dc] = dve.inc()
            p.wait("gpsimd", dve, y_mark[dc])
            p.on("gpsimd", lambda e, dc_=dc, par_=par: e.dma_start(
                out=yout_r[dc_], in_=ybuf[par_][:]).then_inc(stp[par_].h, 16))
            stp[par].inc(16)
        p.wait("gpsimd", stp[0], stp[0].n)
        p.wait("gpsimd", stp[1], stp[1].n)
        p.finalize()
    return nc



def prep_l0ffn_inputs(wq_gate, s_gate, wq_up, s_up, wq_down, s_down, consts):
    wg = np.ascontiguousarray(
        np.asarray(wq_gate).T.reshape(NCH, 128, NH0, 128).transpose(2, 1, 0, 3))
    wu = np.ascontiguousarray(
        np.asarray(wq_up).T.reshape(NCH, 128, NH0, 128).transpose(2, 1, 0, 3))
    wd = np.ascontiguousarray(
        np.asarray(wq_down).T.reshape(NH0, 128, NCH, 128).transpose(2, 1, 0, 3))
    sg = np.ascontiguousarray(np.asarray(s_gate).reshape(NH0, 128).T)
    su = np.ascontiguousarray(np.asarray(s_up).reshape(NH0, 128).T)
    sd = np.ascontiguousarray(np.asarray(s_down).reshape(NCH, 128).T)
    d = dict(wg=wg, wu=wu, wd=wd, sg=sg, su=su, sd=sd)
    d.update(const_inputs(consts))
    return d


def rope_tables_host():
    HALF = 64
    inv = 1.0 / 10000.0 ** (np.arange(0, HALF, 2)[:HALF // 2].astype(np.float64) / HALF)
    ang = np.outer(np.arange(S_FULL), inv)
    cos = np.asarray(np.cos(ang), np.float32).T
    sin = np.asarray(np.sin(ang), np.float32).T
    c4 = np.float32(128.0 ** 0.25)
    CC = np.ascontiguousarray(np.tile(cos, (4, 1)) * c4)
    SS = np.ascontiguousarray(np.concatenate([sin, -sin, sin, -sin], axis=0) * c4)
    return CC, SS


def qk_perm():
    perm = np.zeros(2048, np.int64)
    for hh in range(32):
        b = hh * 64
        perm[b:b + 32] = b + 2 * np.arange(32)
        perm[b + 32:b + 64] = b + 2 * np.arange(32) + 1
    return perm


def _slab16(w_t):
    n = w_t.shape[1] // 128
    return np.ascontiguousarray(
        w_t.reshape(NCH, 128, n, 128).transpose(2, 1, 0, 3))


def prep_l0attn_inputs(wq, wk, wv, lq1, lq2, lk1, lk2, consts):
    perm = qk_perm()
    CC, SS = rope_tables_host()
    lam_in = np.ascontiguousarray(np.concatenate(
        [np.asarray(lq1), np.asarray(lq2), np.asarray(lk1),
         np.asarray(lk2)]).reshape(1, 256))
    shared = dict(cc_t=CC, ss_t=SS, lam_in=lam_in)
    shared.update(const_inputs(consts))
    wq = np.asarray(wq)
    wk = np.asarray(wk)
    wv = np.asarray(wv)
    per_half = []
    for h in range(2):
        rows = perm[h * 1024:(h + 1) * 1024]
        wq_s = _slab16(np.ascontiguousarray(wq[rows].T))
        wk_s = _slab16(np.ascontiguousarray(wk[rows].T))
        wv_s = _slab16(np.ascontiguousarray(wv[h * 1024:(h + 1) * 1024].T))
        per_half.append(dict(wq_s=wq_s, wk_s=wk_s, wv_s=wv_s))
    return per_half, shared


def prep_outproj_inputs(wout, subln, consts):
    wo = _slab16(np.ascontiguousarray(np.asarray(wout).T))
    d = dict(wo=wo, subln=np.ascontiguousarray(
        np.asarray(subln).reshape(128, 1)))
    d.update(const_inputs(consts))
    return d


def run_l0attn(x_fm_full, per_half, shared):
    if "l0attn" not in _nc_cache:
        _nc_cache["l0attn"] = build_l0attn()
    nc = _nc_cache["l0attn"]
    in_maps = []
    for c in range(8):
        be, h = c // 2, c % 2
        in_maps.append(dict(xin=x_fm_full[be], **per_half[h], **shared))
    res = run_bass_kernel_spmd(nc, in_maps, core_ids=list(range(8)))
    return [np.asarray(res.results[c]["aout"]) for c in range(8)]


def run_outproj(x_fm_slices, a_fm_slices, wdict):
    if "outproj" not in _nc_cache:
        _nc_cache["outproj"] = build_outproj()
    nc = _nc_cache["outproj"]
    in_maps = [dict(xin=x_fm_slices[c], ain=a_fm_slices[c], **wdict)
               for c in range(8)]
    res = run_bass_kernel_spmd(nc, in_maps, core_ids=list(range(8)))
    return [np.asarray(res.results[c]["yout"]) for c in range(8)]


def prep_l1attn_inputs(wq, bq, wk, bk, wv, bv, wo, bo, wmap_q, wmap_k, consts):
    d = dict(
        wq_s=_slab16(np.ascontiguousarray(np.asarray(wq).T)),
        wk_s=_slab16(np.ascontiguousarray(np.asarray(wk).T)),
        wv_s=_slab16(np.ascontiguousarray(np.asarray(wv).T)),
        wo_s=_slab16(np.ascontiguousarray(np.asarray(wo).T)),
        bq=np.ascontiguousarray(np.asarray(bq).reshape(NCH, 128).T),
        bk=np.ascontiguousarray(np.asarray(bk).reshape(NCH, 128).T),
        bv=np.ascontiguousarray(np.asarray(bv).reshape(NCH, 128).T),
        bo=np.ascontiguousarray(np.asarray(bo).reshape(NCH, 128).T),
        wmq=np.ascontiguousarray(np.asarray(wmap_q).T),
        wmk=np.ascontiguousarray(np.asarray(wmap_k).T),
    )
    d.update(const_inputs(consts))
    return d


def prep_l1moe_inputs(gate_W, W1, b1, W2, b2, consts):
    W1 = np.asarray(W1)
    W2 = np.asarray(W2)
    d = dict(
        w1=np.ascontiguousarray(
            W1.reshape(6, NCH, 128, NF1, 128).transpose(0, 3, 2, 1, 4)),
        w2=np.ascontiguousarray(
            W2.reshape(6, NF1, 128, NCH, 128).transpose(0, 3, 2, 1, 4)),
        b1t=np.ascontiguousarray(
            np.asarray(b1).reshape(6, NF1, 128).transpose(2, 0, 1).reshape(128, 6 * NF1)),
        b2t=np.ascontiguousarray(
            np.asarray(b2).reshape(6, NCH, 128).transpose(1, 0, 2)),
        gwt=np.ascontiguousarray(np.asarray(gate_W).T.reshape(NCH, 128, 6)),
    )
    d.update(const_inputs(consts))
    return d


def _rot_inputs(x_fm_full):
    outs = []
    for c in range(8):
        be, h = c // 2, c % 2
        xf = x_fm_full[be]
        own = xf[:, h * T:(h + 1) * T]
        peer = xf[:, (1 - h) * T:(2 - h) * T]
        outs.append(np.ascontiguousarray(np.concatenate([own, peer], axis=1)))
    return outs


def run_l1attn(x_fm_full, wdict):
    if "l1attn" not in _nc_cache:
        _nc_cache["l1attn"] = build_l1attn()
    nc = _nc_cache["l1attn"]
    xr = _rot_inputs(x_fm_full)
    in_maps = [dict(xin=xr[c], **wdict) for c in range(8)]
    res = run_bass_kernel_spmd(nc, in_maps, core_ids=list(range(8)))
    return [np.asarray(res.results[c]["yout"]) for c in range(8)]


def run_l1moe(x_fm_full, wdict):
    if "l1moe" not in _nc_cache:
        _nc_cache["l1moe"] = build_l1moe()
    nc = _nc_cache["l1moe"]
    xr = _rot_inputs(x_fm_full)
    in_maps = [dict(xin=xr[c], **wdict) for c in range(8)]
    res = run_bass_kernel_spmd(nc, in_maps, core_ids=list(range(8)))
    return [np.asarray(res.results[c]["yout"]) for c in range(8)]


_nc_cache = {}


def run_l0ffn(x_fm_slices, wdict):
    if "l0ffn" not in _nc_cache:
        _nc_cache["l0ffn"] = build_l0ffn()
    nc = _nc_cache["l0ffn"]
    in_maps = [dict(xin=x_fm_slices[c], **wdict) for c in range(8)]
    res = run_bass_kernel_spmd(nc, in_maps, core_ids=list(range(8)))
    return [np.asarray(res.results[c]["yout"]) for c in range(8)]



def build_fused(debug=False):
    nc = bass.Bass("TRN2", target_bir_lowering=False)
    xin = nc.dram_tensor("xin", [D, T], F32, kind="ExternalInput").ap()
    wg = nc.dram_tensor("wg", [NH0, 128, NCH, 128], I32, kind="ExternalInput").ap()
    wu = nc.dram_tensor("wu", [NH0, 128, NCH, 128], I32, kind="ExternalInput").ap()
    wd = nc.dram_tensor("wd", [NCH, 128, NH0, 128], I32, kind="ExternalInput").ap()
    sg_d = nc.dram_tensor("sg", [128, NH0], F32, kind="ExternalInput").ap()
    su_d = nc.dram_tensor("su", [128, NH0], F32, kind="ExternalInput").ap()
    sd_d = nc.dram_tensor("sd", [128, NCH], F32, kind="ExternalInput").ap()
    wq0_s = nc.dram_tensor("wq0_s", [NCH, 128, NCH, 128], F32, kind="ExternalInput").ap()
    wk0_s = nc.dram_tensor("wk0_s", [NCH, 128, NCH, 128], F32, kind="ExternalInput").ap()
    wv0_s = nc.dram_tensor("wv0_s", [NCH, 128, NCH, 128], F32, kind="ExternalInput").ap()
    wo0_s = nc.dram_tensor("wo0_s", [NCH, 128, NCH, 128], F32, kind="ExternalInput").ap()
    subln_d = nc.dram_tensor("subln", [128, 1], F32, kind="ExternalInput").ap()
    cc_d = nc.dram_tensor("cc_t", [128, T], F32, kind="ExternalInput").ap()
    ss_d = nc.dram_tensor("ss_t", [128, T], F32, kind="ExternalInput").ap()
    lam_d = nc.dram_tensor("lam_in", [1, 256], F32, kind="ExternalInput").ap()
    wq1_s = nc.dram_tensor("wq1_s", [NCH, 128, NCH, 128], F32, kind="ExternalInput").ap()
    wk1_s = nc.dram_tensor("wk1_s", [NCH, 128, NCH, 128], F32, kind="ExternalInput").ap()
    wv1_s = nc.dram_tensor("wv1_s", [NCH, 128, NCH, 128], F32, kind="ExternalInput").ap()
    wo1_s = nc.dram_tensor("wo1_s", [NCH, 128, NCH, 128], F32, kind="ExternalInput").ap()
    bq_d = nc.dram_tensor("bq", [128, NCH], F32, kind="ExternalInput").ap()
    bk_d = nc.dram_tensor("bk", [128, NCH], F32, kind="ExternalInput").ap()
    bv_d = nc.dram_tensor("bv", [128, NCH], F32, kind="ExternalInput").ap()
    bo_d = nc.dram_tensor("bo", [128, NCH], F32, kind="ExternalInput").ap()
    wmq_d = nc.dram_tensor("wmq", [128, 64], F32, kind="ExternalInput").ap()
    wmk_d = nc.dram_tensor("wmk", [128, 64], F32, kind="ExternalInput").ap()
    w1 = nc.dram_tensor("w1", [6, NF1, 128, NCH, 128], F32, kind="ExternalInput").ap()
    w2 = nc.dram_tensor("w2", [6, NCH, 128, NF1, 128], F32, kind="ExternalInput").ap()
    b1_d = nc.dram_tensor("b1t", [128, 6 * NF1], F32, kind="ExternalInput").ap()
    b2_d = nc.dram_tensor("b2t", [16, 6, 128], F32, kind="ExternalInput").ap()
    gw_d = nc.dram_tensor("gwt", [16, 128, 6], F32, kind="ExternalInput").ap()
    iota_d = nc.dram_tensor("iota54", [128, 54], F32, kind="ExternalInput").ap()
    b1r_d = nc.dram_tensor("b1r", [6, 1408], F32, kind="ExternalInput").ap()
    cst_d = {k: nc.dram_tensor("c_" + k, list(v.shape), CONST_DTYPES[k],
                               kind="ExternalInput").ap()
             for k, v in make_consts().items()}
    yout = nc.dram_tensor("yout", [D, T], F32, kind="ExternalOutput").ap()
    xin_r = xin.rearrange("(c p) t -> c p t", p=128)
    yout_r = yout.rearrange("(c p) t -> c p t", p=128)
    ccA_in = nc.dram_tensor("ccA_in", [2 * D, T], BF16).ap()
    ccA_out = nc.dram_tensor("ccA_out", [4 * D, T], BF16).ap()
    ccR_in = nc.dram_tensor("ccR_in", [1040, 128], F32).ap()
    ccR_out = nc.dram_tensor("ccR_out", [1040, 128], F32).ap()
    ccG_in = nc.dram_tensor("ccG_in", [128, NCH], F32).ap()
    ccG_out = nc.dram_tensor("ccG_out", [128, NCH], F32).ap()
    if debug:
        dbg_x = nc.dram_tensor("dbg_x", [6, D, T], F32, kind="ExternalOutput").ap()
        dbg_x_r = dbg_x.rearrange("s (c p) t -> s c p t", p=128)
    groups = [[0, 1], [2, 3], [4, 5], [6, 7]]

    p = Prog(nc)
    _uid = [0]

    def sbuf(name, shape, dt):
        _uid[0] += 1
        return nc.sbuf_tensor(f"{name}_{_uid[0]}", shape, dt)

    def psum(name, shape, dt):
        _uid[0] += 1
        return nc.psum_tensor(f"{name}_{_uid[0]}", shape, dt)

    with ExitStack() as ctx:
        E = ctx.enter_context
        x_all = E(nc.sbuf_tensor("x_all", [128, NCH * T], F32))
        x_fm = [x_all[:, i * T:(i + 1) * T] for i in range(NCH)]
        c_ones128_f = E(nc.sbuf_tensor("c1f", [128, 1], F32))
        c_ones128_b = E(nc.sbuf_tensor("c1b", [128, 1], BF16))
        c_ones1_f = E(nc.sbuf_tensor("c1rf", [1, 128], F32))
        c_m128_b = E(nc.sbuf_tensor("cm128", [1, 128], BF16))
        c_ident_b = E(nc.sbuf_tensor("cidb", [128, 128], BF16))
        ones11b = E(nc.sbuf_tensor("ones11b", [1, 1], BF16))
        stg_i32 = [E(nc.sbuf_tensor(f"stg_i32{i}", [128, 1408], I32)) for i in range(2)]
        stg_f32 = [E(nc.sbuf_tensor(f"stg_f32{i}", [128, 1408], F32)) for i in range(2)]
        cstt = {"ones128_f": c_ones128_f, "ones128_b": c_ones128_b,
                "ones1_f": c_ones1_f, "m128_b": c_m128_b, "ident_b": c_ident_b}
        lx = [Sem(E(nc.semaphore(f"lx{i}"))) for i in range(2)]
        lw = [Sem(E(nc.semaphore(f"lw{i}"))) for i in range(2)]
        lwg = Sem(E(nc.semaphore("lwg")))
        gq = [Sem(E(nc.semaphore(f"gq{i}"))) for i in range(2)]
        stp = [Sem(E(nc.semaphore(f"st{i}"))) for i in range(2)]
        ldc = Sem(E(nc.semaphore("ldc")))
        ca = Sem(E(nc.semaphore("ca")))
        cb = Sem(E(nc.semaphore("cb")))
        cc = Sem(E(nc.semaphore("cc")))
        pe = Sem(E(nc.semaphore("pe")))
        act = Sem(E(nc.semaphore("act")))
        dve = Sem(E(nc.semaphore("dve")))
        sems = dict(lx=lx, lw=lw, gq=gq, stp=stp, ldc=ldc, cc=cc, pe=pe,
                    act=act, dve=dve)

        for dst, srcp in [(c_ones128_f, cst_d["ones128_f"]),
                          (c_ones128_b, cst_d["ones128_b"]),
                          (c_ones1_f, cst_d["ones1_f"]),
                          (c_m128_b, cst_d["m128_b"]),
                          (c_ident_b, cst_d["ident_b"])]:
            p.on("gpsimd", lambda e, d=dst, sr=srcp: e.dma_start(
                out=d[:], in_=sr).then_inc(ldc.h, 16))
            ldc.inc(16)
        for dc in range(NCH):
            p.on("gpsimd", lambda e, dc_=dc: e.dma_start(
                out=x_fm[dc_], in_=xin_r[dc_]).then_inc(ldc.h, 16))
            ldc.inc(16)
        consts_ok = ldc.n
        for eng in ("tensor", "scalar", "vector", "sync"):
            p.wait(eng, ldc, consts_ok)
        p.on("vector", lambda e: e.memset(ones11b[:], 1.0).then_inc(dve.h, 1))
        dve.inc()

        def barrier():
            pv, av, dvv = pe.n, act.n, dve.n
            for eng in Prog.ENGINES:
                p.wait(eng, pe, pv)
                p.wait(eng, act, av)
                p.wait(eng, dve, dvv)

        def ln_res(ls, xln, t0, tl, want_u=False):
            nt = tl // 512
            Pb, rows, xsq = ls["Pbig"], ls["rows"], ls["xsq"]
            mu_sb, rs_sb = ls["mu_sb"], ls["rs_sb"]
            act0, dve0 = act.n, dve.n
            p.wait("tensor", act, act0)
            p.wait("tensor", dve, dve0)
            p.wait("scalar", dve, dve0)
            for dc in range(NCH):
                if dc >= 1:
                    p.wait("scalar", pe, ln_pe_mark[dc - 1])
                p.on("scalar", lambda e, dc_=dc: e.activation(
                    out=xsq[:, 0:tl], in_=x_fm[dc_][:, t0:t0 + tl],
                    func=AF.Square).then_inc(act.h, 1))
                av = act.inc()
                p.wait("tensor", act, av)
                for ti in range(nt):
                    sl = slice(ti * 512, (ti + 1) * 512)
                    p.on("tensor", lambda e, dc_=dc, sl_=sl: e.matmul(
                        out=Pb[0:1, sl_], lhsT=cstt["ones128_f"][:, 0:1],
                        rhs=x_fm[dc_][:, t0 + sl_.start:t0 + sl_.stop],
                        start=(dc_ == 0), stop=(dc_ == NCH - 1),
                        skip_group_check=True))
                    p.on("tensor", lambda e, dc_=dc, sl_=sl,
                         inc=(pe if ti == nt - 1 else None): _inc(e.matmul(
                            out=Pb[0:1, tl + sl_.start:tl + sl_.stop],
                            lhsT=cstt["ones128_f"][:, 0:1], rhs=xsq[:, sl_],
                            start=(dc_ == 0), stop=(dc_ == NCH - 1),
                            skip_group_check=True), inc))
                ln_pe_mark[dc] = pe.inc()
            pe_done = pe.n
            p.wait("vector", pe, pe_done)
            p.on("vector", lambda e: e.tensor_scalar_mul(
                out=rows[0:1, 0:tl], in0=Pb[0:1, 0:tl], scalar1=1.0 / D))
            p.on("vector", lambda e: e.tensor_scalar_mul(
                out=rows[0:1, tl:2 * tl], in0=Pb[0:1, tl:2 * tl], scalar1=1.0 / D))
            p.on("vector", lambda e: e.tensor_mul(
                out=rows[0:1, 2 * tl:3 * tl], in0=rows[0:1, 0:tl],
                in1=rows[0:1, 0:tl]))
            p.on("vector", lambda e: e.tensor_sub(
                out=rows[0:1, tl:2 * tl], in0=rows[0:1, tl:2 * tl],
                in1=rows[0:1, 2 * tl:3 * tl]))
            p.on("vector", lambda e: e.tensor_scalar_add(
                out=rows[0:1, tl:2 * tl], in0=rows[0:1, tl:2 * tl],
                scalar1=1e-5).then_inc(dve.h, 1))
            dv = dve.inc()
            p.wait("scalar", dve, dv)
            p.on("scalar", lambda e: e.activation(
                out=rows[0:1, 2 * tl:3 * tl], in_=rows[0:1, tl:2 * tl],
                func=AF.Sqrt).then_inc(act.h, 1))
            av = act.inc()
            p.wait("vector", act, av)
            p.on("vector", lambda e: e.reciprocal(
                out=rows[0:1, tl:2 * tl],
                in_=rows[0:1, 2 * tl:3 * tl]).then_inc(dve.h, 1))
            dv = dve.inc()
            p.wait("tensor", dve, dv)
            for ti in range(nt):
                sl = slice(ti * 512, (ti + 1) * 512)
                p.on("tensor", lambda e, sl_=sl: e.matmul(
                    out=Pb[:, sl_], lhsT=cstt["ones1_f"][0:1, :],
                    rhs=rows[0:1, sl_], start=True, stop=True,
                    skip_group_check=True))
                p.on("tensor", lambda e, sl_=sl,
                     inc=(pe if ti == nt - 1 else None): _inc(e.matmul(
                        out=Pb[:, tl + sl_.start:tl + sl_.stop],
                        lhsT=cstt["ones1_f"][0:1, :],
                        rhs=rows[0:1, tl + sl_.start:tl + sl_.stop],
                        start=True, stop=True, skip_group_check=True), inc))
            pe_bc = pe.inc()
            p.wait("vector", pe, pe_bc)
            p.on("vector", lambda e: e.tensor_copy(
                out=mu_sb[:, 0:tl], in_=Pb[:, 0:tl]))
            p.on("vector", lambda e: e.tensor_copy(
                out=rs_sb[:, 0:tl], in_=Pb[:, tl:2 * tl]).then_inc(dve.h, 1))
            p.wait("vector", dve, dve.inc())
            for dc in range(NCH):
                p.on("vector", lambda e, dc_=dc: e.tensor_sub(
                    out=xsq[:, 0:tl], in0=x_fm[dc_][:, t0:t0 + tl],
                    in1=mu_sb[:, 0:tl]))
                p.on("vector", lambda e, dc_=dc,
                     inc=(dve if dc == NCH - 1 else None): _inc(e.tensor_mul(
                        out=xln[dc_][:, 0:tl], in0=xsq[:, 0:tl],
                        in1=rs_sb[:, 0:tl]), inc))
            xln_ok = dve.inc()
            if want_u:
                ubf = ls["ubf"]
                p.wait("tensor", dve, xln_ok)
                for dc in range(NCH):
                    for ti in range(nt):
                        sl = slice(ti * 512, (ti + 1) * 512)
                        last = (dc == NCH - 1 and ti == nt - 1)
                        p.on("tensor", lambda e, dc_=dc, sl_=sl,
                             inc=(pe if last else None): _inc(e.matmul(
                                out=Pb[0:1, sl_], lhsT=cstt["ones128_b"][:, 0:1],
                                rhs=xln[dc_][:, sl_], start=(dc_ == 0),
                                stop=(dc_ == NCH - 1), skip_group_check=True), inc))
                pe_u = pe.inc()
                p.wait("vector", pe, pe_u)
                p.on("vector", lambda e: e.tensor_copy(
                    out=ubf[0:1, 0:tl], in_=Pb[0:1, 0:tl]).then_inc(dve.h, 1))
                return dve.inc()
            return xln_ok

        wconv_mark = {}
        gqw = [0]

        def stream_conv(dram_ap, nbytes_cols, dst, dst_sl, pe_free, scale=None,
                        dual_q=False):
            i = gqw[0]
            sb = i % 2
            q = "gpsimd" if (dual_q and sb == 1) else "sync"
            sem = lwg if q == "gpsimd" else lw[sb]
            stg = (stg_i32 if dram_ap.dtype == I32 else stg_f32)[sb]
            if i >= 2:
                p.wait(q, act, wconv_mark[i - 2])
            p.on(q, lambda e, d_=dram_ap, s_=stg, n_=nbytes_cols,
                 sm_=sem: e.dma_start(
                out=s_[:, 0:n_], in_=d_).then_inc(sm_.h, 16))
            lv = sem.inc(16)
            p.wait("scalar", sem, lv)
            if pe_free is not None:
                p.wait("scalar", pe, pe_free)
            if scale is None:
                p.on("scalar", lambda e, s_=stg, n_=nbytes_cols, d_=dst,
                     sl_=dst_sl: e.activation(
                    out=d_[:, sl_], in_=s_[:, 0:n_], func=AF.Copy).then_inc(act.h, 1))
            else:
                p.on("scalar", lambda e, s_=stg, n_=nbytes_cols, d_=dst,
                     sl_=dst_sl, sc_=scale: e.activation(
                    out=d_[:, sl_], in_=s_[:, 0:n_], func=AF.Copy,
                    scale=sc_).then_inc(act.h, 1))
            wconv_mark[i] = act.inc()
            gqw[0] += 1
            return act.n

        def l0ffn_body(fs):
            xln, hidG, ubf = fs["xln"], fs["hid"], fs["ubf"]
            wgu2 = fs["wgu"]
            wdb2 = fs["wdb"]
            sg_t, su_t, sd5_t, sdm_t = fs["sg_t"], fs["su_t"], fs["sd5_t"], fs["sdm_t"]
            sgb, sub = fs["sgb"], fs["sub"]
            u2acc = fs["u2acc"]
            PA, PB = fs["PA"], fs["PB"]
            ubf_ok = ln_res(fs, xln, 0, T, want_u=True)
            p.on("vector", lambda e: e.memset(u2acc[0:1, 0:T], 0.0).then_inc(dve.h, 1))
            p.wait("vector", dve, dve.inc())
            mm_mark = {}
            evac_mark = {}
            hid_mark = {}
            stt_mark = {}
            piece = [0]
            u2add_mark = [0]
            for g in range(NH0 // 4):
                for hl in range(4):
                    hc = g * 4 + hl
                    par = hc % 2
                    pg = PA[:, par * 1024:(par + 1) * 1024]
                    pu = PB[:, 0:1024]
                    wgu = wgu2[par]
                    for mi, wsrc in enumerate([wg, wu]):
                        for q in range(4):
                            stream_conv(
                                wsrc[hc, :, q * 4:(q + 1) * 4, :].rearrange(
                                    "p a b -> p (a b)"), 512, wgu,
                                slice(mi * 2048 + q * 512, mi * 2048 + (q + 1) * 512),
                                mm_mark.get(hc - 2), dual_q=True)
                    p.wait("tensor", act, act.n)
                    if hc == 0:
                        p.wait("tensor", dve, ubf_ok)
                    if hl == 0 and u2add_mark[0] > 0:
                        p.wait("tensor", dve, u2add_mark[0])
                    if hc >= 2:
                        p.wait("tensor", act, evac_mark[hc - 2])
                    if hc >= 1:
                        p.wait("tensor", act, evac_mark[hc - 1])
                    for mi in range(2):
                        ps = pg if mi == 0 else pu
                        for ti in range(2):
                            sl = slice(ti * 512, (ti + 1) * 512)
                            for dc in range(NCH):
                                p.on("tensor", lambda e, dc_=dc, sl_=sl, ps_=ps,
                                     w_=wgu, mi_=mi: e.matmul(
                                        out=ps_[:, sl_],
                                        lhsT=w_[:, mi_ * 2048 + dc_ * 128:
                                                mi_ * 2048 + (dc_ + 1) * 128],
                                        rhs=xln[dc_][:, sl_], start=(dc_ == 0),
                                        stop=False, skip_group_check=True))
                            p.on("tensor", lambda e, sl_=sl, ps_=ps,
                                 inc=(pe if (mi == 1 and ti == 1) else None):
                                 _inc(e.matmul(
                                    out=ps_[:, sl_], lhsT=cstt["m128_b"][0:1, :],
                                    rhs=ubf[0:1, sl_], start=False, stop=True,
                                    skip_group_check=True), inc))
                    mm_mark[hc] = pe.inc()
                    p.wait("scalar", pe, mm_mark[hc])
                    if hc >= 1:
                        p.wait("scalar", dve, hid_mark[hc - 1])
                    p.on("scalar", lambda e, hc_=hc, pg_=pg: e.activation(
                        out=sgb[:, 0:T], in_=pg_[:, 0:1024], func=AF.Silu,
                        scale=sg_t[:, hc_:hc_ + 1]))
                    p.on("scalar", lambda e, hc_=hc: e.activation(
                        out=sub[:, 0:T], in_=PB[:, 0:1024], func=AF.Copy,
                        scale=su_t[:, hc_:hc_ + 1]).then_inc(act.h, 1))
                    evac_mark[hc] = act.inc()
                    p.wait("vector", act, evac_mark[hc])
                    p.on("vector", lambda e, hl_=hl: e.tensor_mul(
                        out=hidG[hl_][:, 0:T], in0=sgb[:, 0:T],
                        in1=sub[:, 0:T]).then_inc(dve.h, 1))
                    hid_mark[hc] = dve.inc()
                p.wait("tensor", dve, hid_mark[g * 4 + 3])
                for hl in range(4):
                    for ti in range(2):
                        sl = slice(ti * 512, (ti + 1) * 512)
                        last = (hl == 3 and ti == 1)
                        p.on("tensor", lambda e, hl_=hl, sl_=sl,
                             inc=(pe if last else None): _inc(e.matmul(
                                out=PA[0:1, sl_], lhsT=cstt["ones128_b"][:, 0:1],
                                rhs=hidG[hl_][:, sl_], start=(hl_ == 0),
                                stop=(hl_ == 3), skip_group_check=True), inc))
                u2p = pe.inc()
                p.wait("vector", pe, u2p)
                p.on("vector", lambda e: e.tensor_add(
                    out=u2acc[0:1, 0:T], in0=u2acc[0:1, 0:T],
                    in1=PA[0:1, 0:1024]).then_inc(dve.h, 1))
                u2add_mark[0] = dve.inc()
                p.wait("vector", dve, u2add_mark[0])
                for dc in range(NCH):
                    wdb = wdb2[dc % 2]
                    pf = stt_mark.get(("pe", g, dc - 2))
                    if pf is None and g > 0:
                        pf = stt_mark.get(("pe", g - 1, NCH - 2 + dc))
                    stream_conv(
                        wd[dc, :, g * 4:(g + 1) * 4, :].rearrange(
                            "p a b -> p (a b)"), 512, wdb, slice(0, 512),
                        pf, dual_q=True)
                    p.wait("tensor", act, act.n)
                    for ti in range(2):
                        pcp = (dc * 2 + ti) % 2
                        pd = PB[:, 1024 + pcp * 512:1024 + (pcp + 1) * 512]
                        if piece[0] >= 2:
                            p.wait("tensor", dve, stt_mark[piece[0] - 2])
                        sl = slice(ti * 512, (ti + 1) * 512)
                        for hl in range(4):
                            last = (hl == 3)
                            p.on("tensor", lambda e, hl_=hl, sl_=sl, pd_=pd,
                                 wb_=wdb, inc=(pe if last else None):
                                 _inc(e.matmul(
                                    out=pd_, lhsT=wb_[:, hl_ * 128:(hl_ + 1) * 128],
                                    rhs=hidG[hl_][:, sl_], start=(hl_ == 0),
                                    stop=(hl_ == 3), skip_group_check=True), inc))
                        pm = pe.inc()
                        stt_mark[("pe", g, dc)] = pm
                        p.wait("vector", pe, pm)
                        p.on("vector", lambda e, dc_=dc, sl_=sl, pd_=pd:
                             e.scalar_tensor_tensor(
                                out=x_fm[dc_][:, sl_], in0=pd_,
                                scalar=sd5_t[:, dc_:dc_ + 1],
                                in1=x_fm[dc_][:, sl_], op0=ALU.mult,
                                op1=ALU.add).then_inc(dve.h, 1))
                        stt_mark[piece[0]] = dve.inc()
                        piece[0] += 1
            lastp = stt_mark[piece[0] - 1]
            p.wait("tensor", dve, lastp)
            for ti in range(2):
                sl = slice(ti * 512, (ti + 1) * 512)
                p.on("tensor", lambda e, sl_=sl,
                     inc=(pe if ti == 1 else None): _inc(e.matmul(
                        out=PB[:, 1024 + sl_.start:1024 + sl_.stop],
                        lhsT=cstt["ones1_f"][0:1, :], rhs=u2acc[0:1, sl_],
                        start=True, stop=True, skip_group_check=True), inc))
            u2bc = pe.inc()
            p.wait("vector", pe, u2bc)
            for dc in range(NCH):
                p.on("vector", lambda e, dc_=dc,
                     inc=(dve if dc == NCH - 1 else None): _inc(
                        e.scalar_tensor_tensor(
                            out=x_fm[dc_][:, 0:T], in0=PB[:, 1024:2048],
                            scalar=sdm_t[:, dc_:dc_ + 1], in1=x_fm[dc_][:, 0:T],
                            op0=ALU.mult, op1=ALU.add), inc))
            dve.inc()

        ln_pe_mark = {}

        def l0ffn_call():
            with ExitStack() as ph:
                E1 = ph.enter_context
                fs = {}
                xln_all = E1(sbuf("f_xln", [128, NCH * T], BF16))
                fs["xln"] = [xln_all[:, i * T:(i + 1) * T] for i in range(NCH)]
                hid_all = E1(sbuf("f_hid", [128, 4 * T], BF16))
                fs["hid"] = [hid_all[:, i * T:(i + 1) * T] for i in range(4)]
                fs["xsq"] = E1(sbuf("f_xsq", [128, T], F32))
                fs["rows"] = E1(sbuf("f_rows", [1, 3 * T], F32))
                fs["mu_sb"] = E1(sbuf("f_mu", [128, T], F32))
                fs["rs_sb"] = E1(sbuf("f_rs", [128, T], F32))
                fs["ubf"] = E1(sbuf("f_ubf", [1, T], BF16))
                fs["u2acc"] = E1(sbuf("f_u2a", [1, T], F32))
                fs["wgu"] = [E1(sbuf("f_wgu", [128, 4096], BF16)) for _ in range(2)]
                fs["wdb"] = [E1(sbuf("f_wdb", [128, 512], BF16)) for _ in range(2)]
                fs["sgb"] = E1(sbuf("f_sgb", [128, T], BF16))
                fs["sub"] = E1(sbuf("f_sub", [128, T], BF16))
                fs["sg_t"] = E1(sbuf("f_sgt", [128, NH0], F32))
                fs["su_t"] = E1(sbuf("f_sut", [128, NH0], F32))
                fs["sd_t"] = E1(sbuf("f_sdt", [128, NCH], F32))
                fs["sd5_t"] = E1(sbuf("f_sd5", [128, NCH], F32))
                fs["sdm_t"] = E1(sbuf("f_sdm", [128, NCH], F32))
                fs["PA"] = E1(psum("f_PA", [128, 2048], F32))
                fs["PB"] = E1(psum("f_PB", [128, 2048], F32))
                fs["Pbig"] = fs["PA"]
                barrier()
                for dst, srcp in [(fs["sg_t"], sg_d), (fs["su_t"], su_d),
                                  (fs["sd_t"], sd_d)]:
                    p.on("gpsimd", lambda e, d=dst, sr=srcp: e.dma_start(
                        out=d[:], in_=sr).then_inc(ldc.h, 16))
                    ldc.inc(16)
                p.wait("vector", ldc, ldc.n)
                p.on("vector", lambda e, fs_=fs: e.tensor_scalar_mul(
                    out=fs_["sd5_t"][:], in0=fs_["sd_t"][:],
                    scalar1=0.5).then_inc(dve.h, 1))
                p.wait("vector", dve, dve.inc())
                p.on("vector", lambda e, fs_=fs: e.tensor_scalar_mul(
                    out=fs_["sdm_t"][:], in0=fs_["sd5_t"][:],
                    scalar1=-128.0).then_inc(dve.h, 1))
                p.wait("vector", dve, dve.inc())
                p.wait("scalar", dve, dve.n)
                p.wait("tensor", dve, dve.n)
                l0ffn_body(fs)
                barrier()

        def l0attn_call():
            with ExitStack() as ph:
                E1 = ph.enter_context
                lamn = E1(sbuf("a_lamn", [128, 1], F32))
                sub_t = E1(sbuf("a_sub", [128, 1], F32))
                Q = E1(sbuf("a_Q", [128, NCH * T], BF16))
                lrows = E1(sbuf("a_lrows", [1, 256], F32))
                barrier()
                p.on("gpsimd", lambda e: e.dma_start(
                    out=sub_t[:], in_=subln_d).then_inc(ldc.h, 16))
                ldc.inc(16)
                with ExitStack() as phA:
                    E2 = phA.enter_context
                    xln_all = E2(sbuf("a_xln", [128, NCH * T], BF16))
                    xln = [xln_all[:, i * T:(i + 1) * T] for i in range(NCH)]
                    xsq = E2(sbuf("a_xsq", [128, T], F32))
                    rows = E2(sbuf("a_rows", [1, 3 * T], F32))
                    mu_sb = E2(sbuf("a_mu", [128, T], F32))
                    rs_sb = E2(sbuf("a_rs", [128, T], F32))
                    cc_s = E2(sbuf("a_ccs", [128, T], F32))
                    ss_s = E2(sbuf("a_sss", [128, T], F32))
                    lam4 = E2(sbuf("a_lam4", [1, 256], F32))
                    wpb2 = [E2(sbuf("a_wpb", [128, 2048], BF16)) for _ in range(2)]
                    t1b = E2(sbuf("a_t1", [128, T], F32))
                    t2b = E2(sbuf("a_t2", [128, T], F32))
                    t2s = E2(sbuf("a_t2s", [128, T], F32))
                    kvb = E2(sbuf("a_kvb", [128, T], BF16))
                    S4 = E2(psum("a_S4", [128, 2048], F32))
                    Pr = E2(psum("a_Pr", [128, 512], F32))
                    for dst, srcp in [(cc_s, cc_d), (ss_s, ss_d), (lam4, lam_d)]:
                        p.on("gpsimd", lambda e, d=dst, sr=srcp: e.dma_start(
                            out=d[:], in_=sr).then_inc(ldc.h, 16))
                        ldc.inc(16)
                    for eng in ("tensor", "scalar", "vector"):
                        p.wait(eng, ldc, ldc.n)
                    p.on("vector", lambda e: e.tensor_mul(
                        out=lrows[0:1, 0:64], in0=lam4[0:1, 0:64],
                        in1=lam4[0:1, 128:192]))
                    p.on("vector", lambda e: e.tensor_mul(
                        out=lrows[0:1, 64:128], in0=lam4[0:1, 64:128],
                        in1=lam4[0:1, 192:256]).then_inc(dve.h, 1))
                    p.wait("vector", dve, dve.inc())
                    p.on("vector", lambda e: e.reduce_sum(
                        out=lrows[0:1, 128:129], in_=lrows[0:1, 0:64],
                        axis=AX.X).then_inc(dve.h, 1))
                    p.wait("vector", dve, dve.inc())
                    p.on("vector", lambda e: e.reduce_sum(
                        out=lrows[0:1, 129:130], in_=lrows[0:1, 64:128],
                        axis=AX.X).then_inc(dve.h, 1))
                    dv = dve.inc()
                    p.wait("scalar", dve, dv)
                    p.on("scalar", lambda e: e.activation(
                        out=lrows[0:1, 130:132], in_=lrows[0:1, 128:130],
                        func=AF.Exp).then_inc(act.h, 1))
                    av = act.inc()
                    p.wait("vector", act, av)
                    p.on("vector", lambda e: e.tensor_sub(
                        out=lrows[0:1, 132:133], in0=lrows[0:1, 130:131],
                        in1=lrows[0:1, 131:132]).then_inc(dve.h, 1))
                    p.wait("vector", dve, dve.inc())
                    p.on("vector", lambda e: e.tensor_scalar(
                        out=lrows[0:1, 133:134], in0=lrows[0:1, 132:133],
                        scalar1=LAMBDA_INIT, scalar2=-1.0, op0=ALU.add,
                        op1=ALU.mult).then_inc(dve.h, 1))
                    dv = dve.inc()
                    p.wait("tensor", dve, dv)
                    p.on("tensor", lambda e: e.matmul(
                        out=Pr[:, 0:1], lhsT=cstt["ones1_f"][0:1, :],
                        rhs=lrows[0:1, 133:134], start=True, stop=True,
                        skip_group_check=True).then_inc(pe.h, 1))
                    pv = pe.inc()
                    p.wait("vector", pe, pv)
                    p.on("vector", lambda e: e.tensor_copy(
                        out=lamn[:, 0:1], in_=Pr[:, 0:1]).then_inc(dve.h, 1))
                    dve.inc()
                    xln_ok = ln_res(dict(Pbig=S4, rows=rows, xsq=xsq,
                                         mu_sb=mu_sb, rs_sb=rs_sb), xln, 0, T)
                    proj_mark = {}
                    rope_mark = {}
                    last_rope = None
                    last_swap = (None, 0)
                    swp = [0]

                    def do_rope(psl, dst_tile, dst_sl, gi):
                        nonlocal last_rope, last_swap
                        if last_swap[0] is not None:
                            p.wait("vector", last_swap[0], last_swap[1])
                        p.on("vector", lambda e, psl_=psl: e.tensor_mul(
                            out=t1b[:, 0:T], in0=S4[:, psl_], in1=cc_s[:, 0:T]))
                        p.on("vector", lambda e, psl_=psl: e.tensor_mul(
                            out=t2b[:, 0:T], in0=S4[:, psl_],
                            in1=ss_s[:, 0:T]).then_inc(dve.h, 1))
                        t2v = dve.inc()
                        sw = swp[0] % 2
                        swp[0] += 1
                        p.wait("gpsimd", dve, t2v)
                        if last_rope is not None:
                            p.wait("gpsimd", dve, last_rope)
                        for (ob, ib) in ((0, 32), (32, 0), (64, 96), (96, 64)):
                            p.on("gpsimd", lambda e, ob_=ob, ib_=ib, sw_=sw:
                                 e.dma_start(out=t2s[ob_:ob_ + 32, 0:T],
                                             in_=t2b[ib_:ib_ + 32, 0:T]).then_inc(
                                     gq[sw_].h, 16))
                        swv = gq[sw].inc(64)
                        p.wait("vector", gq[sw], swv)
                        p.on("vector", lambda e, d_=dst_tile, s_=dst_sl: e.tensor_add(
                            out=d_[:, s_], in0=t1b[:, 0:T],
                            in1=t2s[:, 0:T]).then_inc(dve.h, 1))
                        last_rope = dve.inc()
                        last_swap = (gq[sw], swv)
                        return last_rope

                    gi = 0
                    for pi, wsrc in [(1, wk0_s), (2, wv0_s), (0, wq0_s)]:
                        for oc in range(NCH):
                            par = gi % 2
                            psl = slice(par * 1024, (par + 1) * 1024)
                            wpb = wpb2[par]
                            for hf in range(2):
                                stream_conv(
                                    wsrc[oc, :, hf * 8:(hf + 1) * 8, :].rearrange(
                                        "p a b -> p (a b)"), 1024, wpb,
                                    slice(hf * 1024, (hf + 1) * 1024),
                                    proj_mark.get(gi - 2))
                            p.wait("tensor", act, act.n)
                            if gi == 0:
                                p.wait("tensor", dve, xln_ok)
                            if gi >= 2:
                                p.wait("tensor", dve, rope_mark[gi - 2])
                            for ti in range(2):
                                for dc in range(NCH):
                                    last = (ti == 1 and dc == NCH - 1)
                                    p.on("tensor", lambda e, dc_=dc, ti_=ti,
                                         psl_=psl, wb_=wpb,
                                         inc=(pe if last else None):
                                         _inc(e.matmul(
                                            out=S4[:, psl_.start + ti_ * 512:
                                                   psl_.start + (ti_ + 1) * 512],
                                            lhsT=wb_[:, dc_ * 128:(dc_ + 1) * 128],
                                            rhs=xln[dc_][:, ti_ * 512:(ti_ + 1) * 512],
                                            start=(dc_ == 0), stop=(dc_ == NCH - 1),
                                            skip_group_check=True), inc))
                            proj_mark[gi] = pe.inc()
                            p.wait("vector", pe, proj_mark[gi])
                            if pi == 0:
                                rope_mark[gi] = do_rope(psl, Q,
                                                        slice(oc * T, (oc + 1) * T), gi)
                            elif pi == 1:
                                if ca.n > 0:
                                    p.wait("vector", ca, ca.n)
                                rope_mark[gi] = do_rope(psl, kvb, slice(0, T), gi)
                                p.wait("gpsimd", dve, rope_mark[gi])
                                p.on("gpsimd", lambda e, oc_=oc: e.dma_start(
                                    out=ccA_in[oc_ * 128:(oc_ + 1) * 128, :],
                                    in_=kvb[:, 0:T]).then_inc(ca.h, 16))
                                ca.inc(16)
                            else:
                                if ca.n > 0:
                                    p.wait("vector", ca, ca.n)
                                p.on("vector", lambda e, psl_=psl: e.tensor_copy(
                                    out=kvb[:, 0:T],
                                    in_=S4[:, psl_]).then_inc(dve.h, 1))
                                rope_mark[gi] = dve.inc()
                                p.wait("gpsimd", dve, rope_mark[gi])
                                p.on("gpsimd", lambda e, oc_=oc: e.dma_start(
                                    out=ccA_in[D + oc_ * 128:D + (oc_ + 1) * 128, :],
                                    in_=kvb[:, 0:T]).then_inc(ca.h, 16))
                                ca.inc(16)
                            gi += 1
                            if gi == 32:
                                p.wait("gpsimd", ca, ca.n)
                                p.on("gpsimd", lambda e: e.collective_compute(
                                    "AllGather", ALU.bypass, replica_groups=groups,
                                    ins=[ccA_in.opt()],
                                    outs=[ccA_out.opt()]).then_inc(cc.h, 1))
                                cc.inc()
                with ExitStack() as phB:
                    E2 = phB.enter_context
                    attn_sb = E2(sbuf("a_attn", [128, NCH * T], BF16))
                    phat0 = E2(sbuf("a_ph0", [128, 2048], BF16))
                    phat1 = E2(sbuf("a_ph1", [128, 2048], BF16))
                    dbuf2L = [E2(sbuf("a_db2", [128, 2048], BF16)) for _ in range(2)]
                    diffT = E2(sbuf("a_dT", [128, 16 * 512], BF16))
                    Kvh = E2(sbuf("a_Kvh", [128, 2048], BF16))
                    Vtm = E2(sbuf("a_Vtm", [128, 2048], BF16))
                    sqf = E2(sbuf("a_sqf", [128, 512], F32))
                    rs512 = E2(sbuf("a_rs5", [128, 512], BF16))
                    rsrow = E2(sbuf("a_rsr", [1, 512], F32))
                    mrow = E2(sbuf("a_mrow", [128, 8], F32))
                    wob2 = [E2(sbuf("a_wob", [128, 2048], BF16)) for _ in range(2)]
                    S4 = E2(psum("b_S4", [128, 2048], F32))
                    Tp = E2(psum("b_Tp", [128, 2048], BF16))
                    Po = E2(psum("b_Po", [128, 512], F32))
                    Pr = E2(psum("b_Pr", [128, 512], F32))
                    p.wait("gpsimd", cc, cc.n)
                    sc_prev = None
                    last_cb = None
                    last_tp = None
                    last_tp_ev = None
                    pv_prev = None
                    ao_prev = None
                    kv_free = None
                    for vh in range(NCH):
                        if kv_free is not None:
                            p.wait("gpsimd", pe, kv_free)
                        Vvh = dbuf2L[0]
                        for (dst, base) in ((Kvh, 0), (Vvh, D)):
                            for half in range(2):
                                p.on("gpsimd", lambda e, d_=dst, b_=base, vh_=vh,
                                     hf_=half: e.dma_start(
                                    out=d_[:, hf_ * T:(hf_ + 1) * T],
                                    in_=ccA_out[hf_ * 2 * D + b_ + vh_ * 128:
                                                hf_ * 2 * D + b_ + (vh_ + 1) * 128,
                                                :]).then_inc(cb.h, 16))
                                cb.inc(16)
                        p.wait("tensor", cb, cb.n)
                        if last_tp_ev is not None:
                            p.wait("tensor", last_tp_ev[0], last_tp_ev[1])
                        for tc in range(16):
                            last = (tc == 15)
                            p.on("tensor", lambda e, tc_=tc,
                                 inc=(pe if last else None): _inc(e.transpose(
                                    out=Tp[:, tc_ * 128:(tc_ + 1) * 128],
                                    in_=Vvh[:, tc_ * 128:(tc_ + 1) * 128],
                                    identity=cstt["ident_b"][:, :]), inc))
                        vt = pe.inc()
                        last_tp = vt
                        p.wait("vector", pe, vt)
                        p.on("vector", lambda e: e.tensor_copy(
                            out=Vtm[:], in_=Tp[:, 0:2048]).then_inc(dve.h, 1))
                        vtm_ok = dve.inc()
                        last_tp_ev = (dve, vtm_ok)
                        for qg in range(2):
                            pend = None
                            for qt in range(4):
                                qgi = qg * 4 + qt
                                dbuf2 = dbuf2L[qt % 2]
                                for hh in range(2):
                                    hb = hh * 64
                                    if sc_prev is not None:
                                        p.wait("tensor", act, sc_prev)
                                    for kt in range(4):
                                        last = (kt == 3)
                                        p.on("tensor", lambda e, vh_=vh, qgi_=qgi,
                                             hb_=hb, kt_=kt,
                                             inc=(pe if last else None):
                                             _inc(e.matmul(
                                                out=S4[:, kt_ * 512:(kt_ + 1) * 512],
                                                lhsT=Q[hb_:hb_ + 64,
                                                       vh_ * T + qgi_ * 128:
                                                       vh_ * T + (qgi_ + 1) * 128],
                                                rhs=Kvh[hb_:hb_ + 64,
                                                        kt_ * 512:(kt_ + 1) * 512],
                                                start=True, stop=True,
                                                skip_group_check=True), inc))
                                    scv = pe.inc()
                                    p.wait("vector", pe, scv)
                                    p.on("vector", lambda e, hh_=hh: e.reduce_max(
                                        out=mrow[:, hh_:hh_ + 1], in_=S4[:, 0:2048],
                                        axis=AX.X, negate=True).then_inc(dve.h, 1))
                                    mv = dve.inc()
                                    p.wait("scalar", dve, mv)
                                    if hh == 0 and last_cb is not None:
                                        p.wait("scalar", dve, last_cb)
                                    p.on("scalar", lambda e, hh_=hh,
                                         ph_=(phat0 if hh == 0 else phat1):
                                         e.activation(
                                            out=ph_[:], in_=S4[:, 0:2048],
                                            func=AF.Exp, bias=mrow[:, hh_:hh_ + 1],
                                            accum_out=mrow[:, 2 + hh_:3 + hh_]
                                         ).then_inc(act.h, 1))
                                    sc_prev = act.inc()
                                p.wait("vector", act, sc_prev)
                                if last_tp is not None:
                                    p.wait("vector", pe, last_tp)
                                p.on("vector", lambda e: e.reciprocal(
                                    out=mrow[:, 4:5], in_=mrow[:, 2:3]))
                                p.on("vector", lambda e: e.reciprocal(
                                    out=mrow[:, 5:6],
                                    in_=mrow[:, 3:4]).then_inc(dve.h, 1))
                                p.wait("vector", dve, dve.inc())
                                p.on("vector", lambda e: e.tensor_mul(
                                    out=mrow[:, 6:7], in0=mrow[:, 5:6],
                                    in1=lamn[:, 0:1]).then_inc(dve.h, 1))
                                p.wait("vector", dve, dve.inc())
                                p.on("vector", lambda e: e.tensor_scalar_mul(
                                    out=phat0[:], in0=phat0[:], scalar1=mrow[:, 4:5]))
                                p.on("vector", lambda e, d2_=dbuf2:
                                     e.scalar_tensor_tensor(
                                        out=d2_[:], in0=phat1[:], scalar=mrow[:, 6:7],
                                        in1=phat0[:], op0=ALU.mult,
                                        op1=ALU.add).then_inc(dve.h, 1))
                                last_cb = dve.inc()
                                cb_of = {}
                                def emit_tp(qt_p, cbm):
                                    nonlocal last_tp, last_tp_ev
                                    p.wait("tensor", dve, cbm)
                                    if last_tp_ev is not None:
                                        p.wait("tensor", last_tp_ev[0],
                                               last_tp_ev[1])
                                    if qt_p == 0 and pv_prev is not None:
                                        p.wait("tensor", pe, pv_prev)
                                    d2p = dbuf2L[qt_p % 2]
                                    for kt in range(16):
                                        last = (kt == 15)
                                        p.on("tensor", lambda e, kt_=kt, d2_=d2p,
                                             inc=(pe if last else None):
                                             _inc(e.transpose(
                                                out=Tp[:, kt_ * 128:(kt_ + 1) * 128],
                                                in_=d2_[:, kt_ * 128:(kt_ + 1) * 128],
                                                identity=cstt["ident_b"][:, :]), inc))
                                    last_tp = pe.inc()
                                    p.wait("scalar", pe, last_tp)
                                    dTr = diffT.rearrange("p (k q) -> p k q", q=512)
                                    p.on("scalar", lambda e, qt_=qt_p, dTr_=dTr:
                                         e.activation(
                                            out=dTr_[:, :, qt_ * 128:(qt_ + 1) * 128],
                                            in_=Tp.rearrange("p (k q) -> p k q",
                                                             q=128)[:, :, :],
                                            func=AF.Copy).then_inc(act.h, 1))
                                    last_tp_ev = (act, act.inc())
                                if pend is not None:
                                    emit_tp(pend[0], pend[1])
                                pend = (qt, last_cb)
                            emit_tp(pend[0], pend[1])
                            p.wait("tensor", last_tp_ev[0], last_tp_ev[1])
                            if ao_prev is not None:
                                p.wait("tensor", dve, ao_prev)
                            p.wait("tensor", dve, vtm_ok)
                            for kt in range(16):
                                last = (kt == 15)
                                p.on("tensor", lambda e, kt_=kt,
                                     inc=(pe if last else None): _inc(e.matmul(
                                        out=Po[:, 0:512],
                                        lhsT=Vtm[:, kt_ * 128:(kt_ + 1) * 128],
                                        rhs=diffT[:, kt_ * 512:(kt_ + 1) * 512],
                                        start=(kt_ == 0), stop=(kt_ == 15),
                                        skip_group_check=True), inc))
                            pv_prev = pe.inc()
                            p.wait("scalar", pe, pv_prev)
                            p.on("scalar", lambda e: e.activation(
                                out=sqf[:], in_=Po[:, 0:512],
                                func=AF.Square).then_inc(act.h, 1))
                            sqv = act.inc()
                            p.wait("tensor", act, sqv)
                            p.on("tensor", lambda e: e.matmul(
                                out=Pr[0:1, 0:512], lhsT=cstt["ones128_f"][:, 0:1],
                                rhs=sqf[:], start=True, stop=True,
                                skip_group_check=True).then_inc(pe.h, 1))
                            msv = pe.inc()
                            p.wait("vector", pe, msv)
                            p.on("vector", lambda e: e.tensor_scalar(
                                out=rsrow[0:1, 0:512], in0=Pr[0:1, 0:512],
                                scalar1=1.0 / 128.0, scalar2=1e-5, op0=ALU.mult,
                                op1=ALU.add).then_inc(dve.h, 1))
                            rv = dve.inc()
                            p.wait("scalar", dve, rv)
                            p.on("scalar", lambda e: e.activation(
                                out=sqf[0:1, 0:512], in_=rsrow[0:1, 0:512],
                                func=AF.Sqrt).then_inc(act.h, 1))
                            sqv2 = act.inc()
                            p.wait("vector", act, sqv2)
                            p.on("vector", lambda e: e.reciprocal(
                                out=rsrow[0:1, 0:512],
                                in_=sqf[0:1, 0:512]).then_inc(dve.h, 1))
                            rv2 = dve.inc()
                            p.wait("tensor", dve, rv2)
                            p.on("tensor", lambda e: e.matmul(
                                out=Pr[:, 0:512], lhsT=cstt["ones1_f"][0:1, :],
                                rhs=rsrow[0:1, 0:512], start=True, stop=True,
                                skip_group_check=True).then_inc(pe.h, 1))
                            bcv = pe.inc()
                            p.wait("vector", pe, bcv)
                            p.on("vector", lambda e: e.tensor_copy(
                                out=rs512[:], in_=Pr[:, 0:512]).then_inc(dve.h, 1))
                            p.wait("vector", dve, dve.inc())
                            p.on("vector", lambda e, vh_=vh, qg_=qg: e.tensor_mul(
                                out=attn_sb[:, vh_ * T + qg_ * 512:
                                            vh_ * T + (qg_ + 1) * 512],
                                in0=Po[:, 0:512], in1=rs512[:]).then_inc(dve.h, 1))
                            ao_prev = dve.inc()
                        kv_free = pe.n
                    attn_done = dve.n
                    omm = {}
                    y_mark = {}
                    p.wait("scalar", ldc, ldc.n)
                    for oc in range(NCH):
                        par = oc % 2
                        pd = S4[:, par * T:(par + 1) * T]
                        wob = wob2[par]
                        for hf in range(2):
                            stream_conv(
                                wo0_s[oc, :, hf * 8:(hf + 1) * 8, :].rearrange(
                                    "p a b -> p (a b)"), 1024, wob,
                                slice(hf * 1024, (hf + 1) * 1024),
                                omm.get(oc - 2), scale=sub_t[:, 0:1])
                        p.wait("tensor", act, act.n)
                        if oc == 0:
                            p.wait("tensor", dve, attn_done)
                        if oc >= 2:
                            p.wait("tensor", dve, y_mark[oc - 2])
                        for ti in range(2):
                            sl = slice(ti * 512, (ti + 1) * 512)
                            for ic in range(NCH):
                                last = (ti == 1 and ic == NCH - 1)
                                p.on("tensor", lambda e, ic_=ic, sl_=sl, pd_=pd,
                                     wb_=wob,
                                     inc=(pe if last else None): _inc(e.matmul(
                                        out=pd_[:, sl_],
                                        lhsT=wb_[:, ic_ * 128:(ic_ + 1) * 128],
                                        rhs=attn_sb[:, ic_ * T + sl_.start:
                                                    ic_ * T + sl_.stop],
                                        start=(ic_ == 0), stop=(ic_ == NCH - 1),
                                        skip_group_check=True), inc))
                        omm[oc] = pe.inc()
                        p.wait("vector", pe, omm[oc])
                        p.on("vector", lambda e, oc_=oc, pd_=pd: e.tensor_add(
                            out=x_fm[oc_][:, 0:T], in0=x_fm[oc_][:, 0:T],
                            in1=pd_[:, 0:T]).then_inc(dve.h, 1))
                        y_mark[oc] = dve.inc()
                barrier()

        def moe_call():
            with ExitStack() as ph:
                E1 = ph.enter_context
                xln_all = E1(sbuf("m_xln", [128, NCH * T], BF16))
                xln = [xln_all[:, i * T:(i + 1) * T] for i in range(NCH)]
                hid_all = E1(sbuf("m_hid", [128, NF1 * T], BF16))
                hid = [hid_all[:, i * T:(i + 1) * T] for i in range(NF1)]
                xsq = E1(sbuf("m_xsq", [128, T], F32))
                rows = E1(sbuf("m_rows", [1, 3 * T], F32))
                mu_sb = E1(sbuf("m_mu", [128, T], F32))
                rs_sb = E1(sbuf("m_rs", [128, T], F32))
                xsum = E1(sbuf("m_xsum", [128, NCH], F32))
                gw_t = E1(sbuf("m_gw", [128, NCH * 6], F32))
                b1_t = E1(sbuf("m_b1", [128, 6 * NF1], F32))
                b2_t = E1(sbuf("m_b2", [6, NCH * 128], F32))
                b2_b = E1(sbuf("m_b2b", [6, NCH * 128], BF16))
                b2w5 = E1(sbuf("m_b2w", [128, NCH], F32))
                grow = E1(sbuf("m_grow", [1, 64], F32))
                iota_f = E1(sbuf("m_iota", [128, 54], F32))
                idx_f = E1(sbuf("m_idxf", [128, 64], F32))
                idx_i = E1(sbuf("m_idxi", [128, 64], I32))
                base2 = E1(sbuf("m_base2", [128, 2], F32))
                b1r6 = E1(sbuf("m_b1r6", [6, 1408], F32))
                b1r6b = E1(sbuf("m_b1r6b", [128, 1408], BF16))
                eqc = E1(sbuf("m_eqc", [6, 2], BF16))
                eqrow_b = E1(sbuf("m_eqrb", [1, 12], BF16))
                b1sel = E1(sbuf("m_b1sel", [128, 2 * NF1], F32))
                wcol2 = E1(sbuf("m_wcol2", [128, 2], F32))
                wrow_b = E1(sbuf("m_wrow", [1, 6], BF16))
                w6col_b = E1(sbuf("m_w6c", [6, 1], BF16))
                wcol_sb = E1(sbuf("m_wcol", [128, 6], F32))
                w1b2 = [E1(sbuf("m_w1b", [128, 2048], BF16)) for _ in range(2)]
                w2b2 = [E1(sbuf("m_w2b", [128, 1408], BF16)) for _ in range(2)]
                sgb = E1(sbuf("m_sgb", [128, T], F32))
                PA = E1(psum("m_PA", [128, 2048], F32))
                Pr = E1(psum("m_Pr", [128, 1024], F32))
                barrier()
                for dst, srcp in [
                        (gw_t[:].rearrange("p (a b) -> p a b", b=6),
                         gw_d.transpose([1, 0, 2])),
                        (b1_t[:], b1_d),
                        (iota_f[:], iota_d),
                        (b1r6[:], b1r_d),
                        (b2_t[:].rearrange("p (a q) -> p a q", q=128),
                         b2_d.transpose([1, 0, 2]))]:
                    p.on("gpsimd", lambda e, d=dst, sr=srcp: e.dma_start(
                        out=d, in_=sr).then_inc(ldc.h, 16))
                    ldc.inc(16)
                for eng in ("tensor", "scalar", "vector"):
                    p.wait(eng, ldc, ldc.n)
                p.on("vector", lambda e: e.memset(b1r6b[:], 0.0).then_inc(dve.h, 1))
                p.wait("vector", dve, dve.inc())
                p.on("vector", lambda e: e.tensor_copy(
                    out=b1r6b[0:6, :], in_=b1r6[:]))
                p.on("vector", lambda e: e.tensor_copy(
                    out=b2_b[:], in_=b2_t[:]).then_inc(dve.h, 1))
                dve.inc()
                xln_ok = ln_res(dict(Pbig=PA, rows=rows, xsq=xsq,
                                     mu_sb=mu_sb, rs_sb=rs_sb), xln, 0, T)
                p.wait("vector", dve, xln_ok)
                for dc in range(NCH):
                    last = (dc == NCH - 1)
                    p.on("vector", lambda e, dc_=dc, inc=(dve if last else None):
                         _inc(e.reduce_sum(out=xsum[:, dc_:dc_ + 1],
                                           in_=xln[dc_][:, 0:T], axis=AX.X), inc))
                xs_ok = dve.inc()
                p.wait("gpsimd", dve, xs_ok)
                p.on("gpsimd", lambda e: e.dma_start(
                    out=ccG_in, in_=xsum[:]).then_inc(ca.h, 16))
                ca.inc(16)
                p.wait("gpsimd", ca, ca.n)
                p.on("gpsimd", lambda e: e.collective_compute(
                    "AllReduce", ALU.add, replica_groups=groups,
                    ins=[ccG_in.opt()], outs=[ccG_out.opt()]).then_inc(cc.h, 1))
                cc.inc()
                p.wait("gpsimd", cc, cc.n)
                p.on("gpsimd", lambda e: e.dma_start(
                    out=xsum[:], in_=ccG_out).then_inc(cb.h, 16))
                cb.inc(16)
                p.wait("vector", cb, cb.n)
                p.on("vector", lambda e: e.tensor_scalar_mul(
                    out=xsum[:], in0=xsum[:],
                    scalar1=1.0 / 2048.0).then_inc(dve.h, 1))
                gm = dve.inc()
                p.wait("tensor", dve, gm)
                for dc in range(NCH):
                    last = (dc == NCH - 1)
                    p.on("tensor", lambda e, dc_=dc, inc=(pe if last else None):
                         _inc(e.matmul(out=Pr[0:1, 0:6],
                                       lhsT=xsum[:, dc_:dc_ + 1],
                                       rhs=gw_t[:, dc_ * 6:(dc_ + 1) * 6],
                                       start=(dc_ == 0), stop=(dc_ == NCH - 1),
                                       skip_group_check=True), inc))
                lg = pe.inc()
                p.wait("vector", pe, lg)
                p.on("vector", lambda e: e.tensor_copy(
                    out=grow[0:1, 0:6], in_=Pr[0:1, 0:6]).then_inc(dve.h, 1))
                p.wait("vector", dve, dve.inc())
                p.on("vector", lambda e: e.reduce_max(
                    out=grow[0:1, 8:9], in_=grow[0:1, 0:6], axis=AX.X,
                    negate=True).then_inc(dve.h, 1))
                mxv = dve.inc()
                p.wait("scalar", dve, mxv)
                p.on("scalar", lambda e: e.activation(
                    out=grow[0:1, 16:22], in_=grow[0:1, 0:6], func=AF.Exp,
                    bias=grow[0:1, 8:9]).then_inc(act.h, 1))
                exv = act.inc()
                p.wait("vector", act, exv)
                p.on("vector", lambda e: e.reduce_max(
                    out=grow[0:1, 9:10], in_=grow[0:1, 16:22],
                    axis=AX.X).then_inc(dve.h, 1))
                p.wait("vector", dve, dve.inc())
                p.on("vector", lambda e: e.tensor_single_scalar(
                    out=grow[0:1, 24:30], in_=grow[0:1, 16:22],
                    scalar=grow[0:1, 9:10], op=ALU.is_ge).then_inc(dve.h, 1))
                p.wait("vector", dve, dve.inc())
                p.on("vector", lambda e: e.scalar_tensor_tensor(
                    out=grow[0:1, 32:38], in0=grow[0:1, 24:30], scalar=-1e4,
                    in1=grow[0:1, 16:22], op0=ALU.mult,
                    op1=ALU.add).then_inc(dve.h, 1))
                p.wait("vector", dve, dve.inc())
                p.on("vector", lambda e: e.reduce_max(
                    out=grow[0:1, 10:11], in_=grow[0:1, 32:38],
                    axis=AX.X).then_inc(dve.h, 1))
                p.wait("vector", dve, dve.inc())
                p.on("vector", lambda e: e.tensor_single_scalar(
                    out=grow[0:1, 40:46], in_=grow[0:1, 16:22],
                    scalar=grow[0:1, 10:11], op=ALU.is_ge).then_inc(dve.h, 1))
                p.wait("vector", dve, dve.inc())
                p.on("vector", lambda e: e.tensor_add(
                    out=grow[0:1, 11:12], in0=grow[0:1, 9:10],
                    in1=grow[0:1, 10:11]).then_inc(dve.h, 1))
                p.wait("vector", dve, dve.inc())
                p.on("vector", lambda e: e.reciprocal(
                    out=grow[0:1, 12:13], in_=grow[0:1, 11:12]).then_inc(dve.h, 1))
                p.wait("vector", dve, dve.inc())
                p.on("vector", lambda e: e.tensor_mul(
                    out=grow[0:1, 48:54], in0=grow[0:1, 16:22],
                    in1=grow[0:1, 40:46]).then_inc(dve.h, 1))
                p.wait("vector", dve, dve.inc())
                p.on("vector", lambda e: e.tensor_single_scalar(
                    out=grow[0:1, 56:62], in_=grow[0:1, 48:54],
                    scalar=grow[0:1, 12:13], op=ALU.mult).then_inc(dve.h, 1))
                p.wait("vector", dve, dve.inc())
                p.on("vector", lambda e: e.tensor_copy(
                    out=wrow_b[0:1, 0:6], in_=grow[0:1, 56:62]).then_inc(dve.h, 1))
                p.wait("vector", dve, dve.inc())
                p.on("vector", lambda e: e.tensor_single_scalar(
                    out=grow[0:1, 32:38], in_=grow[0:1, 32:38],
                    scalar=grow[0:1, 10:11], op=ALU.is_ge).then_inc(dve.h, 1))
                p.wait("vector", dve, dve.inc())
                p.on("vector", lambda e: e.tensor_copy(
                    out=eqrow_b[0:1, 0:6], in_=grow[0:1, 24:30]))
                p.on("vector", lambda e: e.tensor_copy(
                    out=eqrow_b[0:1, 6:12], in_=grow[0:1, 32:38]).then_inc(dve.h, 1))
                p.wait("vector", dve, dve.inc())
                for k in range(2):
                    eqs = slice(24, 30) if k == 0 else slice(32, 38)
                    p.on("vector", lambda e, k_=k, eqs_=eqs: e.tensor_mul(
                        out=grow[0:1, 48 + 8 + k_ * 6:62 + k_ * 6] if False else
                        grow[0:1, 0:6], in0=grow[0:1, 56:62],
                        in1=grow[0:1, eqs_]).then_inc(dve.h, 1))
                    p.wait("vector", dve, dve.inc())
                    p.on("vector", lambda e, k_=k: e.reduce_sum(
                        out=grow[0:1, 13 + k_:14 + k_], in_=grow[0:1, 0:6],
                        axis=AX.X).then_inc(dve.h, 1))
                    p.wait("vector", dve, dve.inc())
                wrow_ok = dve.n
                p.wait("tensor", dve, wrow_ok)
                p.on("tensor", lambda e: e.matmul(
                    out=Pr[:, 8:14], lhsT=cstt["ones1_f"][0:1, :],
                    rhs=grow[0:1, 56:62], start=True, stop=True,
                    skip_group_check=True))
                p.on("tensor", lambda e: e.matmul(
                    out=Pr[0:6, 16:17], lhsT=wrow_b[0:1, 0:6],
                    rhs=ones11b[0:1, 0:1], start=True, stop=True,
                    skip_group_check=True).then_inc(pe.h, 1))
                bc1 = pe.inc()
                p.wait("vector", pe, bc1)
                p.on("vector", lambda e: e.tensor_copy(
                    out=wcol_sb[:], in_=Pr[:, 8:14]))
                p.on("vector", lambda e: e.tensor_copy(
                    out=w6col_b[:], in_=Pr[0:6, 16:17]).then_inc(dve.h, 1))
                p.wait("vector", dve, dve.inc())
                w6_ok = dve.n
                p.wait("tensor", dve, w6_ok)
                for dc in range(NCH):
                    last = (dc == NCH - 1)
                    p.on("tensor", lambda e, dc_=dc, inc=(pe if last else None):
                         _inc(e.matmul(out=Pr[:, 32 + dc_:33 + dc_],
                                       lhsT=b2_b[0:6, dc_ * 128:(dc_ + 1) * 128],
                                       rhs=w6col_b[0:6, 0:1], start=True,
                                       stop=True, skip_group_check=True), inc))
                b2wp = pe.inc()
                p.wait("vector", pe, b2wp)
                p.on("vector", lambda e: e.tensor_scalar_mul(
                    out=b2w5[:], in0=Pr[:, 32:48], scalar1=0.5).then_inc(dve.h, 1))
                b2w_ok = dve.inc()
                p.wait("vector", dve, b2w_ok)
                p.on("vector", lambda e: e.memset(grow[0:1, 0:6], 0.0).then_inc(dve.h, 1))
                p.wait("vector", dve, dve.inc())
                for j in range(6):
                    p.on("vector", lambda e, j_=j,
                         inc=(dve if j == 5 else None): _inc(e.memset(
                            grow[0:1, j_:j_ + 1], float(j_)), inc))
                p.wait("vector", dve, dve.inc())
                for k in range(2):
                    eqs = slice(24, 30) if k == 0 else slice(32, 38)
                    p.on("vector", lambda e, eqs_=eqs: e.tensor_mul(
                        out=grow[0:1, 40:46], in0=grow[0:1, 0:6],
                        in1=grow[0:1, eqs_]).then_inc(dve.h, 1))
                    p.wait("vector", dve, dve.inc())
                    p.on("vector", lambda e, k_=k: e.reduce_sum(
                        out=grow[0:1, 15 + k_:16 + k_], in_=grow[0:1, 40:46],
                        axis=AX.X).then_inc(dve.h, 1))
                    p.wait("vector", dve, dve.inc())
                ebase_ok = dve.n
                p.wait("tensor", dve, ebase_ok)
                for k in range(2):
                    p.on("tensor", lambda e, k_=k: e.matmul(
                        out=Pr[:, 50 + k_:51 + k_], lhsT=cstt["ones1_f"][0:1, :],
                        rhs=grow[0:1, 15 + k_:16 + k_], start=True, stop=True,
                        skip_group_check=True))
                    p.on("tensor", lambda e, k_=k,
                         inc=(pe if k == 1 else None): _inc(e.matmul(
                            out=Pr[:, 52 + k_:53 + k_], lhsT=cstt["ones1_f"][0:1, :],
                            rhs=grow[0:1, 13 + k_:14 + k_], start=True, stop=True,
                            skip_group_check=True), inc))
                ebc = pe.inc()
                p.wait("vector", pe, ebc)
                p.on("vector", lambda e: e.tensor_copy(
                    out=base2[:, 0:2], in_=Pr[:, 50:52]))
                p.on("vector", lambda e: e.tensor_copy(
                    out=wcol2[:, 0:2], in_=Pr[:, 52:54]).then_inc(dve.h, 1))
                p.wait("vector", dve, dve.inc())
                for k in range(2):
                    for fc in range(NF1):
                        col = k * NF1 + fc
                        p.on("vector", lambda e, k_=k, col_=col:
                             e.scalar_tensor_tensor(
                                out=idx_f[:, col_:col_ + 1], in0=base2[:, k_:k_ + 1],
                                scalar=float(NF1 * 128), in1=iota_f[:, col_:col_ + 1],
                                op0=ALU.mult, op1=ALU.add))
                for k in range(2):
                    for dc in range(NCH):
                        col = 22 + k * NCH + dc
                        last = (k == 1 and dc == NCH - 1)
                        p.on("vector", lambda e, k_=k, col_=col,
                             inc=(dve if last else None): _inc(
                                e.scalar_tensor_tensor(
                                    out=idx_f[:, col_:col_ + 1],
                                    in0=base2[:, k_:k_ + 1],
                                    scalar=float(NCH * 128),
                                    in1=iota_f[:, col_:col_ + 1],
                                    op0=ALU.mult, op1=ALU.add), inc))
                p.wait("vector", dve, dve.inc())
                p.on("vector", lambda e: e.tensor_copy(
                    out=idx_i[:, 0:54], in_=idx_f[:, 0:54]).then_inc(dve.h, 1))
                p.wait("vector", dve, dve.inc())
                idx_ok = dve.n
                p.wait("tensor", dve, idx_ok)
                for k in range(2):
                    p.on("tensor", lambda e, k_=k,
                         inc=(pe if k == 1 else None): _inc(e.matmul(
                            out=Pr[0:6, 56 + k_:57 + k_],
                            lhsT=eqrow_b[0:1, k_ * 6:(k_ + 1) * 6],
                            rhs=ones11b[0:1, 0:1], start=True, stop=True,
                            skip_group_check=True), inc))
                eqc_pe = pe.inc()
                p.wait("vector", pe, eqc_pe)
                p.on("vector", lambda e: e.tensor_copy(
                    out=eqc[0:6, 0:2], in_=Pr[0:6, 56:58]).then_inc(dve.h, 1))
                p.wait("vector", dve, dve.inc())
                eqc_ok = dve.n
                p.wait("tensor", dve, eqc_ok)
                for k in range(2):
                    for fc in range(NF1):
                        last = (k == 1 and fc == NF1 - 1)
                        p.on("tensor", lambda e, k_=k, fc_=fc,
                             inc=(pe if last else None): _inc(e.matmul(
                                out=Pr[:, 60 + k_ * NF1 + fc_:61 + k_ * NF1 + fc_],
                                lhsT=b1r6b[0:6, fc_ * 128:(fc_ + 1) * 128],
                                rhs=eqc[0:6, k_:k_ + 1], start=True, stop=True,
                                skip_group_check=True), inc))
                b1s_pe = pe.inc()
                p.wait("vector", pe, b1s_pe)
                p.on("vector", lambda e: e.tensor_copy(
                    out=b1sel[:, 0:2 * NF1], in_=Pr[:, 60:60 + 2 * NF1]).then_inc(dve.h, 1))
                sel_ok = dve.inc()

                w1rows = w1.rearrange("a b p c d -> (a b p) (c d)")
                w2rows = w2.rearrange("a b p c d -> (a b p) (c d)")
                prev_pe = None
                prev_pe2 = None
                hid_gelu = {}
                y_mark = {}
                gidx = [0]

                def stream_gather(rows_view, ncols, idxcol, dst, pe_free):
                    nh = ncols // 2
                    for hf in range(2):
                        i = gqw[0]
                        sb = i % 2
                        stg = stg_f32[sb]
                        if i >= 2:
                            p.wait("gpsimd", act, wconv_mark[i - 2])
                        p.on("gpsimd", lambda e, rv_=rows_view, n_=nh, s_=stg,
                             hf_=hf, ic_=idxcol: e.indirect_dma_start(
                                out=s_[:, 0:n_], out_offset=None,
                                in_=rv_,
                                in_offset=IndirectOffsetOnAxis(ap=ic_, axis=0),
                                element_offset=hf_ * n_,
                             ).then_inc(lwg.h, 16))
                        lv = lwg.inc(16)
                        p.wait("scalar", lwg, lv)
                        if pe_free is not None and hf == 0:
                            p.wait("scalar", pe, pe_free)
                        p.on("scalar", lambda e, s_=stg, n_=nh, d_=dst,
                             hf_=hf: e.activation(
                            out=d_[:, hf_ * n_:(hf_ + 1) * n_], in_=s_[:, 0:n_],
                            func=AF.Copy).then_inc(act.h, 1))
                        wconv_mark[i] = act.inc()
                        gqw[0] += 1
                    return act.n

                p.wait("gpsimd", dve, idx_ok)
                for k in range(2):
                    for fc in range(NF1):
                        par = fc % 2
                        phs = PA[:, par * T:(par + 1) * T]
                        w1b = w1b2[par]
                        wready = stream_gather(
                            w1rows, 2048, idx_i[:, k * NF1 + fc:k * NF1 + fc + 1],
                            w1b, prev_pe2)
                        p.wait("tensor", act, wready)
                        if k == 0 and fc == 0:
                            p.wait("tensor", dve, xln_ok)
                            p.wait("tensor", dve, b2w_ok)
                            p.wait("tensor", dve, sel_ok)
                            p.wait("tensor", dve, idx_ok)
                        for ti in range(2):
                            sl = slice(ti * 512, (ti + 1) * 512)
                            for dc in range(NCH):
                                last = (ti == 1 and dc == NCH - 1)
                                p.on("tensor", lambda e, dc_=dc, sl_=sl, ph_=phs,
                                     wb_=w1b,
                                     inc=(pe if last else None): _inc(e.matmul(
                                        out=ph_[:, sl_],
                                        lhsT=wb_[:, dc_ * 128:(dc_ + 1) * 128],
                                        rhs=xln[dc_][:, sl_], start=(dc_ == 0),
                                        stop=(dc_ == NCH - 1),
                                        skip_group_check=True), inc))
                        prev_pe2 = prev_pe
                        prev_pe = pe.inc()
                        p.wait("scalar", pe, prev_pe)
                        if (k * NF1 + fc - 1) in hid_gelu:
                            p.wait("scalar", dve, hid_gelu[k * NF1 + fc - 1])
                        p.on("scalar", lambda e, k_=k, fc_=fc, ph_=phs:
                             e.activation(
                                out=sgb[:], in_=ph_[:, 0:T], func=AF.Gelu,
                                bias=b1sel[:, k_ * NF1 + fc_:
                                           k_ * NF1 + fc_ + 1]).then_inc(act.h, 1))
                        gv = act.inc()
                        p.wait("vector", act, gv)
                        p.on("vector", lambda e, k_=k, fc_=fc: e.tensor_scalar_mul(
                            out=hid[fc_][:, 0:T], in0=sgb[:],
                            scalar1=wcol2[:, k_:k_ + 1]).then_inc(dve.h, 1))
                        hid_gelu[k * NF1 + fc] = dve.inc()
                    hid_done = dve.n
                    for dc in range(NCH):
                        par = dc % 2
                        pd = PA[:, par * T:(par + 1) * T]
                        w2b = w2b2[par]
                        wready = stream_gather(
                            w2rows, 1408, idx_i[:, 22 + k * NCH + dc:
                                                23 + k * NCH + dc],
                            w2b, prev_pe2)
                        p.wait("tensor", act, wready)
                        if dc == 0:
                            p.wait("tensor", dve, hid_done)
                        key = (k, dc - 2) if dc >= 2 else (k - 1, NCH - 2 + dc)
                        if key in y_mark:
                            p.wait("tensor", dve, y_mark[key])
                        for ti in range(2):
                            sl = slice(ti * 512, (ti + 1) * 512)
                            for hc in range(NF1):
                                last = (ti == 1 and hc == NF1 - 1)
                                p.on("tensor", lambda e, hc_=hc, sl_=sl, pd_=pd,
                                     wb_=w2b,
                                     inc=(pe if last else None): _inc(e.matmul(
                                        out=pd_[:, sl_],
                                        lhsT=wb_[:, hc_ * 128:(hc_ + 1) * 128],
                                        rhs=hid[hc_][:, sl_], start=(hc_ == 0),
                                        stop=(hc_ == NF1 - 1),
                                        skip_group_check=True), inc))
                        prev_pe2 = prev_pe
                        prev_pe = pe.inc()
                        p.wait("vector", pe, prev_pe)
                        p.on("vector", lambda e, dc_=dc, pd_=pd:
                             e.scalar_tensor_tensor(
                                out=x_fm[dc_][:, 0:T], in0=pd_[:, 0:T], scalar=0.5,
                                in1=x_fm[dc_][:, 0:T], op0=ALU.mult,
                                op1=ALU.add).then_inc(dve.h, 1))
                        y_mark[(k, dc)] = dve.inc()
                for dc in range(NCH):
                    p.on("vector", lambda e, dc_=dc,
                         inc=(dve if dc == NCH - 1 else None): _inc(
                            e.tensor_scalar_add(
                                out=x_fm[dc_][:, 0:T], in0=x_fm[dc_][:, 0:T],
                                scalar1=b2w5[:, dc_:dc_ + 1]), inc))
                dve.inc()
                barrier()

        def l1attn_call():
            with ExitStack() as ph:
                E1 = ph.enter_context
                qm_all = E1(sbuf("l_qm", [64, NCH * T], BF16))
                bo_t = E1(sbuf("l_bo", [128, NCH], F32))
                kvpack = E1(sbuf("l_kvp", [128, 192], F32))
                S4 = E1(psum("l_S4", [128, 2048], F32))
                Pr = E1(psum("l_Pr", [128, 512], F32))
                Tp = E1(psum("l_Tp", [128, 1024], BF16))
                phA = ExitStack()
                E1A = phA.enter_context
                xln_all = E1A(sbuf("l_xln", [128, NCH * T], BF16))
                xln = [xln_all[:, i * T:(i + 1) * T] for i in range(NCH)]
                xsq = E1A(sbuf("l_xsq", [128, T], F32))
                rows = E1A(sbuf("l_rows", [1, 3 * T], F32))
                mu_sb = E1A(sbuf("l_mu", [128, T], F32))
                rs_sb = E1A(sbuf("l_rs", [128, T], F32))
                bq_t = E1A(sbuf("l_bq", [128, NCH], F32))
                bk_t = E1A(sbuf("l_bk", [128, NCH], F32))
                bv_t = E1A(sbuf("l_bv", [128, NCH], F32))
                wmq_f = E1A(sbuf("l_wmqf", [128, 64], F32))
                wmk_f = E1A(sbuf("l_wmkf", [128, 64], F32))
                wmq_b = E1A(sbuf("l_wmqb", [128, 64], BF16))
                wmk_b = E1A(sbuf("l_wmkb", [128, 64], BF16))
                kf_bf = E1A(sbuf("l_kf", [128, T], BF16))
                vf_bf = E1A(sbuf("l_vf", [128, T], BF16))
                v_tm = E1A(sbuf("l_vtm", [128, T], BF16))
                qf_bf = E1A(sbuf("l_qf", [128, T], BF16))
                kmt = E1A(sbuf("l_kmt", [128, 8 * 64], BF16))
                e1t = E1A(sbuf("l_e1", [128, 512], F32))
                e2t = E1A(sbuf("l_e2", [128, 512], F32))
                wpbL = [E1A(sbuf("l_wpb", [128, 2048], BF16)) for _ in range(2)]
                wslab = [0]
                barrier()
                for dst, srcp in [(bq_t, bq_d), (bk_t, bk_d), (bv_t, bv_d),
                                  (bo_t, bo_d), (wmq_f, wmq_d), (wmk_f, wmk_d)]:
                    p.on("gpsimd", lambda e, d=dst, sr=srcp: e.dma_start(
                        out=d[:], in_=sr).then_inc(ldc.h, 16))
                    ldc.inc(16)
                for eng in ("tensor", "scalar", "vector"):
                    p.wait(eng, ldc, ldc.n)
                p.on("vector", lambda e: e.tensor_copy(out=wmq_b[:], in_=wmq_f[:]))
                p.on("vector", lambda e: e.memset(kvpack[64:65, 64:128], 0.0))
                p.on("vector", lambda e: e.tensor_copy(
                    out=wmk_b[:], in_=wmk_f[:]).then_inc(dve.h, 1))
                wm_ok = dve.inc()
                xln_ok = ln_res(dict(Pbig=S4, rows=rows, xsq=xsq,
                                     mu_sb=mu_sb, rs_sb=rs_sb), xln, 0, T)

                def elu1(src, dst, fd, pb, rng):
                    p.on("vector", lambda e, s_=src: e.tensor_scalar_min(
                        out=e1t[pb:pb + rng, 0:fd], in0=s_,
                        scalar1=0.0).then_inc(dve.h, 1))
                    dv = dve.inc()
                    p.wait("scalar", dve, dv)
                    p.on("scalar", lambda e: e.activation(
                        out=e2t[pb:pb + rng, 0:fd], in_=e1t[pb:pb + rng, 0:fd],
                        func=AF.Exp).then_inc(act.h, 1))
                    av = act.inc()
                    p.wait("vector", act, av)
                    p.on("vector", lambda e, s_=src: e.tensor_scalar_max(
                        out=e1t[pb:pb + rng, 0:fd], in0=s_,
                        scalar1=0.0).then_inc(dve.h, 1))
                    p.wait("vector", dve, dve.inc())
                    p.on("vector", lambda e, d_=dst: e.tensor_add(
                        out=d_, in0=e2t[pb:pb + rng, 0:fd],
                        in1=e1t[pb:pb + rng, 0:fd]).then_inc(dve.h, 1))
                    return dve.inc()

                prev_pe = None
                last_pack = None
                for h in range(NCH):
                    wpb = wpbL[wslab[0] % 2]
                    wslab[0] += 1
                    for hf in range(2):
                        stream_conv(
                            wk1_s[h, :, hf * 8:(hf + 1) * 8, :].rearrange(
                                "p a b -> p (a b)"), 1024, wpb,
                            slice(hf * 1024, (hf + 1) * 1024), prev_pe, dual_q=True)
                    p.wait("tensor", act, act.n)
                    if h == 0:
                        p.wait("tensor", dve, xln_ok)
                        p.wait("tensor", dve, wm_ok)
                    if last_pack is not None:
                        p.wait("tensor", dve, last_pack)
                    for ti in range(2):
                        sl = slice(ti * 512, (ti + 1) * 512)
                        for dc in range(NCH):
                            last = (ti == 1 and dc == NCH - 1)
                            p.on("tensor", lambda e, dc_=dc, sl_=sl, wb_=wpb,
                                 inc=(pe if last else None): _inc(e.matmul(
                                    out=S4[:, sl_],
                                    lhsT=wb_[:, dc_ * 128:(dc_ + 1) * 128],
                                    rhs=xln[dc_][:, sl_], start=(dc_ == 0),
                                    stop=(dc_ == NCH - 1),
                                    skip_group_check=True), inc))
                    kp = pe.inc()
                    p.wait("vector", pe, kp)
                    p.on("vector", lambda e, h_=h: e.tensor_scalar_add(
                        out=kf_bf[:], in0=S4[:, 0:1024],
                        scalar1=bk_t[:, h_:h_ + 1]).then_inc(dve.h, 1))
                    kev = dve.inc()
                    wpb = wpbL[wslab[0] % 2]
                    wslab[0] += 1
                    for hf in range(2):
                        stream_conv(
                            wv1_s[h, :, hf * 8:(hf + 1) * 8, :].rearrange(
                                "p a b -> p (a b)"), 1024, wpb,
                            slice(hf * 1024, (hf + 1) * 1024), kp, dual_q=True)
                    p.wait("tensor", act, act.n)
                    for ti in range(2):
                        sl = slice(1024 + ti * 512, 1024 + (ti + 1) * 512)
                        for dc in range(NCH):
                            last = (ti == 1 and dc == NCH - 1)
                            p.on("tensor", lambda e, dc_=dc, sl_=sl, wb_=wpb,
                                 inc=(pe if last else None): _inc(e.matmul(
                                    out=S4[:, sl_],
                                    lhsT=wb_[:, dc_ * 128:(dc_ + 1) * 128],
                                    rhs=xln[dc_][:, sl_.start - 1024:sl_.stop - 1024],
                                    start=(dc_ == 0), stop=(dc_ == NCH - 1),
                                    skip_group_check=True), inc))
                    vp = pe.inc()
                    p.wait("vector", pe, vp)
                    p.on("vector", lambda e, h_=h: e.tensor_scalar_add(
                        out=vf_bf[:], in0=S4[:, 1024:2048],
                        scalar1=bv_t[:, h_:h_ + 1]).then_inc(dve.h, 1))
                    vev = dve.inc()
                    p.wait("tensor", dve, vev)
                    for tc in range(8):
                        last = (tc == 7)
                        p.on("tensor", lambda e, tc_=tc,
                             inc=(pe if last else None): _inc(e.transpose(
                                out=Tp[:, tc_ * 128:(tc_ + 1) * 128],
                                in_=vf_bf[:, tc_ * 128:(tc_ + 1) * 128],
                                identity=cstt["ident_b"][:, :]), inc))
                    vt = pe.inc()
                    p.wait("vector", pe, vt)
                    p.on("vector", lambda e: e.tensor_copy(
                        out=v_tm[:], in_=Tp[:, 0:1024]).then_inc(dve.h, 1))
                    vtm_ok = dve.inc()
                    wpb = wpbL[wslab[0] % 2]
                    wslab[0] += 1
                    for hf in range(2):
                        stream_conv(
                            wq1_s[h, :, hf * 8:(hf + 1) * 8, :].rearrange(
                                "p a b -> p (a b)"), 1024, wpb,
                            slice(hf * 1024, (hf + 1) * 1024), vp, dual_q=True)
                    p.wait("tensor", act, act.n)
                    p.wait("tensor", dve, kev)
                    for ti in range(2):
                        sl = slice(ti * 512, (ti + 1) * 512)
                        for dc in range(NCH):
                            last = (ti == 1 and dc == NCH - 1)
                            p.on("tensor", lambda e, dc_=dc, sl_=sl, wb_=wpb,
                                 inc=(pe if last else None): _inc(e.matmul(
                                    out=S4[:, sl_],
                                    lhsT=wb_[:, dc_ * 128:(dc_ + 1) * 128],
                                    rhs=xln[dc_][:, sl_], start=(dc_ == 0),
                                    stop=(dc_ == NCH - 1),
                                    skip_group_check=True), inc))
                    qp = pe.inc()
                    p.wait("vector", pe, qp)
                    p.on("vector", lambda e, h_=h: e.tensor_scalar_add(
                        out=qf_bf[:], in0=S4[:, 0:1024],
                        scalar1=bq_t[:, h_:h_ + 1]).then_inc(dve.h, 1))
                    qev = dve.inc()
                    p.wait("tensor", dve, qev)
                    for ti in range(2):
                        last = (ti == 1)
                        p.on("tensor", lambda e, ti_=ti,
                             inc=(pe if last else None): _inc(e.matmul(
                                out=S4[0:64, 1024 + ti_ * 512:
                                       1024 + (ti_ + 1) * 512],
                                lhsT=wmq_b[:, 0:64],
                                rhs=qf_bf[:, ti_ * 512:(ti_ + 1) * 512],
                                start=True, stop=True,
                                skip_group_check=True), inc))
                    qm = pe.inc()
                    p.wait("vector", pe, qm)
                    for ti in range(2):
                        qm_ok = elu1(S4[0:64, 1024 + ti * 512:1024 + (ti + 1) * 512],
                                     qm_all[0:64, h * T + ti * 512:
                                            h * T + (ti + 1) * 512], 512, 0, 64)
                    km = {}
                    for tc in range(8):
                        slot = (tc % 2) * 512
                        if tc >= 2:
                            p.wait("tensor", dve, km[tc - 2])
                        p.on("tensor", lambda e, tc_=tc, sl_=slot: e.matmul(
                            out=S4[:, sl_:sl_ + 64],
                            lhsT=kf_bf[:, tc_ * 128:(tc_ + 1) * 128],
                            rhs=wmk_b[:, 0:64], start=True, stop=True,
                            skip_group_check=True).then_inc(pe.h, 1))
                        mp = pe.inc()
                        p.wait("vector", pe, mp)
                        km[tc] = elu1(S4[:, slot:slot + 64],
                                      kmt[:, tc * 64:(tc + 1) * 64], 64, 0, 128)
                        p.wait("tensor", dve, km[tc])
                        p.on("tensor", lambda e, tc_=tc: e.matmul(
                            out=S4[0:64, 1024:1152],
                            lhsT=kmt[:, tc_ * 64:(tc_ + 1) * 64],
                            rhs=v_tm[:, tc_ * 128:(tc_ + 1) * 128],
                            start=(tc_ == 0), stop=(tc_ == 7),
                            skip_group_check=True))
                        p.on("tensor", lambda e, tc_=tc,
                             inc=(pe if tc == 7 else None): _inc(e.matmul(
                                out=S4[0:1, 1536:1600],
                                lhsT=cstt["ones128_b"][:, 0:1],
                                rhs=kmt[:, tc_ * 64:(tc_ + 1) * 64],
                                start=(tc_ == 0), stop=(tc_ == 7),
                                skip_group_check=True), inc))
                    kvp = pe.inc()
                    prev_pe = kvp
                    p.wait("vector", pe, kvp)
                    if ca.n > 0:
                        p.wait("vector", ca, ca.n)
                    p.on("vector", lambda e: e.tensor_copy(
                        out=kvpack[0:64, 0:128], in_=S4[0:64, 1024:1152]))
                    p.on("vector", lambda e: e.tensor_copy(
                        out=kvpack[64:65, 0:64],
                        in_=S4[0:1, 1536:1600]).then_inc(dve.h, 1))
                    pk = dve.inc()
                    last_pack = pk
                    p.wait("gpsimd", dve, pk)
                    p.on("gpsimd", lambda e, h_=h: e.dma_start(
                        out=ccR_in[h_ * 64:(h_ + 1) * 64, :],
                        in_=kvpack[0:64, 0:128]).then_inc(ca.h, 16))
                    ca.inc(16)
                    p.on("gpsimd", lambda e, h_=h: e.dma_start(
                        out=ccR_in[1024 + h_:1025 + h_, :],
                        in_=kvpack[64:65, 0:128]).then_inc(ca.h, 16))
                    ca.inc(16)
                p.wait("gpsimd", ca, ca.n)
                p.on("gpsimd", lambda e: e.collective_compute(
                    "AllReduce", ALU.add, replica_groups=groups,
                    ins=[ccR_in.opt()], outs=[ccR_out.opt()]).then_inc(cc.h, 1))
                cc.inc()
                p.wait("gpsimd", cc, cc.n)
                barrier()
                phA.close()
                phB = ph.enter_context(ExitStack())
                E1B = phB.enter_context
                attn_sb = E1B(sbuf("l_attn", [128, NCH * T], BF16))
                qm_sc = E1B(sbuf("l_qmsc", [64, T], BF16))
                kv_bf = E1B(sbuf("l_kvbf", [64, 128], BF16))
                krow_sb = E1B(sbuf("l_krow", [1, 64], BF16))
                kcol_sb = E1B(sbuf("l_kcol", [64, 1], BF16))
                nrm_sb = E1B(sbuf("l_nrm", [1, T], F32))
                wpb2L = [E1B(sbuf("l_wpb2", [128, 2048], BF16)) for _ in range(2)]
                ph2_prev = None
                kb_prev = None
                for h in range(NCH):
                    if kb_prev is not None:
                        p.wait("gpsimd", dve, kb_prev)
                    p.on("gpsimd", lambda e, h_=h: e.dma_start(
                        out=kvpack[0:64, 0:128],
                        in_=ccR_out[h_ * 64:(h_ + 1) * 64, :]).then_inc(cb.h, 16))
                    cb.inc(16)
                    p.on("gpsimd", lambda e, h_=h: e.dma_start(
                        out=kvpack[64:65, 0:64],
                        in_=ccR_out[1024 + h_:1025 + h_, 0:64]).then_inc(cb.h, 16))
                    cb.inc(16)
                    p.wait("vector", cb, cb.n)
                    p.on("vector", lambda e: e.tensor_copy(
                        out=kv_bf[:], in_=kvpack[0:64, 0:128]))
                    p.on("vector", lambda e: e.tensor_copy(
                        out=krow_sb[0:1, 0:64],
                        in_=kvpack[64:65, 0:64]).then_inc(dve.h, 1))
                    kb = dve.inc()
                    kb_prev = kb
                    p.wait("vector", dve, kb)
                    p.wait("tensor", dve, kb)
                    p.on("tensor", lambda e: e.matmul(
                        out=Pr[0:64, 0:1], lhsT=krow_sb[0:1, 0:64],
                        rhs=ones11b[0:1, 0:1], start=True, stop=True,
                        skip_group_check=True).then_inc(pe.h, 1))
                    kc = pe.inc()
                    p.wait("vector", pe, kc)
                    p.on("vector", lambda e: e.tensor_copy(
                        out=kcol_sb[:], in_=Pr[0:64, 0:1]).then_inc(dve.h, 1))
                    p.wait("vector", dve, dve.inc())
                    kc2 = dve.n
                    p.wait("tensor", dve, kc2)
                    if ph2_prev is not None:
                        p.wait("tensor", dve, ph2_prev)
                    for ti in range(2):
                        last = (ti == 1)
                        p.on("tensor", lambda e, ti_=ti, h_=h,
                             inc=(pe if last else None): _inc(e.matmul(
                                out=S4[0:1, ti_ * 512:(ti_ + 1) * 512],
                                lhsT=kcol_sb[0:64, 0:1],
                                rhs=qm_all[0:64, h_ * T + ti_ * 512:
                                           h_ * T + (ti_ + 1) * 512],
                                start=True, stop=True,
                                skip_group_check=True), inc))
                    nm = pe.inc()
                    p.wait("vector", pe, nm)
                    p.on("vector", lambda e: e.tensor_scalar_add(
                        out=nrm_sb[0:1, 0:T], in0=S4[0:1, 0:1024],
                        scalar1=1e-6).then_inc(dve.h, 1))
                    p.wait("vector", dve, dve.inc())
                    p.on("vector", lambda e: e.reciprocal(
                        out=nrm_sb[0:1, 0:T],
                        in_=nrm_sb[0:1, 0:T]).then_inc(dve.h, 1))
                    rn = dve.inc()
                    p.wait("tensor", dve, rn)
                    for ti in range(2):
                        last = (ti == 1)
                        p.on("tensor", lambda e, ti_=ti,
                             inc=(pe if last else None): _inc(e.matmul(
                                out=S4[0:64, 1024 + ti_ * 512:
                                       1024 + (ti_ + 1) * 512],
                                lhsT=cstt["ones1_f"][0:1, 0:64],
                                rhs=nrm_sb[0:1, ti_ * 512:(ti_ + 1) * 512],
                                start=True, stop=True,
                                skip_group_check=True), inc))
                    bcm = pe.inc()
                    p.wait("vector", pe, bcm)
                    p.on("vector", lambda e, h_=h: e.tensor_mul(
                        out=qm_sc[0:64, 0:T], in0=qm_all[0:64, h_ * T:(h_ + 1) * T],
                        in1=S4[0:64, 1024:2048]).then_inc(dve.h, 1))
                    qs = dve.inc()
                    p.wait("tensor", dve, qs)
                    for ti in range(2):
                        last = (ti == 1)
                        p.on("tensor", lambda e, ti_=ti,
                             inc=(pe if last else None): _inc(e.matmul(
                                out=S4[:, ti_ * 512:(ti_ + 1) * 512],
                                lhsT=kv_bf[0:64, :],
                                rhs=qm_sc[0:64, ti_ * 512:(ti_ + 1) * 512],
                                start=True, stop=True,
                                skip_group_check=True), inc))
                    om = pe.inc()
                    p.wait("vector", pe, om)
                    p.on("vector", lambda e, h_=h: e.tensor_copy(
                        out=attn_sb[:, h_ * T:(h_ + 1) * T],
                        in_=S4[:, 0:1024]).then_inc(dve.h, 1))
                    ph2_prev = dve.inc()
                attn_done = dve.n
                omm = {}
                y_mark = {}
                for oc in range(NCH):
                    par = oc % 2
                    pd = S4[:, par * T:(par + 1) * T]
                    wpb2 = wpb2L[par]
                    for hf in range(2):
                        stream_conv(
                            wo1_s[oc, :, hf * 8:(hf + 1) * 8, :].rearrange(
                                "p a b -> p (a b)"), 1024, wpb2,
                            slice(hf * 1024, (hf + 1) * 1024),
                            omm.get(oc - 2), dual_q=True)
                    p.wait("tensor", act, act.n)
                    if oc == 0:
                        p.wait("tensor", dve, attn_done)
                    if oc >= 2:
                        p.wait("tensor", dve, y_mark[oc - 2])
                    for ti in range(2):
                        sl = slice(ti * 512, (ti + 1) * 512)
                        for ic in range(NCH):
                            last = (ti == 1 and ic == NCH - 1)
                            p.on("tensor", lambda e, ic_=ic, sl_=sl, pd_=pd,
                                 wb_=wpb2,
                                 inc=(pe if last else None): _inc(e.matmul(
                                    out=pd_[:, sl_],
                                    lhsT=wb_[:, ic_ * 128:(ic_ + 1) * 128],
                                    rhs=attn_sb[:, ic_ * T + sl_.start:
                                                ic_ * T + sl_.stop],
                                    start=(ic_ == 0), stop=(ic_ == NCH - 1),
                                    skip_group_check=True), inc))
                    omm[oc] = pe.inc()
                    p.wait("vector", pe, omm[oc])
                    p.on("vector", lambda e, oc_=oc, pd_=pd:
                         e.scalar_tensor_tensor(
                            out=x_fm[oc_][:, 0:T], in0=pd_[:, 0:T],
                            scalar=bo_t[:, oc_:oc_ + 1], in1=x_fm[oc_][:, 0:T],
                            op0=ALU.add, op1=ALU.add).then_inc(dve.h, 1))
                    y_mark[oc] = dve.inc()
                barrier()

        def dump(si):
            if not debug:
                return
            barrier()
            for dc in range(NCH):
                p.on("gpsimd", lambda e, dc_=dc, si_=si: e.dma_start(
                    out=dbg_x_r[si_, dc_], in_=x_fm[dc_][:, 0:T]).then_inc(
                    stp[1].h, 16))
                stp[1].inc(16)
            p.wait("gpsimd", stp[1], stp[1].n)

        l0ffn_call()
        dump(0)
        l0attn_call()
        dump(1)
        l0ffn_call()
        dump(2)
        moe_call()
        dump(3)
        l1attn_call()
        dump(4)
        moe_call()
        dump(5)
        for dc in range(NCH):
            p.on("gpsimd", lambda e, dc_=dc: e.dma_start(
                out=yout_r[dc_], in_=x_fm[dc_][:, 0:T]).then_inc(stp[0].h, 16))
            stp[0].inc(16)
        p.wait("gpsimd", stp[0], stp[0].n)
        p.finalize()
    return nc
def prep_fused_inputs(ii, consts):
    perm = qk_perm()
    CC, SS = rope_tables_host()
    shared = {}
    shared.update(prep_l0ffn_inputs(ii["L0_ffn_wq_gate"], ii["L0_ffn_s_gate"],
                                    ii["L0_ffn_wq_up"], ii["L0_ffn_s_up"],
                                    ii["L0_ffn_wq_down"], ii["L0_ffn_s_down"],
                                    consts))
    wq = np.asarray(ii["L0_attn_wq"])
    wk = np.asarray(ii["L0_attn_wk"])
    shared["wq0_s"] = _slab16(np.ascontiguousarray(wq[perm].T))
    shared["wk0_s"] = _slab16(np.ascontiguousarray(wk[perm].T))
    shared["wv0_s"] = _slab16(np.ascontiguousarray(np.asarray(ii["L0_attn_wv"]).T))
    shared["wo0_s"] = _slab16(np.ascontiguousarray(np.asarray(ii["L0_attn_wout"]).T))
    shared["subln"] = np.ascontiguousarray(
        np.asarray(ii["L0_attn_subln_weight"]).reshape(128, 1))
    shared["lam_in"] = np.ascontiguousarray(np.concatenate(
        [np.asarray(ii["L0_attn_lambda_q1"]), np.asarray(ii["L0_attn_lambda_q2"]),
         np.asarray(ii["L0_attn_lambda_k1"]),
         np.asarray(ii["L0_attn_lambda_k2"])]).reshape(1, 256))
    l1a = prep_l1attn_inputs(
        ii["L1_attn_wq"], ii["L1_attn_bq"], ii["L1_attn_wk"], ii["L1_attn_bk"],
        ii["L1_attn_wv"], ii["L1_attn_bv"], ii["L1_attn_wo"], ii["L1_attn_bo"],
        ii["L1_attn_wmap_q"], ii["L1_attn_wmap_k"], consts)
    shared["wq1_s"] = l1a["wq_s"]
    shared["wk1_s"] = l1a["wk_s"]
    shared["wv1_s"] = l1a["wv_s"]
    shared["wo1_s"] = l1a["wo_s"]
    for k in ("bq", "bk", "bv", "bo", "wmq", "wmk"):
        shared[k] = l1a[k]
    moe = prep_l1moe_inputs(ii["L1_ffn_gate_W"], ii["L1_ffn_W1"], ii["L1_ffn_b1"],
                            ii["L1_ffn_W2"], ii["L1_ffn_b2"], consts)
    for k in ("w1", "w2", "b1t", "b2t", "gwt"):
        shared[k] = moe[k]
    io = np.arange(128, dtype=np.float32).reshape(128, 1)
    cols = []
    for k in range(2):
        for fc in range(NF1):
            cols.append(io + fc * 128.0)
    for k in range(2):
        for dc in range(NCH):
            cols.append(io + dc * 128.0)
    shared["iota54"] = np.ascontiguousarray(np.concatenate(cols, axis=1))
    shared["b1r"] = np.ascontiguousarray(np.asarray(ii["L1_ffn_b1"],
                                                    dtype=np.float32))
    shared.update(const_inputs(consts))
    percore = []
    for c in range(8):
        h = c % 2
        percore.append(dict(
            cc_t=np.ascontiguousarray(CC[:, h * T:(h + 1) * T]),
            ss_t=np.ascontiguousarray(SS[:, h * T:(h + 1) * T])))
    return shared, percore


def kernel_fused(**inputs):
    ii = {k: np.asarray(v) for k, v in inputs.items()}
    consts = make_consts()
    x = ii["L0_attn_x"].astype(np.float32, copy=False)
    x_fm = [np.ascontiguousarray(x[b].T) for b in range(4)]
    shared, percore = prep_fused_inputs(ii, consts)
    if "fused" not in _nc_cache:
        _nc_cache["fused"] = build_fused()
    nc = _nc_cache["fused"]
    in_maps = []
    for c in range(8):
        be, h = c // 2, c % 2
        in_maps.append(dict(
            xin=np.ascontiguousarray(x_fm[be][:, h * T:(h + 1) * T]),
            **shared, **percore[c]))
    res = run_bass_kernel_spmd(nc, in_maps, core_ids=list(range(8)))
    ys = [np.asarray(res.results[c]["yout"]) for c in range(8)]
    out = np.empty((4, 2048, 2048), np.float32)
    for c in range(8):
        be, h = c // 2, c % 2
        out[be, h * T:(h + 1) * T, :] = ys[c].T
    return out


def _slices_own(x_fm):
    return [np.ascontiguousarray(x_fm[c // 2][:, (c % 2) * T:((c % 2) + 1) * T])
            for c in range(8)]


def _fm_from_ys(ys):
    return [np.ascontiguousarray(np.concatenate([ys[2 * b], ys[2 * b + 1]], axis=1))
            for b in range(4)]


def kernel_unfused(**inputs):
    ii = {k: np.asarray(v) for k, v in inputs.items()}
    consts = make_consts()
    x = ii["L0_attn_x"].astype(np.float32, copy=False)
    x_fm = [np.ascontiguousarray(x[b].T) for b in range(4)]

    wd_ffn = prep_l0ffn_inputs(ii["L0_ffn_wq_gate"], ii["L0_ffn_s_gate"],
                               ii["L0_ffn_wq_up"], ii["L0_ffn_s_up"],
                               ii["L0_ffn_wq_down"], ii["L0_ffn_s_down"], consts)
    per_half, shared = prep_l0attn_inputs(
        ii["L0_attn_wq"], ii["L0_attn_wk"], ii["L0_attn_wv"],
        ii["L0_attn_lambda_q1"], ii["L0_attn_lambda_q2"],
        ii["L0_attn_lambda_k1"], ii["L0_attn_lambda_k2"], consts)
    wd_out = prep_outproj_inputs(ii["L0_attn_wout"], ii["L0_attn_subln_weight"],
                                 consts)
    wd_l1a = prep_l1attn_inputs(
        ii["L1_attn_wq"], ii["L1_attn_bq"], ii["L1_attn_wk"], ii["L1_attn_bk"],
        ii["L1_attn_wv"], ii["L1_attn_bv"], ii["L1_attn_wo"], ii["L1_attn_bo"],
        ii["L1_attn_wmap_q"], ii["L1_attn_wmap_k"], consts)
    wd_moe = prep_l1moe_inputs(ii["L1_ffn_gate_W"], ii["L1_ffn_W1"],
                               ii["L1_ffn_b1"], ii["L1_ffn_W2"],
                               ii["L1_ffn_b2"], consts)

    x_fm = _fm_from_ys(run_l0ffn(_slices_own(x_fm), wd_ffn))
    aouts = run_l0attn(x_fm, per_half, shared)
    a_slices = []
    for c in range(8):
        be, h = c // 2, c % 2
        afull = np.concatenate([aouts[2 * be], aouts[2 * be + 1]], axis=0)
        a_slices.append(np.ascontiguousarray(afull[:, h * T:(h + 1) * T]))
    x_fm = _fm_from_ys(run_outproj(_slices_own(x_fm), a_slices, wd_out))
    x_fm = _fm_from_ys(run_l0ffn(_slices_own(x_fm), wd_ffn))
    x_fm = _fm_from_ys(run_l1moe(x_fm, wd_moe))
    x_fm = _fm_from_ys(run_l1attn(x_fm, wd_l1a))
    x_fm = _fm_from_ys(run_l1moe(x_fm, wd_moe))

    return np.ascontiguousarray(
        np.stack([x_fm[b].T for b in range(4)])).astype(np.float32, copy=False)


def kernel(**inputs):
    return kernel_fused(**inputs)
